# version 03
# baseline (speedup 1.0000x reference)
import os
import numpy as np
import concourse.bass as bass
import concourse.mybir as mybir
from bass_rust import BassInstruction as _BassInstruction
from concourse.bass_utils import run_bass_kernel_spmd
from concourse.alu_op_type import AluOpType as Op
from contextlib import ExitStack

F32 = mybir.dt.float32
BF16 = mybir.dt.bfloat16
AF = mybir.ActivationFunctionType
AX = mybir.AxisListType

PI = float(np.pi)
INV2PI = float(1.0 / (2 * np.pi))
TWOPI = float(2 * np.pi)
MAGIC = float(1.5 * 2 ** 23)
NEG = -1e30
NEG2 = -2e30

P = 1024
S2 = 2048
D = 2048
NKC = 16
HL = 8
HKV = 8
LAT = 1024
NE = 32
SH = 4096
EPS_LN = 1e-5
SFRAC = float(2.0 / 128.0)


class Sem:
    __slots__ = ("h", "n")

    def __init__(self, h):
        self.h = h
        self.n = 0

    def inc(self, inst, k=1):
        inst.then_inc(self.h, k)
        self.n += k
        return self.n


_NOTRACK = {"wait_ge", "nop", "sem_inc", "sem_clear", "print", "breakpoint",
            "maybe_wait_ge", "drain"}


class TrackEng:

    def __init__(self, e):
        object.__setattr__(self, "_e", e)
        object.__setattr__(self, "last", None)

    def __getattr__(self, n):
        a = getattr(object.__getattribute__(self, "_e"), n)
        if callable(a) and n not in _NOTRACK:
            def w(*args, **kw):
                r = a(*args, **kw)
                if isinstance(r, _BassInstruction):
                    object.__setattr__(self, "last", r)
                return r
            return w
        return a


STAGE = int(os.environ.get("MK_STAGE", "5"))
SUB = int(os.environ.get("MK_SUB", "9"))
DEBUG = int(os.environ.get("MK_DEBUG", "0"))


def build():
    nc = bass.Bass("TRN2", target_bir_lowering=False)

    def din(name, shape, dt=F32):
        return nc.dram_tensor(name, shape, dt, kind="ExternalInput").ap()

    xfull = din("xfull", [S2, D])
    xr_in = din("xr", [P, D])
    out_hbm = nc.dram_tensor("out", [P, D], F32, kind="ExternalOutput").ap()
    dbg = {}
    if DEBUG:
        for nm, w in [("d_rinv", 64), ("d_wrp", 1024), ("d_abs", 1024),
                      ("d_qp", 1024), ("d_kp", 1024), ("d_sred", 64),
                      ("d_acc", 1024), ("d_qf", 4096), ("d_kf", 4096),
                      ("d_vtm", 1024), ("d_lnq", 2048), ("d_n2", 64)]:
            dbg[nm] = nc.dram_tensor(nm, [128, w], F32,
                                     kind="ExternalOutput").ap()

    wqt = din("wqt", [D, HL * 128])
    wkt = din("wkt", [D, HKV * 128])
    wvt = din("wvt", [D, HKV * 128])
    wot = din("wot", [D, D])
    rt_in = din("rt", [128, 64])
    gatet_in = din("gatet", [D, NE]); fc1t = din("fc1t", [D, LAT])
    fc2t = din("fc2t", [LAT, D])
    w1t = din("w1t", [NE, LAT, LAT]); w2t = din("w2t", [NE, LAT, LAT])
    upt = din("upt", [D, SH]); downt = din("downt", [SH, D])
    bqf_in = din("bqf", [128, HL]); bkf_in = din("bkf", [128, HKV])
    bvb_in = din("bvb", [128, HKV * 128]); bob_in = din("bob", [128, D])
    gbb_in = din("gbb", [128, NE])
    cosf_in = din("cosf", [128, S2]); sinf_in = din("sinf", [128, S2])
    ident_in = din("ident", [128, 128], BF16)
    sel32_in = din("sel32", [32, NE * 128], BF16)
    ones1_in = din("ones1", [128, 1], BF16)

    top = ExitStack()

    def sbt(stack, name, shape, dt=F32):
        return stack.enter_context(nc.sbuf_tensor(name, shape, dt))

    with top:
        xres = sbt(top, "xres", [128, 8 * D])
        ident = sbt(top, "ident_s", [128, 128], BF16)
        ones1 = sbt(top, "ones1_s", [128, 1], BF16)
        stat = sbt(top, "stat", [128, 24])
        mv2 = sbt(top, "mv2", [128, 2])
        colA = sbt(top, "colA", [128, 1])
        lntmp = sbt(top, "lntmp", [128, D], BF16)
        halfpi = sbt(top, "halfpi", [128, 1])
        bqf = sbt(top, "bqf_s", [128, HL])
        bkf = sbt(top, "bkf_s", [128, HKV])
        gbb = sbt(top, "gbb_s", [128, NE])

        banks = [top.enter_context(nc.psum_tensor(f"bk{i}", [128, 512], F32))
                 for i in range(6)]
        psbf = [top.enter_context(nc.psum_tensor(f"pb{i}", [128, 1024], BF16))
                for i in range(2)]

        dW = Sem(top.enter_context(nc.semaphore("dW")))
        dX = Sem(top.enter_context(nc.semaphore("dX")))
        dO = Sem(top.enter_context(nc.semaphore("dO")))
        sT = Sem(top.enter_context(nc.semaphore("sT")))
        sV = Sem(top.enter_context(nc.semaphore("sV")))
        sA = Sem(top.enter_context(nc.semaphore("sA")))

        guard = {}

        def take(eng, *keys):
            for key in keys:
                g = guard.pop(key, None)
                if g is not None:
                    eng.wait_ge(g[0].h, g[1])

        def put(key, sem):
            guard[key] = (sem, sem.n)

        def bk(b):
            return ("bank", id(b))

        def kick(eng, sem):
            sem.inc(eng.last)
            eng.wait_ge(sem.h, sem.n)

        with nc.Block() as blk:
            @blk.sync
            def _(s):
                s = TrackEng(s)
                for tt in range(8):
                    dX.inc(s.dma_start(xres[:, tt * D:(tt + 1) * D],
                                       xr_in[tt * 128:(tt + 1) * 128, :]), 16)
                dX.inc(s.dma_start(bqf[:, :], bqf_in[:, :]), 16)
                dX.inc(s.dma_start(bkf[:, :], bkf_in[:, :]), 16)
                dX.inc(s.dma_start(gbb[:, :], gbb_in[:, :]), 16)
                dX.inc(s.dma_start(ident[:, :], ident_in[:, :]), 16)
                dX.inc(s.dma_start(ones1[:, :], ones1_in[:, :]), 16)

            @blk.vector
            def _(v):
                v = TrackEng(v)
                v.memset(halfpi[:, :], PI / 2)
        x_loaded = dX.n

        def load_pair(g, buf, src, col0, width=256):
            ap = src[:, col0:col0 + width].rearrange("(kc p) o -> p kc o",
                                                     p=128)
            return g.dma_start(buf[:, 0:NKC * width], ap)

        def ln_stats(blk, src_ap, wait_sems=()):
            @blk.vector
            def _(v):
                v = TrackEng(v)
                for sem, val in wait_sems:
                    v.wait_ge(sem.h, val)
                for c in range(4):
                    v.bn_stats(stat[:, c * 6:(c + 1) * 6],
                               src_ap[:, c * 512:(c + 1) * 512])
                kick(v, sV)
                v.bn_aggr(mv2[:, :], stat[:, :])
                kick(v, sV)
                v.tensor_scalar(colA[:, :], mv2[:, 1:2], EPS_LN, None, Op.add)
                sV.inc(v.last)
            vst = sV.n

            @blk.scalar
            def _(sc):
                sc = TrackEng(sc)
                sc.wait_ge(sV.h, vst)
                sc.activation(colA[:, :], colA[:, :], AF.Sqrt)
                sA.inc(sc.last)
            ast = sA.n

            @blk.vector
            def _(v):
                v = TrackEng(v)
                v.wait_ge(sA.h, ast)
                v.reciprocal(colA[:, :], colA[:, :])
                kick(v, sV)

        def transpose_fm(blk, src, dst_ap_fn, pe_waits=()):
            @blk.tensor
            def _(t):
                t = TrackEng(t)
                for sem, val in pe_waits:
                    t.wait_ge(sem.h, val)
                take(t, ("psbf", 0), ("psbf", 1))
                for kc in range(16):
                    b, j = kc // 8, kc % 8
                    t.transpose(psbf[b][:, j * 128:(j + 1) * 128],
                                src[:, kc * 128:(kc + 1) * 128], ident[:, :])
                sT.inc(t.last)
            tdone = sT.n

            @blk.vector
            def _(v):
                v = TrackEng(v)
                v.wait_ge(sT.h, tdone)
                for b in range(2):
                    v.tensor_copy(dst_ap_fn(b),
                                  psbf[b][:, :].rearrange("p (j c) -> p j c",
                                                          j=8, c=128))
                sV.inc(v.last)
            put(("psbf", 0), sV)
            put(("psbf", 1), sV)
            return sV.n

        stkA = ExitStack()
        with stkA:
            cosf = sbt(stkA, "cosf_sb", [128, 512], BF16)
            sinf = sbt(stkA, "sinf_sb", [128, 512], BF16)
            bvb = sbt(stkA, "bvb_sb", [128, HKV * 128], BF16)
            bob = sbt(stkA, "bob_sb", [128, D], BF16)
            rt = sbt(stkA, "rt_sb", [128, 64], BF16)
            evtmp = sbt(stkA, "evt", [128, 512])
            rotb = sbt(stkA, "rot", [128, 512])
            wpA = sbt(stkA, "wpA", [128, NKC * 256], BF16)
            wpB = sbt(stkA, "wpB", [128, NKC * 256], BF16)
            otm = sbt(stkA, "otm", [128, 2 * D])
            lnq = sbt(stkA, "lnq", [128, NKC * 512], BF16)
            qf = sbt(stkA, "qf", [128, HL * 512], BF16)
            kf = sbt(stkA, "kf", [128, HKV * 512], BF16)
            vtm = sbt(stkA, "vtm", [128, 4 * HKV * 128], BF16)
            sqb = sbt(stkA, "sqb", [128, 512], BF16)
            rinv = sbt(stkA, "rinv", [128, 64])
            pbuf = sbt(stkA, "pbuf", [128, 1024])
            wrp = sbt(stkA, "wrp", [128, 1024])
            qp = sbt(stkA, "qp", [128, HL * 128], BF16)
            kp = sbt(stkA, "kp", [128, HKV * 128], BF16)
            sred = sbt(stkA, "sred", [128, HL * HKV])
            term = sbt(stkA, "term", [128, HL * 128], BF16)
            acc = sbt(stkA, "acc", [128, HL * 128])
            accb = sbt(stkA, "accb", [128, HL * 128], BF16)
            afm = sbt(stkA, "afm", [128, HL * 2048], BF16)

            with nc.Block() as blk:
                @blk.gpsimd
                def _(g):
                    g = TrackEng(g)
                    dW.inc(g.dma_start(bvb[:, :], bvb_in[:, :]), 16)
                    dW.inc(g.dma_start(bob[:, :], bob_in[:, :]), 16)
                    dW.inc(g.dma_start(rt[:, :], rt_in[:, :]), 16)
                consts_W = dW.n

                xperm = xfull.rearrange("(p j) d -> j p d", j=16)

                for c in range(4 if STAGE >= 2 else 1):
                    @blk.gpsimd
                    def _(g, c=c):
                        g = TrackEng(g)
                        take(g, ("trig",))
                        dW.inc(g.dma_start(cosf[:, :],
                                           cosf_in[:, c * 512:(c + 1) * 512]),
                               16)
                        dW.inc(g.dma_start(sinf[:, :],
                                           sinf_in[:, c * 512:(c + 1) * 512]),
                               16)
                    trig_W = dW.n

                    for tl in range(4):
                        j = 4 * c + tl

                        @blk.sync
                        def _(s, j=j):
                            s = TrackEng(s)
                            take(s, ("xtile",))
                            dX.inc(s.dma_start(otm[:, 0:D], xperm[j, :, :]), 16)
                        xtl = dX.n

                        ln_stats(blk, otm[:, 0:D], [(dX, xtl)])

                        @blk.vector
                        def _(v):
                            v = TrackEng(v)
                            take(v, ("lntmp",))
                            v.tensor_scalar(lntmp[:, :], otm[:, 0:D],
                                            mv2[:, 0:1], colA[:, :],
                                            Op.subtract, Op.mult)
                            sV.inc(v.last)
                        lnv = sV.n
                        put(("xtile",), sV)

                        def lnq_dst(b, tl=tl):
                            base = lnq[:, :].rearrange("p (kc w) -> p kc w",
                                                       kc=16, w=512)
                            return base[:, b * 8:(b + 1) * 8,
                                        tl * 128:(tl + 1) * 128]
                        transpose_fm(blk, lntmp, lnq_dst,
                                     pe_waits=[(sV, lnv)])
                        put(("lntmp",), sT)
                    lnq_ready = sV.n

                    plans = ([("q", wqt, i) for i in range(4)]
                             + [("k", wkt, i) for i in range(4)])
                    if SUB < 2:
                        plans = []
                    for pi, (kind, src, i2) in enumerate(plans):
                        buf = [wpA, wpB][pi % 2]

                        @blk.gpsimd
                        def _(g, buf=buf, src=src, i2=i2, pi=pi):
                            g = TrackEng(g)
                            take(g, ("wp", pi % 2))
                            dW.inc(load_pair(g, buf, src, i2 * 256), 16)
                        wready = dW.n

                        for sub in range(2):
                            h = i2 * 2 + sub
                            dst, bias = (qf, bqf) if kind == "q" else (kf, bkf)
                            bank = banks[(pi * 2 + sub) % 2]

                            @blk.tensor
                            def _(t, buf=buf, sub=sub, bank=bank,
                                  wready=wready,
                                  first=(pi == 0 and sub == 0)):
                                t = TrackEng(t)
                                t.wait_ge(dW.h, wready)
                                if first:
                                    t.wait_ge(sV.h, lnq_ready)
                                take(t, bk(bank))
                                for kc in range(16):
                                    t.matmul(bank[:, :],
                                             buf[:, kc * 256 + sub * 128:
                                                 kc * 256 + sub * 128 + 128],
                                             lnq[:, kc * 512:(kc + 1) * 512],
                                             start=(kc == 0), stop=(kc == 15))
                                sT.inc(t.last)
                            mmdone = sT.n
                            if sub == 1:
                                put(("wp", pi % 2), sT)

                            @blk.vector
                            def _(v, bank=bank, h=h, dst=dst, bias=bias,
                                  mmdone=mmdone,
                                  firstc=(pi == 0 and sub == 0 and c == 0)):
                                v = TrackEng(v)
                                v.wait_ge(sT.h, mmdone)
                                if firstc:
                                    v.wait_ge(dW.h, consts_W)
                                v.wait_ge(dW.h, trig_W)
                                v.tensor_scalar(evtmp[:, :], bank[:, :],
                                                bias[:, h:h + 1], None, Op.add)
                                v.tensor_scalar(rotb[0:64, :],
                                                evtmp[64:128, :], -1.0, None,
                                                Op.mult)
                                v.tensor_copy(rotb[64:128, :], evtmp[0:64, :])
                                v.tensor_tensor(evtmp[:, :], evtmp[:, :],
                                                cosf[:, :], Op.mult)
                                v.tensor_tensor(rotb[:, :], rotb[:, :],
                                                sinf[:, :], Op.mult)
                                v.tensor_tensor(dst[:, h * 512:(h + 1) * 512],
                                                evtmp[:, :], rotb[:, :],
                                                Op.add)
                                sV.inc(v.last)
                            evdone = sV.n
                            put(bk(bank), sV)

                            @blk.scalar
                            def _(sc, h=h, dst=dst, evdone=evdone):
                                sc = TrackEng(sc)
                                sc.wait_ge(sV.h, evdone)
                                take(sc, ("sqb",))
                                sc.activation(sqb[:, :],
                                              dst[:, h * 512:(h + 1) * 512],
                                              AF.Square)
                                sA.inc(sc.last)
                            sqdone = sA.n

                            hidx = h if kind == "q" else HL + h
                            nb = banks[5]

                            @blk.tensor
                            def _(t, sqdone=sqdone, hidx=hidx, nb=nb):
                                t = TrackEng(t)
                                t.wait_ge(sA.h, sqdone)
                                take(t, bk(nb))
                                for tl in range(4):
                                    t.matmul(nb[:, tl * 16 + hidx:
                                                tl * 16 + hidx + 1],
                                             sqb[:, tl * 128:(tl + 1) * 128],
                                             ones1[:, :], start=True,
                                             stop=True)
                                sT.inc(t.last)
                            put(("sqb",), sT)
                    rope_done = sV.n
                    put(("trig",), sV)
                    n2done = sT.n

                    for i2 in range(4 if SUB >= 3 else 0):
                        buf = [wpA, wpB][i2 % 2]

                        @blk.gpsimd
                        def _(g, buf=buf, i2=i2):
                            g = TrackEng(g)
                            take(g, ("wp", i2 % 2))
                            dW.inc(load_pair(g, buf, wvt, i2 * 256), 16)
                        vwready = dW.n

                        for tl in range(4):
                            bank = banks[2 + (tl % 2)]

                            @blk.tensor
                            def _(t, buf=buf, i2=i2, tl=tl, bank=bank,
                                  vwready=vwready):
                                t = TrackEng(t)
                                t.wait_ge(dW.h, vwready)
                                take(t, bk(bank))
                                for kc in range(16):
                                    t.matmul(
                                        bank[:, 0:256],
                                        lnq[:, kc * 512 + tl * 128:
                                            kc * 512 + tl * 128 + 128],
                                        buf[:, kc * 256:(kc + 1) * 256],
                                        start=(kc == 0), stop=(kc == 15))
                                sT.inc(t.last)
                            mmdone = sT.n
                            if tl == 3:
                                put(("wp", i2 % 2), sT)

                            @blk.vector
                            def _(v, i2=i2, tl=tl, bank=bank, mmdone=mmdone):
                                v = TrackEng(v)
                                v.wait_ge(sT.h, mmdone)
                                v.tensor_tensor(
                                    vtm[:, tl * 1024 + i2 * 256:
                                        tl * 1024 + i2 * 256 + 256],
                                    bank[:, 0:256],
                                    bvb[:, i2 * 256:(i2 + 1) * 256], Op.add)
                                sV.inc(v.last)
                            put(bk(bank), sV)

                    if SUB < 4:
                        continue
                    @blk.scalar
                    def _(sc, n2done=n2done):
                        sc = TrackEng(sc)
                        sc.wait_ge(sT.h, n2done)
                        take(sc, ("rinv",))
                        sc.activation(rinv[:, :], banks[5][:, 0:64], AF.Sqrt)
                        sA.inc(sc.last)
                    ndone = sA.n
                    put(bk(banks[5]), sA)

                    @blk.vector
                    def _(v, ndone=ndone):
                        v = TrackEng(v)
                        v.wait_ge(sA.h, ndone)
                        v.tensor_scalar(rinv[:, :], rinv[:, :], 1e-6, None,
                                        Op.max)
                        kick(v, sV)
                        v.reciprocal(rinv[:, :], rinv[:, :])
                        sV.inc(v.last)
                    rinvdone = sV.n

                    for tl in range(4):
                        j = 4 * c + tl
                        pb_q, pb_k = banks[2], banks[3]

                        @blk.tensor
                        def _(t, tl=tl, first=(tl == 0)):
                            t = TrackEng(t)
                            if first:
                                t.wait_ge(sV.h, rope_done)
                                t.wait_ge(dW.h, consts_W)
                            take(t, bk(pb_q), bk(pb_k))
                            for h in range(HL):
                                t.matmul(pb_q[:, h * 64:h * 64 + 64],
                                         qf[:, h * 512 + tl * 128:
                                            h * 512 + tl * 128 + 128],
                                         rt[:, :], start=True, stop=True)
                            for h in range(HKV):
                                t.matmul(pb_k[:, h * 64:h * 64 + 64],
                                         kf[:, h * 512 + tl * 128:
                                            h * 512 + tl * 128 + 128],
                                         rt[:, :], start=True, stop=True)
                            sT.inc(t.last)
                        pdone = sT.n

                        @blk.vector
                        def _(v, tl=tl, pdone=pdone):
                            v = TrackEng(v)
                            v.wait_ge(sT.h, pdone)
                            v.wait_ge(sV.h, rinvdone)
                            take(v, ("pwrp",))
                            for h in range(16):
                                bank = pb_q if h < 8 else pb_k
                                v.tensor_scalar(
                                    pbuf[:, h * 64:(h + 1) * 64],
                                    bank[:, (h % 8) * 64:(h % 8) * 64 + 64],
                                    rinv[:, tl * 16 + h:tl * 16 + h + 1],
                                    None, Op.mult)
                            kick(v, sV)
                            v.tensor_scalar(wrp[:, :], pbuf[:, :], INV2PI,
                                            MAGIC, Op.mult, Op.add)
                            v.tensor_scalar(wrp[:, :], wrp[:, :], MAGIC,
                                            None, Op.subtract)
                            v.scalar_tensor_tensor(wrp[:, :], wrp[:, :],
                                                   -TWOPI, pbuf[:, :],
                                                   Op.mult, Op.add)
                            sV.inc(v.last)
                        wdone = sV.n
                        put(bk(pb_q), sV)
                        put(bk(pb_k), sV)
                        if tl == 3:
                            put(("rinv",), sV)

                        @blk.scalar
                        def _(sc, wdone=wdone):
                            sc = TrackEng(sc)
                            sc.wait_ge(sV.h, wdone)
                            take(sc, ("qpkp",))
                            sc.activation(pbuf[:, :], wrp[:, :], AF.Abs)
                            qs = qp[:, :].rearrange("p (h w) -> p h w",
                                                    h=HL, w=128)
                            ks = kp[:, :].rearrange("p (h w) -> p h w",
                                                    h=HKV, w=128)
                            wq_ = wrp[:, 0:HL * 64].rearrange(
                                "p (h w) -> p h w", h=HL, w=64)
                            wk_ = wrp[:, HL * 64:].rearrange(
                                "p (h w) -> p h w", h=HKV, w=64)
                            aq = pbuf[:, 0:HL * 64].rearrange(
                                "p (h w) -> p h w", h=HL, w=64)
                            ak = pbuf[:, HL * 64:].rearrange(
                                "p (h w) -> p h w", h=HKV, w=64)
                            sc.activation(qs[:, :, 0:64], wq_, AF.Sin)
                            sc.activation(qs[:, :, 64:128], aq, AF.Sin,
                                          scale=-1.0, bias=halfpi[:, :])
                            sc.activation(ks[:, :, 0:64], wk_, AF.Sin)
                            sc.activation(ks[:, :, 64:128], ak, AF.Sin,
                                          scale=-1.0, bias=halfpi[:, :])
                            sA.inc(sc.last)
                        featdone = sA.n
                        put(("pwrp",), sA)

                        @blk.vector
                        def _(v, tl=tl, featdone=featdone):
                            v = TrackEng(v)
                            v.wait_ge(sA.h, featdone)
                            kp3 = kp[:, :].rearrange("p (a b) -> p a b",
                                                     a=8, b=128)
                            t1k = term[:, :].rearrange("p (a b) -> p a b",
                                                       a=8, b=128)
                            for h in range(HL):
                                qph = qp[:, h * 128:(h + 1) * 128]
                                v.tensor_tensor(
                                    t1k, kp3,
                                    qph.unsqueeze(1).to_broadcast(
                                        [128, 8, 128]), Op.mult)
                                v.tensor_reduce(sred[:, h * 8:(h + 1) * 8],
                                                t1k, AX.X, Op.add)
                            kick(v, sV)
                            v.tensor_scalar(sred[:, :], sred[:, :], SFRAC,
                                            None, Op.mult)
                            kick(v, sV)
                            s3 = sred[:, :].rearrange("p (a b) -> p a b",
                                                      a=HL, b=HKV)
                            t3 = term[:, :].rearrange("p (a b) -> p a b",
                                                      a=HL, b=128)
                            for hp in range(HKV):
                                vk = vtm[:, tl * 1024 + hp * 128:
                                         tl * 1024 + hp * 128 + 128]
                                v.tensor_tensor(
                                    t3,
                                    vk.unsqueeze(1).to_broadcast(
                                        [128, HL, 128]),
                                    s3[:, :, hp:hp + 1].to_broadcast(
                                        [128, HL, 128]), Op.mult)
                                if hp == 0:
                                    v.tensor_copy(acc[:, :], term[:, :])
                                else:
                                    v.tensor_tensor(acc[:, :], acc[:, :],
                                                    term[:, :], Op.add)
                            take(v, ("accb",))
                            v.tensor_copy(accb[:, :], acc[:, :])
                            sV.inc(v.last)
                        put(("qpkp",), sV)
                        accdone = sV.n

                        pbx = psbf[tl % 2]

                        @blk.tensor
                        def _(t, pbx=pbx, tl=tl, accdone=accdone):
                            t = TrackEng(t)
                            t.wait_ge(sV.h, accdone)
                            take(t, ("psbf", tl % 2))
                            for h in range(HL):
                                t.transpose(pbx[:, h * 128:(h + 1) * 128],
                                            accb[:, h * 128:(h + 1) * 128],
                                            ident[:, :])
                            sT.inc(t.last)
                        tpd = sT.n
                        put(("accb",), sT)

                        @blk.vector
                        def _(v, pbx=pbx, j=j, tpd=tpd):
                            v = TrackEng(v)
                            v.wait_ge(sT.h, tpd)
                            base = afm[:, :].rearrange("p (h jj w) -> p h jj w",
                                                       h=HL, jj=16, w=128)
                            v.tensor_copy(
                                base[:, :, j, :],
                                pbx[:, :].rearrange("p (h w) -> p h w",
                                                    h=HL, w=128))
                            sV.inc(v.last)
                        put(("psbf", tl % 2), sV)

                        if DEBUG and c == 0 and tl == 0:
                            dbg_done = sV.n

                            @blk.gpsimd
                            def _(g):
                                g.wait_ge(sV.h, dbg_done)
                                dW.inc(g.dma_start(dbg["d_rinv"],
                                                   rinv[:, 0:64]), 16)
                                dW.inc(g.dma_start(dbg["d_wrp"], wrp[:, :]),
                                       16)
                                dW.inc(g.dma_start(dbg["d_abs"], pbuf[:, :]),
                                       16)
                                dW.inc(g.dma_start(dbg["d_qp"], qp[:, :]), 16)
                                dW.inc(g.dma_start(dbg["d_kp"], kp[:, :]), 16)
                                dW.inc(g.dma_start(dbg["d_sred"],
                                                   sred[:, :]), 16)
                                dW.inc(g.dma_start(dbg["d_acc"], acc[:, :]),
                                       16)
                                dW.inc(g.dma_start(dbg["d_qf"],
                                                   qf[:, 0:4096]), 16)
                                dW.inc(g.dma_start(dbg["d_kf"],
                                                   kf[:, 0:4096]), 16)
                                dW.inc(g.dma_start(dbg["d_vtm"],
                                                   vtm[:, 0:1024]), 16)
                                dW.inc(g.dma_start(dbg["d_lnq"],
                                                   lnq[:, 0:2048]), 16)
                            dbg_W = dW.n

                            @blk.vector
                            def _(v):
                                v = TrackEng(v)
                                v.wait_ge(dW.h, dbg_W)
                afm_ready = sV.n

                for rp in range(4 if STAGE >= 2 else 0):
                    for oc in range(4):
                        for s2b in range(2):
                            buf = [wpA, wpB][s2b]

                            @blk.gpsimd
                            def _(g, buf=buf, oc=oc, s2b=s2b, s2i=s2b):
                                g = TrackEng(g)
                                take(g, ("wp", s2i))
                                dW.inc(load_pair(g, buf, wot,
                                                 oc * 512 + s2b * 256), 16)
                        woready = dW.n

                        for sub in range(2):
                            r = 2 * rp + sub
                            bank = banks[(oc * 2 + sub) % 2]

                            @blk.tensor
                            def _(t, r=r, sub=sub, oc=oc, bank=bank,
                                  woready=woready,
                                  first=(rp == 0 and oc == 0 and sub == 0)):
                                t = TrackEng(t)
                                t.wait_ge(dW.h, woready)
                                if first:
                                    t.wait_ge(sV.h, afm_ready)
                                take(t, bk(bank))
                                for s2b in range(2):
                                    buf = [wpA, wpB][s2b]
                                    for kc in range(16):
                                        t.matmul(
                                            bank[:, s2b * 256:
                                                 s2b * 256 + 256],
                                            afm[:, r * 2048 + kc * 128:
                                                r * 2048 + kc * 128 + 128],
                                            buf[:, kc * 256:(kc + 1) * 256],
                                            start=(kc == 0), stop=(kc == 15))
                                sT.inc(t.last)
                            mmdone = sT.n
                            if sub == 1:
                                put(("wp", 0), sT)
                                put(("wp", 1), sT)

                            @blk.vector
                            def _(v, sub=sub, oc=oc, bank=bank,
                                  mmdone=mmdone):
                                v = TrackEng(v)
                                v.wait_ge(sT.h, mmdone)
                                v.tensor_tensor(
                                    otm[:, sub * D + oc * 512:
                                        sub * D + oc * 512 + 512],
                                    bank[:, :],
                                    bob[:, oc * 512:(oc + 1) * 512], Op.add)
                                sV.inc(v.last)
                            put(bk(bank), sV)
                    wo_done = sV.n

                    for sub in range(2):
                        r = 2 * rp + sub
                        oap = otm[:, sub * D:(sub + 1) * D]
                        ln_stats(blk, oap,
                                 [(sV, wo_done), (dX, x_loaded)]
                                 if sub == 0 else [])

                        @blk.vector
                        def _(v, oap=oap, r=r):
                            v = TrackEng(v)
                            take(v, ("lntmp",))
                            v.tensor_scalar(lntmp[:, :], oap, mv2[:, 0:1],
                                            colA[:, :], Op.subtract, Op.mult)
                            v.tensor_tensor(xres[:, r * D:(r + 1) * D],
                                            xres[:, r * D:(r + 1) * D],
                                            lntmp[:, :], Op.add)
                            sV.inc(v.last)
                        put(("lntmp",), sV)
        attn_done = sV.n

        if STAGE < 5:
            with nc.Block() as blk:
                @blk.sync
                def _(s):
                    s.wait_ge(sV.h, sV.n)
                    for tt in range(8):
                        dO.inc(s.dma_start(
                            out_hbm[tt * 128:(tt + 1) * 128, :],
                            xres[:, tt * D:(tt + 1) * D]), 16)
                    s.wait_ge(dO.h, dO.n)

        stkE = ExitStack()
        sel32 = sbt(stkE, "sel32_s", [32, NE * 128], BF16)
        cwfm = sbt(stkE, "cwfm", [32, P], BF16)
        stkY = ExitStack()
        ylb = sbt(stkY, "ylb", [128, 8 * P], BF16)
        stkX = ExitStack()
        xlat = sbt(stkX, "xlat", [128, 8 * P], BF16)
        ysb = sbt(stkX, "ysb", [128, 8 * P])

        gate_ready = cwfm_ready = xlat_ready = 0

        stkL = ExitStack()
        with stkL:
            lnf = sbt(stkL, "lnf2", [128, NKC * P], BF16)
            wpAf = sbt(stkL, "wpAf", [128, NKC * 256], BF16)
            wpBf = sbt(stkL, "wpBf", [128, NKC * 256], BF16)
            gateb = sbt(stkL, "gateb", [128, NKC * NE], BF16)
            scs = sbt(stkL, "scs", [128, NE])
            sbb = sbt(stkL, "sbb", [128, NE])
            g8 = sbt(stkL, "g8", [128, 4 * 8])
            gpad = sbt(stkL, "gpad", [128, 12])
            gm8 = sbt(stkL, "gm8", [128, 8])
            repl8 = sbt(stkL, "repl8", [128, 8])
            gzap = sbt(stkL, "gzap", [128, 12])
            gmask = sbt(stkL, "gmask", [128, 4])
            msb = sbt(stkL, "msb", [128, NE])
            mzap = sbt(stkL, "mzap", [128, NE])
            selm = sbt(stkL, "selm", [128, NE])
            t8v = sbt(stkL, "t8v", [128, 8])
            cwb16 = sbt(stkL, "cwb16", [128, NE], BF16)

            with nc.Block() as blk:
                @blk.gpsimd
                def _(g):
                    g = TrackEng(g)
                    ap = gatet_in[:, :].rearrange("(kc p) o -> p kc o", p=128)
                    dW.inc(g.dma_start(gateb[:, :], ap), 16)
                    dW.inc(g.dma_start(sel32[:, :], sel32_in[:, :]), 16)
                gate_ready = dW.n

                for tt in range(8 if STAGE >= 3 else 0):
                    xap = xres[:, tt * D:(tt + 1) * D]
                    ln_stats(blk, xap,
                             [(sV, attn_done)] if tt == 0 else [])

                    @blk.vector
                    def _(v, xap=xap):
                        v = TrackEng(v)
                        take(v, ("lntmp",))
                        v.tensor_scalar(lntmp[:, :], xap, mv2[:, 0:1],
                                        colA[:, :], Op.subtract, Op.mult)
                        sV.inc(v.last)
                    lnv = sV.n

                    def lnf_dst(b, tt=tt):
                        base = lnf[:, :].rearrange("p (kc w) -> p kc w",
                                                   kc=16, w=P)
                        return base[:, b * 8:(b + 1) * 8,
                                    tt * 128:(tt + 1) * 128]
                    transpose_fm(blk, lntmp, lnf_dst, pe_waits=[(sV, lnv)])
                    put(("lntmp",), sT)
                ln2_ready = sV.n

                for tt in range(8 if STAGE >= 3 else 0):
                    bank = banks[tt % 2]

                    @blk.tensor
                    def _(t, tt=tt, bank=bank):
                        t = TrackEng(t)
                        t.wait_ge(dW.h, gate_ready)
                        if tt == 0:
                            t.wait_ge(sV.h, ln2_ready)
                        take(t, bk(bank))
                        for kc in range(16):
                            t.matmul(bank[:, 0:NE],
                                     lnf[:, kc * P + tt * 128:
                                         kc * P + tt * 128 + 128],
                                     gateb[:, kc * NE:(kc + 1) * NE],
                                     start=(kc == 0), stop=(kc == 15))
                        sT.inc(t.last)
                    gmm = sT.n

                    @blk.scalar
                    def _(sc, bank=bank, gmm=gmm):
                        sc = TrackEng(sc)
                        sc.wait_ge(sT.h, gmm)
                        take(sc, ("gatevec",))
                        sc.activation(scs[:, :], bank[:, 0:NE], AF.Sigmoid)
                        sA.inc(sc.last)
                    sigd = sA.n
                    put(bk(bank), sA)

                    @blk.vector
                    def _(v, sigd=sigd):
                        v = TrackEng(v)
                        v.wait_ge(sA.h, sigd)
                        v.tensor_tensor(sbb[:, :], scs[:, :], gbb[:, :],
                                        Op.add)
                        kick(v, sV)
                        for gi in range(4):
                            v.max(out=g8[:, gi * 8:(gi + 1) * 8],
                                  in_=sbb[:, gi * 8:(gi + 1) * 8])
                        kick(v, sV)
                        g3 = g8[:, :].rearrange("p (a b) -> p a b", a=4, b=8)
                        v.tensor_tensor(gpad[:, 0:4], g3[:, :, 0],
                                        g3[:, :, 1], Op.add)
                        v.memset(gpad[:, 4:12], NEG)
                        kick(v, sV)
                        v.max(out=gm8[:, :], in_=gpad[:, :])
                        kick(v, sV)
                        v.tensor_copy(repl8[:, 0:2], gm8[:, 0:2])
                        v.memset(repl8[:, 2:8], NEG)
                        kick(v, sV)
                        v.match_replace(out=gzap[:, :],
                                        in_to_replace=repl8[:, :],
                                        in_values=gpad[:, :], imm_value=NEG)
                        kick(v, sV)
                        v.tensor_tensor(gmask[:, :], gpad[:, 0:4],
                                        gzap[:, 0:4], Op.not_equal)
                        kick(v, sV)
                        v.tensor_scalar(gmask[:, :], gmask[:, :], 1.0, 1e30,
                                        Op.subtract, Op.mult)
                        kick(v, sV)
                        m3 = msb[:, :].rearrange("p (a b) -> p a b", a=4, b=8)
                        s3 = sbb[:, :].rearrange("p (a b) -> p a b", a=4, b=8)
                        v.tensor_tensor(m3, s3,
                                        gmask[:, :].unsqueeze(2)
                                        .to_broadcast([128, 4, 8]), Op.add)
                        kick(v, sV)
                        v.max(out=t8v[:, :], in_=msb[:, :])
                        kick(v, sV)
                        v.match_replace(out=mzap[:, :],
                                        in_to_replace=t8v[:, :],
                                        in_values=msb[:, :], imm_value=NEG2)
                        kick(v, sV)
                        v.tensor_tensor(selm[:, :], msb[:, :], mzap[:, :],
                                        Op.not_equal)
                        kick(v, sV)
                        v.tensor_tensor(selm[:, :], scs[:, :], selm[:, :],
                                        Op.mult)
                        kick(v, sV)
                        v.tensor_reduce(colA[:, :], selm[:, :], AX.X, Op.add)
                        kick(v, sV)
                        v.reciprocal(colA[:, :], colA[:, :])
                        kick(v, sV)
                        v.tensor_scalar(cwb16[:, :], selm[:, :], colA[:, :],
                                        2.5, Op.mult, Op.mult)
                        sV.inc(v.last)
                    cwd = sV.n
                    put(("gatevec",), sV)

                    @blk.tensor
                    def _(t, cwd=cwd):
                        t = TrackEng(t)
                        take(t, ("psbf", 0))
                        t.wait_ge(sV.h, cwd)
                        t.transpose(psbf[0][0:32, 0:128], cwb16[:, :],
                                    ident[:, :])
                        sT.inc(t.last)
                    cwt = sT.n

                    @blk.vector
                    def _(v, tt=tt, cwt=cwt):
                        v = TrackEng(v)
                        v.wait_ge(sT.h, cwt)
                        v.tensor_copy(cwfm[:, tt * 128:(tt + 1) * 128],
                                      psbf[0][0:32, 0:128])
                        sV.inc(v.last)
                    put(("psbf", 0), sV)
                cwfm_ready = sV.n

                for lc2 in range(4 if STAGE >= 3 else 0):
                    buf = [wpAf, wpBf][lc2 % 2]

                    @blk.gpsimd
                    def _(g, buf=buf, lc2=lc2):
                        g = TrackEng(g)
                        take(g, ("wpf", lc2 % 2))
                        dW.inc(load_pair(g, buf, fc1t, lc2 * 256), 16)
                    fcw = dW.n

                    for sub in range(2):
                        lc = lc2 * 2 + sub
                        for tc in range(2):
                            bank = banks[(lc * 2 + tc) % 2]

                            @blk.tensor
                            def _(t, buf=buf, sub=sub, tc=tc, bank=bank,
                                  fcw=fcw):
                                t = TrackEng(t)
                                t.wait_ge(dW.h, fcw)
                                take(t, bk(bank))
                                for kc in range(16):
                                    t.matmul(
                                        bank[:, :],
                                        buf[:, kc * 256 + sub * 128:
                                            kc * 256 + sub * 128 + 128],
                                        lnf[:, kc * P + tc * 512:
                                            kc * P + tc * 512 + 512],
                                        start=(kc == 0), stop=(kc == 15))
                                sT.inc(t.last)
                            mmd = sT.n
                            if sub == 1 and tc == 1:
                                put(("wpf", lc2 % 2), sT)

                            @blk.vector
                            def _(v, lc=lc, tc=tc, bank=bank, mmd=mmd):
                                v = TrackEng(v)
                                v.wait_ge(sT.h, mmd)
                                v.tensor_copy(
                                    xlat[:, lc * P + tc * 512:
                                         lc * P + tc * 512 + 512],
                                    bank[:, :])
                                sV.inc(v.last)
                            put(bk(bank), sV)
                xlat_ready = sV.n

                @blk.vector
                def _(v):
                    v = TrackEng(v)
                    v.memset(ysb[:, :], 0.0)
                    sV.inc(v.last)

        stkW = ExitStack()
        with stkW:
            w1b = [sbt(stkW, f"w1_{i}", [128, 8 * 512], BF16)
                   for i in range(2)]
            w2b = [sbt(stkW, f"w2_{i}", [128, 8 * 512], BF16)
                   for i in range(2)]
            he = sbt(stkW, "he", [128, 8 * 1024], BF16)
            gsb = sbt(stkW, "gsb", [128, 512], BF16)
            usb = sbt(stkW, "usb", [128, 512], BF16)
            cwsb = sbt(stkW, "cwsb", [128, 1024], BF16)

            def load_w1(g, buf, e, ih):
                ap = w1t[e, :, ih * 512:(ih + 1) * 512].rearrange(
                    "(lc p) i -> p lc i", p=128)
                return g.dma_start(buf[:, :], ap)

            def load_w2(g, buf, e, lh):
                ap = w2t[e, :, lh * 512:(lh + 1) * 512].rearrange(
                    "(ic p) l -> p ic l", p=128)
                return g.dma_start(buf[:, :], ap)

            with nc.Block() as blk:
                for e in range(NE if STAGE >= 4 else 0):
                    w1r = [0, 0]
                    for ih in range(2):
                        @blk.gpsimd
                        def _(g, ih=ih, e=e):
                            g = TrackEng(g)
                            take(g, ("w1", ih))
                            dW.inc(load_w1(g, w1b[ih], e, ih), 16)
                        w1r[ih] = dW.n

                    cb = banks[4]
                    for tc in range(2):
                        @blk.tensor
                        def _(t, e=e, tc=tc, first=(e == 0 and tc == 0)):
                            t = TrackEng(t)
                            if first:
                                t.wait_ge(sV.h, cwfm_ready)
                                t.wait_ge(dW.h, gate_ready)
                            take(t, bk(cb))
                            t.matmul(cb[:, :],
                                     sel32[:, e * 128:(e + 1) * 128],
                                     cwfm[:, tc * 512:(tc + 1) * 512],
                                     start=True, stop=True)
                            sT.inc(t.last)
                        cbd = sT.n

                        @blk.vector
                        def _(v, tc=tc, cbd=cbd):
                            v = TrackEng(v)
                            v.wait_ge(sT.h, cbd)
                            v.tensor_copy(cwsb[:, tc * 512:(tc + 1) * 512],
                                          banks[4][:, :])
                            sV.inc(v.last)
                        put(bk(cb), sV)
                    cwr = sV.n

                    for ih in range(2):
                        for icl in range(4):
                            ic = ih * 4 + icl
                            for tc in range(2):
                                bank = banks[(2 * ic + tc) % 2]

                                @blk.tensor
                                def _(t, ih=ih, icl=icl, tc=tc, bank=bank,
                                      first=(e == 0 and ih == 0
                                             and icl == 0 and tc == 0)):
                                    t = TrackEng(t)
                                    t.wait_ge(dW.h, w1r[ih])
                                    if first:
                                        t.wait_ge(sV.h, xlat_ready)
                                    take(t, bk(bank))
                                    for lc in range(8):
                                        t.matmul(
                                            bank[:, :],
                                            w1b[ih][:, lc * 512 + icl * 128:
                                                    lc * 512 + icl * 128
                                                    + 128],
                                            xlat[:, lc * P + tc * 512:
                                                 lc * P + tc * 512 + 512],
                                            start=(lc == 0), stop=(lc == 7))
                                    sT.inc(t.last)
                                hmm_ = sT.n
                                if icl == 3 and tc == 1:
                                    put(("w1", ih), sT)

                                @blk.scalar
                                def _(sc, bank=bank, hmm_=hmm_):
                                    sc = TrackEng(sc)
                                    sc.wait_ge(sT.h, hmm_)
                                    take(sc, ("gsb",))
                                    sc.activation(gsb[:, :], bank[:, :],
                                                  AF.Relu)
                                    sA.inc(sc.last)
                                reld = sA.n
                                put(bk(bank), sA)

                                @blk.vector
                                def _(v, ic=ic, tc=tc, reld=reld, cwr=cwr,
                                      firsth=(ih == 0 and icl == 0
                                              and tc == 0)):
                                    v = TrackEng(v)
                                    v.wait_ge(sA.h, reld)
                                    if firsth:
                                        v.wait_ge(sV.h, cwr)
                                        take(v, ("he",))
                                    v.tensor_tensor(
                                        usb[:, :], gsb[:, :],
                                        cwsb[:, tc * 512:(tc + 1) * 512],
                                        Op.mult)
                                    v.tensor_tensor(
                                        he[:, ic * 1024 + tc * 512:
                                           ic * 1024 + tc * 512 + 512],
                                        usb[:, :], gsb[:, :], Op.mult)
                                    sV.inc(v.last)
                                put(("gsb",), sV)
                    hdone = sV.n

                    w2r = [0, 0]
                    for lh in range(2):
                        @blk.gpsimd
                        def _(g, lh=lh, e=e):
                            g = TrackEng(g)
                            take(g, ("w2", lh))
                            dW.inc(load_w2(g, w2b[lh], e, lh), 16)
                        w2r[lh] = dW.n

                    for tc in range(2):
                        for lc in range(8):
                            lh, lcl = lc // 4, lc % 4
                            bank = banks[2 + (lc + tc) % 2]

                            @blk.tensor
                            def _(t, tc=tc, lc=lc, lh=lh, lcl=lcl, bank=bank,
                                  first=(tc == 0 and lc == 0)):
                                t = TrackEng(t)
                                t.wait_ge(dW.h, w2r[lh])
                                if first:
                                    t.wait_ge(sV.h, hdone)
                                take(t, bk(bank))
                                for ic in range(8):
                                    t.matmul(
                                        bank[:, :],
                                        w2b[lh][:, ic * 512 + lcl * 128:
                                                ic * 512 + lcl * 128 + 128],
                                        he[:, ic * 1024 + tc * 512:
                                           ic * 1024 + tc * 512 + 512],
                                        start=(ic == 0), stop=(ic == 7))
                                sT.inc(t.last)
                            ymm = sT.n
                            if tc == 1 and lc == 3:
                                put(("w2", 0), sT)
                            if tc == 1 and lc == 7:
                                put(("w2", 1), sT)
                                put(("he",), sT)

                            @blk.vector
                            def _(v, tc=tc, lc=lc, bank=bank, ymm=ymm):
                                v = TrackEng(v)
                                v.wait_ge(sT.h, ymm)
                                yap = ysb[:, lc * P + tc * 512:
                                          lc * P + tc * 512 + 512]
                                v.tensor_tensor(yap, yap, bank[:, :], Op.add)
                                sV.inc(v.last)
                            put(bk(bank), sV)
        y_done = sV.n

        with nc.Block() as blk:
            @blk.vector
            def _(v):
                v = TrackEng(v)
                for i in range(8):
                    v.tensor_copy(ylb[:, i * P:(i + 1) * P],
                                  ysb[:, i * P:(i + 1) * P])
                sV.inc(v.last)
        ylb_ready = sV.n
        stkX.close()

        stkS = ExitStack()
        with stkS:
            upw = sbt(stkS, "upw", [128, NKC * 256], BF16)
            shfm = sbt(stkS, "shfm", [128, 32 * 512], BF16)
            dwn = sbt(stkS, "dwn", [128, 32 * 256], BF16)
            fc2b = sbt(stkS, "fc2b", [128, 8 * 256], BF16)
            lnh2 = sbt(stkS, "lnh2", [128, NKC * 512], BF16)
            otmh = sbt(stkS, "otmh", [128, 4 * D])
            gsh = sbt(stkS, "gsh", [128, 512], BF16)

            with nc.Block() as blk:
                for half in range(2 if STAGE >= 5 else 0):
                    for tl in range(4):
                        tt = half * 4 + tl
                        xap = xres[:, tt * D:(tt + 1) * D]
                        ln_stats(blk, xap)

                        @blk.vector
                        def _(v, xap=xap):
                            v = TrackEng(v)
                            take(v, ("lntmp",))
                            v.tensor_scalar(lntmp[:, :], xap, mv2[:, 0:1],
                                            colA[:, :], Op.subtract, Op.mult)
                            sV.inc(v.last)
                        lnv = sV.n

                        def lnh_dst(b, tl=tl):
                            base = lnh2[:, :].rearrange("p (kc w) -> p kc w",
                                                        kc=16, w=512)
                            return base[:, b * 8:(b + 1) * 8,
                                        tl * 128:(tl + 1) * 128]
                        transpose_fm(blk, lntmp, lnh_dst,
                                     pe_waits=[(sV, lnv)])
                        put(("lntmp",), sT)
                    lnh2_ready = sV.n

                    for ig in range(16):
                        @blk.gpsimd
                        def _(g, ig=ig):
                            g = TrackEng(g)
                            take(g, ("upw",))
                            dW.inc(load_pair(g, upw, upt, ig * 256), 16)
                        upr = dW.n

                        for icl in range(2):
                            icg = ig * 2 + icl
                            bank = banks[icl % 2]

                            @blk.tensor
                            def _(t, icl=icl, bank=bank, upr=upr,
                                  first=(ig == 0 and icl == 0)):
                                t = TrackEng(t)
                                t.wait_ge(dW.h, upr)
                                if first:
                                    t.wait_ge(sV.h, lnh2_ready)
                                take(t, bk(bank))
                                for kc in range(16):
                                    t.matmul(
                                        bank[:, :],
                                        upw[:, kc * 256 + icl * 128:
                                            kc * 256 + icl * 128 + 128],
                                        lnh2[:, kc * 512:(kc + 1) * 512],
                                        start=(kc == 0), stop=(kc == 15))
                                sT.inc(t.last)
                            umm = sT.n
                            if icl == 1:
                                put(("upw",), sT)

                            @blk.scalar
                            def _(sc, bank=bank, umm=umm):
                                sc = TrackEng(sc)
                                sc.wait_ge(sT.h, umm)
                                take(sc, ("gsh",))
                                sc.activation(gsh[:, :], bank[:, :], AF.Relu)
                                sA.inc(sc.last)
                            rl = sA.n
                            put(bk(bank), sA)

                            @blk.vector
                            def _(v, icg=icg, rl=rl):
                                v = TrackEng(v)
                                v.wait_ge(sA.h, rl)
                                v.tensor_tensor(
                                    shfm[:, icg * 512:(icg + 1) * 512],
                                    gsh[:, :], gsh[:, :], Op.mult)
                                sV.inc(v.last)
                            put(("gsh",), sV)
                    sh_ready = sV.n

                    for oc in range(4):
                        for s2b in range(2):
                            col0 = oc * 512 + s2b * 256

                            @blk.gpsimd
                            def _(g, col0=col0):
                                g = TrackEng(g)
                                take(g, ("dwn",), ("fc2b",))
                                ap1 = downt[0:2048, col0:col0 + 256] \
                                    .rearrange("(ic p) o -> p ic o", p=128)
                                ap2 = downt[2048:4096, col0:col0 + 256] \
                                    .rearrange("(ic p) o -> p ic o", p=128)
                                dW.inc(g.dma_start(dwn[:, 0:16 * 256], ap1),
                                       16)
                                dW.inc(g.dma_start(dwn[:, 16 * 256:], ap2),
                                       16)
                                apf = fc2t[:, col0:col0 + 256].rearrange(
                                    "(lc p) o -> p lc o", p=128)
                                dW.inc(g.dma_start(fc2b[:, :], apf), 16)
                            dwr = dW.n

                            for tl in range(4):
                                tt = half * 4 + tl
                                bank = banks[2 + tl % 2]

                                @blk.tensor
                                def _(t, tl=tl, tt=tt, bank=bank, dwr=dwr,
                                      first=(oc == 0 and s2b == 0
                                             and tl == 0)):
                                    t = TrackEng(t)
                                    t.wait_ge(dW.h, dwr)
                                    if first:
                                        t.wait_ge(sV.h, sh_ready)
                                        t.wait_ge(sV.h, ylb_ready)
                                    take(t, bk(bank))
                                    for ic in range(32):
                                        t.matmul(
                                            bank[:, 0:256],
                                            shfm[:, ic * 512 + tl * 128:
                                                 ic * 512 + tl * 128 + 128],
                                            dwn[:, ic * 256:(ic + 1) * 256],
                                            start=(ic == 0), stop=False)
                                    for lc in range(8):
                                        t.matmul(
                                            bank[:, 0:256],
                                            ylb[:, lc * P + tt * 128:
                                                lc * P + tt * 128 + 128],
                                            fc2b[:, lc * 256:(lc + 1) * 256],
                                            start=False, stop=(lc == 7))
                                    sT.inc(t.last)
                                dmm = sT.n
                                if tl == 3:
                                    put(("dwn",), sT)
                                    put(("fc2b",), sT)

                                @blk.vector
                                def _(v, tl=tl, oc=oc, s2b=s2b, bank=bank,
                                      dmm=dmm):
                                    v = TrackEng(v)
                                    v.wait_ge(sT.h, dmm)
                                    v.tensor_copy(
                                        otmh[:, tl * D + oc * 512
                                             + s2b * 256:
                                             tl * D + oc * 512
                                             + s2b * 256 + 256],
                                        bank[:, 0:256])
                                    sV.inc(v.last)
                                put(bk(bank), sV)
                    ffn_done = sV.n

                    for tl in range(4):
                        tt = half * 4 + tl
                        oap = otmh[:, tl * D:(tl + 1) * D]
                        ln_stats(blk, oap,
                                 [(sV, ffn_done)] if tl == 0 else [])

                        @blk.vector
                        def _(v, oap=oap, tt=tt):
                            v = TrackEng(v)
                            take(v, ("lntmp",))
                            v.tensor_scalar(lntmp[:, :], oap, mv2[:, 0:1],
                                            colA[:, :], Op.subtract, Op.mult)
                            v.tensor_tensor(xres[:, tt * D:(tt + 1) * D],
                                            xres[:, tt * D:(tt + 1) * D],
                                            lntmp[:, :], Op.add)
                            sV.inc(v.last)
                        put(("lntmp",), sV)
                        xdone = sV.n

                        @blk.sync
                        def _(s, tt=tt, xdone=xdone):
                            s = TrackEng(s)
                            s.wait_ge(sV.h, xdone)
                            dO.inc(s.dma_start(
                                out_hbm[tt * 128:(tt + 1) * 128, :],
                                xres[:, tt * D:(tt + 1) * D]), 16)

                @blk.sync
                def _(s):
                    s = TrackEng(s)
                    s.wait_ge(dO.h, dO.n)

        stkY.close()
        stkE.close()
    return nc


_CACHE = {}


def kernel(**inputs):
    import ml_dtypes

    x = np.asarray(inputs["L0_attn_hidden_states"], dtype=np.float32)
    B = x.shape[0]
    assert x.shape == (B, S2, D) and B == 4

    f = np.ascontiguousarray
    wq = np.asarray(inputs["L0_attn_Wq"], np.float32)
    wk = np.asarray(inputs["L0_attn_Wk"], np.float32)
    wv = np.asarray(inputs["L0_attn_Wv"], np.float32)
    wo = np.asarray(inputs["L0_attn_Wo"], np.float32)
    bq = np.asarray(inputs["L0_attn_bq"], np.float32)
    bkb = np.asarray(inputs["L0_attn_bk"], np.float32)
    bv = np.asarray(inputs["L0_attn_bv"], np.float32)
    bo = np.asarray(inputs["L0_attn_bo"], np.float32)
    R = np.asarray(inputs["L0_attn_rand_matrix"], np.float32)
    gate_w = np.asarray(inputs["L0_ffn_gate_w"], np.float32)
    gate_b = np.asarray(inputs["L0_ffn_gate_bias"], np.float32)
    fc1 = np.asarray(inputs["L0_ffn_fc1_w"], np.float32)
    fc2 = np.asarray(inputs["L0_ffn_fc2_w"], np.float32)
    w1 = np.asarray(inputs["L0_ffn_w1"], np.float32)
    w2 = np.asarray(inputs["L0_ffn_w2"], np.float32)
    up = np.asarray(inputs["L0_ffn_shared_up_w"], np.float32)
    down = np.asarray(inputs["L0_ffn_shared_down_w"], np.float32)

    wqT = f(wq.T)
    shared = {
        "wkt": f(wk.T), "wvt": f(wv.T), "wot": f(wo.T),
        "rt": f(R.T),
        "gatet": f(gate_w.T), "fc1t": f(fc1.T), "fc2t": f(fc2.T),
        "w1t": f(w1.transpose(0, 2, 1)), "w2t": f(w2.transpose(0, 2, 1)),
        "upt": f(up.T), "downt": f(down.T),
        "bkf": f(bkb.reshape(HKV, 128).T),
        "bvb": f(np.broadcast_to(bv, (128, HKV * 128))),
        "bob": f(np.broadcast_to(bo, (128, D))),
        "gbb": f(np.broadcast_to(gate_b, (128, NE))),
    }
    bqf_full = f(bq.reshape(16, 128).T)

    ident = np.eye(128, dtype=ml_dtypes.bfloat16)
    sel = np.zeros((32, NE * 128), dtype=ml_dtypes.bfloat16)
    for e in range(NE):
        sel[e, e * 128:(e + 1) * 128] = 1
    ones1 = np.ones((128, 1), dtype=ml_dtypes.bfloat16)
    hd = 128
    t32 = np.arange(0, hd, 2).astype(np.float32) / np.float32(hd)
    inv_freq = (np.float32(1.0) / (np.float32(10000.0) ** t32)) \
        .astype(np.float32)
    pos = np.arange(S2, dtype=np.float32)
    freqs = (pos[:, None] * inv_freq[None, :]).astype(np.float32)
    emb = np.concatenate([freqs, freqs], axis=1)
    cos_t = np.cos(emb).astype(np.float32)
    sin_t = np.sin(emb).astype(np.float32)
    i_idx = np.arange(S2)
    tperm = 16 * (i_idx % 128) + (i_idx // 128)
    cosf = f(cos_t[tperm, :].T)
    sinf = f(sin_t[tperm, :].T)
    consts = {"ident": ident, "sel32": sel, "ones1": ones1,
              "cosf": cosf, "sinf": sinf}

    in_maps = []
    for core in range(8):
        b, hh = core // 2, core % 2
        m = dict(shared)
        m.update(consts)
        m["xfull"] = f(x[b])
        m["xr"] = f(x[b, hh * P:(hh + 1) * P, :])
        m["wqt"] = f(wqT[:, hh * 1024:(hh + 1) * 1024])
        m["bqf"] = f(bqf_full[:, hh * 8:(hh + 1) * 8])
        in_maps.append(m)

    if "nc" not in _CACHE:
        _CACHE["nc"] = build()
    nc = _CACHE["nc"]
    res = run_bass_kernel_spmd(nc, in_maps, core_ids=list(range(8)))
    out = np.empty((B, S2, D), dtype=np.float32)
    for core in range(8):
        b, hh = core // 2, core % 2
        out[b, hh * P:(hh + 1) * P, :] = res.results[core]["out"]
    return out


# version 07
# speedup vs baseline: 1.1195x; 1.1195x over previous
import os
import numpy as np
import concourse.bass as bass
import concourse.mybir as mybir
from bass_rust import BassInstruction as _BassInstruction
from concourse.bass_utils import run_bass_kernel_spmd
from concourse.alu_op_type import AluOpType as Op
from contextlib import ExitStack

F32 = mybir.dt.float32
BF16 = mybir.dt.bfloat16
AF = mybir.ActivationFunctionType
AX = mybir.AxisListType

PI = float(np.pi)
INV2PI = float(1.0 / (2 * np.pi))
TWOPI = float(2 * np.pi)
MAGIC = float(1.5 * 2 ** 23)
NEG = -1e30
NEG2 = -2e30

P = 1024
S2 = 2048
D = 2048
NKC = 16
HL = 8
HKV = 8
LAT = 1024
NE = 32
SH = 4096
EPS_LN = 1e-5
SFRAC = float(2.0 / 128.0)


class Sem:
    __slots__ = ("h", "n")

    def __init__(self, h):
        self.h = h
        self.n = 0

    def inc(self, inst, k=1):
        inst.then_inc(self.h, k)
        self.n += k
        return self.n


_NOTRACK = {"wait_ge", "nop", "sem_inc", "sem_clear", "print", "breakpoint",
            "maybe_wait_ge", "drain"}


class TrackEng:

    def __init__(self, e):
        object.__setattr__(self, "_e", e)
        object.__setattr__(self, "last", None)

    def __getattr__(self, n):
        a = getattr(object.__getattribute__(self, "_e"), n)
        if callable(a) and n not in _NOTRACK:
            def w(*args, **kw):
                r = a(*args, **kw)
                if isinstance(r, _BassInstruction):
                    object.__setattr__(self, "last", r)
                return r
            return w
        return a


STAGE = int(os.environ.get("MK_STAGE", "5"))
SUB = int(os.environ.get("MK_SUB", "9"))
DEBUG = int(os.environ.get("MK_DEBUG", "0"))


def build():
    nc = bass.Bass("TRN2", target_bir_lowering=False)

    def din(name, shape, dt=F32):
        return nc.dram_tensor(name, shape, dt, kind="ExternalInput").ap()

    xfull = din("xfull", [S2, D])
    xr_in = din("xr", [P, D])
    out_hbm = nc.dram_tensor("out", [P, D], F32, kind="ExternalOutput").ap()
    dbg = {}
    if DEBUG:
        for nm, w in [("d_rinv", 64), ("d_wrp", 1024), ("d_abs", 1024),
                      ("d_qp", 1024), ("d_kp", 1024), ("d_sred", 64),
                      ("d_acc", 1024), ("d_qf", 4096), ("d_kf", 4096),
                      ("d_vtm", 1024), ("d_lnq", 2048), ("d_n2", 64)]:
            dbg[nm] = nc.dram_tensor(nm, [128, w], F32,
                                     kind="ExternalOutput").ap()

    wqt = din("wqt", [D, HL * 128])
    wkt = din("wkt", [D, HKV * 128])
    wvt = din("wvt", [D, HKV * 128])
    wot = din("wot", [D, D])
    rt_in = din("rt", [128, 64])
    gatet_in = din("gatet", [D, NE]); fc1t = din("fc1t", [D, LAT])
    fc2t = din("fc2t", [LAT, D])
    w1t = din("w1t", [NE, LAT, LAT]); w2t = din("w2t", [NE, LAT, LAT])
    upt = din("upt", [D, SH]); downt = din("downt", [SH, D])
    bqf_in = din("bqf", [128, HL]); bkf_in = din("bkf", [128, HKV])
    bvb_in = din("bvb", [128, HKV * 128]); bob_in = din("bob", [128, D])
    gbb_in = din("gbb", [128, NE])
    cosf_in = din("cosf", [128, S2]); sinf_in = din("sinf", [128, S2])
    ident_in = din("ident", [128, 128], BF16)
    sel32_in = din("sel32", [32, NE * 128], BF16)
    ones1_in = din("ones1", [128, 1], BF16)

    top = ExitStack()

    def sbt(stack, name, shape, dt=F32):
        return stack.enter_context(nc.sbuf_tensor(name, shape, dt))

    with top:
        xres = sbt(top, "xres", [128, 8 * D])
        ident = sbt(top, "ident_s", [128, 128], BF16)
        ones1 = sbt(top, "ones1_s", [128, 1], BF16)
        stat = sbt(top, "stat", [128, 24])
        mv2 = sbt(top, "mv2", [128, 2])
        colA = sbt(top, "colA", [128, 1])
        lntmp = sbt(top, "lntmp", [128, D], BF16)
        halfpi = sbt(top, "halfpi", [128, 1])
        nhalf = sbt(top, "nhalf", [128, 64])
        bqf = sbt(top, "bqf_s", [128, HL])
        bkf = sbt(top, "bkf_s", [128, HKV])
        gbb = sbt(top, "gbb_s", [128, NE])

        banks = [top.enter_context(nc.psum_tensor(f"bk{i}", [128, 512], F32))
                 for i in range(6)]
        psbf = [top.enter_context(nc.psum_tensor(f"pb{i}", [128, 1024], BF16))
                for i in range(2)]

        dW = Sem(top.enter_context(nc.semaphore("dW")))
        dX = Sem(top.enter_context(nc.semaphore("dX")))
        dO = Sem(top.enter_context(nc.semaphore("dO")))
        sT = Sem(top.enter_context(nc.semaphore("sT")))
        sV = Sem(top.enter_context(nc.semaphore("sV")))
        sA = Sem(top.enter_context(nc.semaphore("sA")))
        sG = Sem(top.enter_context(nc.semaphore("sG")))

        guard = {}

        def take(eng, *keys):
            for key in keys:
                g = guard.pop(key, None)
                if g is not None:
                    eng.wait_ge(g[0].h, g[1])

        def put(key, sem):
            guard[key] = (sem, sem.n)

        def bk(b):
            return ("bank", id(b))

        def kick(eng, sem):
            sem.inc(eng.last)
            eng.wait_ge(sem.h, sem.n)

        with nc.Block() as blk:
            @blk.sync
            def _(s):
                s = TrackEng(s)
                for tt in range(8):
                    dX.inc(s.dma_start(xres[:, tt * D:(tt + 1) * D],
                                       xr_in[tt * 128:(tt + 1) * 128, :]), 16)
                dX.inc(s.dma_start(bqf[:, :], bqf_in[:, :]), 16)
                dX.inc(s.dma_start(bkf[:, :], bkf_in[:, :]), 16)
                dX.inc(s.dma_start(gbb[:, :], gbb_in[:, :]), 16)
                dX.inc(s.dma_start(ident[:, :], ident_in[:, :]), 16)
                dX.inc(s.dma_start(ones1[:, :], ones1_in[:, :]), 16)

            @blk.vector
            def _(v):
                v = TrackEng(v)
                v.memset(halfpi[:, :], PI / 2)
                v.memset(nhalf[:, :], -0.5)
                sV.inc(v.last)
        x_loaded = dX.n
        consts_v = sV.n

        def load_pair(g, buf, src, col0, width=256):
            ap = src[:, col0:col0 + width].rearrange("(kc p) o -> p kc o",
                                                     p=128)
            return g.dma_start(buf[:, 0:NKC * width], ap)

        def ln_stats(blk, src_ap, wait_sems=()):
            @blk.vector
            def _(v):
                v = TrackEng(v)
                for sem, val in wait_sems:
                    v.wait_ge(sem.h, val)
                for c in range(4):
                    v.bn_stats(stat[:, c * 6:(c + 1) * 6],
                               src_ap[:, c * 512:(c + 1) * 512])
                kick(v, sV)
                v.bn_aggr(mv2[:, :], stat[:, :])
                kick(v, sV)
                v.tensor_scalar(colA[:, :], mv2[:, 1:2], EPS_LN, None, Op.add)
                sV.inc(v.last)
            vst = sV.n

            @blk.gpsimd
            def _(g):
                g = TrackEng(g)
                g.wait_ge(sV.h, vst)
                g.wait_ge(sV.h, consts_v)
                g.tensor_tensor(colA[:, :], colA[:, :], nhalf[:, 0:1], Op.pow)
                sG.inc(g.last)
            gst = sG.n

            @blk.vector
            def _(v):
                v = TrackEng(v)
                v.wait_ge(sG.h, gst)

        def transpose_fm(blk, src, dst_ap_fn, pe_waits=()):
            @blk.tensor
            def _(t):
                t = TrackEng(t)
                for sem, val in pe_waits:
                    t.wait_ge(sem.h, val)
                take(t, ("psbf", 0), ("psbf", 1))
                for kc in range(16):
                    b, j = kc // 8, kc % 8
                    t.transpose(psbf[b][:, j * 128:(j + 1) * 128],
                                src[:, kc * 128:(kc + 1) * 128], ident[:, :])
                sT.inc(t.last)
            tdone = sT.n

            @blk.vector
            def _(v):
                v = TrackEng(v)
                v.wait_ge(sT.h, tdone)
                for b in range(2):
                    v.tensor_copy(dst_ap_fn(b),
                                  psbf[b][:, :].rearrange("p (j c) -> p j c",
                                                          j=8, c=128))
                sV.inc(v.last)
            put(("psbf", 0), sV)
            put(("psbf", 1), sV)
            return sV.n

        stkAb = ExitStack()
        with stkAb:
            wpA = sbt(stkAb, "wpA", [128, NKC * 256], BF16)
            wpB = sbt(stkAb, "wpB", [128, NKC * 256], BF16)
            afm = sbt(stkAb, "afm", [128, HL * 2048], BF16)
            stkA = ExitStack()
            stkA.__enter__()
            cosf = sbt(stkA, "cosf_sb", [128, 512], BF16)
            sinf = sbt(stkA, "sinf_sb", [128, 512], BF16)
            bvb = sbt(stkA, "bvb_sb", [128, HKV * 128], BF16)
            rt = sbt(stkA, "rt_sb", [128, 64], BF16)
            evtmp = sbt(stkA, "evt", [128, 512], BF16)
            rotb = sbt(stkA, "rot", [128, 512], BF16)
            xtile = sbt(stkA, "xtile", [128, D])
            lnq2 = [sbt(stkA, f"lnq{i}", [128, NKC * 512], BF16)
                    for i in range(2)]
            qf = sbt(stkA, "qf", [128, HL * 512], BF16)
            kf = sbt(stkA, "kf", [128, HKV * 512], BF16)
            vtm = sbt(stkA, "vtm", [128, 4 * HKV * 128], BF16)
            sqb = sbt(stkA, "sqb", [128, 512], BF16)
            rinv = sbt(stkA, "rinv", [128, 64])
            pbuf = sbt(stkA, "pbuf", [128, 1024])
            wrp = sbt(stkA, "wrp", [128, 1024])
            qpk = sbt(stkA, "qpk", [128, 16 * 128], BF16)
            sred = sbt(stkA, "sred", [128, HL * HKV])
            term = sbt(stkA, "term", [128, HL * 128], BF16)
            acc = sbt(stkA, "acc", [128, HL * 128], BF16)

            with nc.Block() as blk:
                @blk.gpsimd
                def _(g):
                    g = TrackEng(g)
                    dW.inc(g.dma_start(bvb[:, :], bvb_in[:, :]), 16)
                    dW.inc(g.dma_start(rt[:, :], rt_in[:, :]), 16)
                consts_W = dW.n

                xperm = xfull.rearrange("(p j) d -> j p d", j=16)

                for c in range(4 if STAGE >= 2 else 1):
                    @blk.gpsimd
                    def _(g, c=c):
                        g = TrackEng(g)
                        take(g, ("trig",))
                        dW.inc(g.dma_start(cosf[:, :],
                                           cosf_in[:, c * 512:(c + 1) * 512]),
                               16)
                        dW.inc(g.dma_start(sinf[:, :],
                                           sinf_in[:, c * 512:(c + 1) * 512]),
                               16)
                    trig_W = dW.n

                    def emit_ln_chunk(cc):
                        lnqd = lnq2[cc % 2]
                        for tl in range(4):
                            j = 4 * cc + tl

                            @blk.sync
                            def _(s, j=j):
                                s = TrackEng(s)
                                take(s, ("xtile",))
                                dX.inc(s.dma_start(xtile[:, :],
                                                   xperm[j, :, :]), 16)
                            xtl = dX.n

                            ln_stats(blk, xtile[:, :], [(dX, xtl)])

                            @blk.vector
                            def _(v):
                                v = TrackEng(v)
                                take(v, ("lntmp",))
                                v.tensor_scalar(lntmp[:, :], xtile[:, :],
                                                mv2[:, 0:1], colA[:, :],
                                                Op.subtract, Op.mult)
                                sV.inc(v.last)
                            lnv = sV.n
                            put(("xtile",), sV)

                            def lnq_dst(b, tl=tl, lnqd=lnqd):
                                base = lnqd[:, :].rearrange(
                                    "p (kc w) -> p kc w", kc=16, w=512)
                                return base[:, b * 8:(b + 1) * 8,
                                            tl * 128:(tl + 1) * 128]
                            transpose_fm(blk, lntmp, lnq_dst,
                                         pe_waits=[(sV, lnv)])
                            put(("lntmp",), sT)
                        return sV.n

                    if c == 0:
                        lnq_ready = emit_ln_chunk(0)
                    else:
                        lnq_ready = lnq_ready_next
                    lnq = lnq2[c % 2]

                    plans = ([("q", wqt, i) for i in range(4)]
                             + [("k", wkt, i) for i in range(4)])
                    if SUB < 2:
                        plans = []
                    for pi, (kind, src, i2) in enumerate(plans):
                        buf = [wpA, wpB][pi % 2]

                        @blk.gpsimd
                        def _(g, buf=buf, src=src, i2=i2, pi=pi):
                            g = TrackEng(g)
                            take(g, ("wp", pi % 2))
                            dW.inc(load_pair(g, buf, src, i2 * 256), 16)
                        wready = dW.n

                        for sub in range(2):
                            h = i2 * 2 + sub
                            dst, bias = (qf, bqf) if kind == "q" else (kf, bkf)
                            bank = banks[(pi * 2 + sub) % 2]

                            @blk.tensor
                            def _(t, buf=buf, sub=sub, bank=bank,
                                  wready=wready,
                                  first=(pi == 0 and sub == 0)):
                                t = TrackEng(t)
                                t.wait_ge(dW.h, wready)
                                if first:
                                    t.wait_ge(sV.h, lnq_ready)
                                take(t, bk(bank))
                                for kc in range(16):
                                    t.matmul(bank[:, :],
                                             buf[:, kc * 256 + sub * 128:
                                                 kc * 256 + sub * 128 + 128],
                                             lnq[:, kc * 512:(kc + 1) * 512],
                                             start=(kc == 0), stop=(kc == 15))
                                sT.inc(t.last)
                            mmdone = sT.n
                            if sub == 1:
                                put(("wp", pi % 2), sT)

                            @blk.vector
                            def _(v, bank=bank, h=h, dst=dst, bias=bias,
                                  mmdone=mmdone,
                                  firstc=(pi == 0 and sub == 0 and c == 0)):
                                v = TrackEng(v)
                                v.wait_ge(sT.h, mmdone)
                                if firstc:
                                    v.wait_ge(dW.h, consts_W)
                                v.wait_ge(dW.h, trig_W)
                                v.tensor_scalar(evtmp[:, :], bank[:, :],
                                                bias[:, h:h + 1], None, Op.add)
                                v.tensor_scalar(rotb[0:64, :],
                                                evtmp[64:128, :], -1.0, None,
                                                Op.mult)
                                v.tensor_copy(rotb[64:128, :], evtmp[0:64, :])
                                v.tensor_tensor(evtmp[:, :], evtmp[:, :],
                                                cosf[:, :], Op.mult)
                                v.tensor_tensor(rotb[:, :], rotb[:, :],
                                                sinf[:, :], Op.mult)
                                v.tensor_tensor(dst[:, h * 512:(h + 1) * 512],
                                                evtmp[:, :], rotb[:, :],
                                                Op.add)
                                sV.inc(v.last)
                            evdone = sV.n
                            put(bk(bank), sV)

                            @blk.vector
                            def _(v, h=h, dst=dst):
                                v = TrackEng(v)
                                take(v, ("sqb",))
                                v.tensor_tensor(sqb[:, :],
                                                dst[:, h * 512:(h + 1) * 512],
                                                dst[:, h * 512:(h + 1) * 512],
                                                Op.mult)
                                sV.inc(v.last)
                            sqdone = sV.n

                            hidx = h if kind == "q" else HL + h
                            nb = banks[5]

                            @blk.tensor
                            def _(t, sqdone=sqdone, hidx=hidx, nb=nb):
                                t = TrackEng(t)
                                t.wait_ge(sV.h, sqdone)
                                take(t, bk(nb))
                                for tl in range(4):
                                    t.matmul(nb[:, tl * 16 + hidx:
                                                tl * 16 + hidx + 1],
                                             sqb[:, tl * 128:(tl + 1) * 128],
                                             ones1[:, :], start=True,
                                             stop=True)
                                sT.inc(t.last)
                            put(("sqb",), sT)
                    rope_done = sV.n
                    put(("trig",), sV)
                    n2done = sT.n

                    for i2 in range(4 if SUB >= 3 else 0):
                        buf = [wpA, wpB][i2 % 2]

                        @blk.gpsimd
                        def _(g, buf=buf, i2=i2):
                            g = TrackEng(g)
                            take(g, ("wp", i2 % 2))
                            dW.inc(load_pair(g, buf, wvt, i2 * 256), 16)
                        vwready = dW.n

                        for tl in range(4):
                            bank = banks[2 + (tl % 2)]

                            @blk.tensor
                            def _(t, buf=buf, i2=i2, tl=tl, bank=bank,
                                  vwready=vwready):
                                t = TrackEng(t)
                                t.wait_ge(dW.h, vwready)
                                take(t, bk(bank))
                                for kc in range(16):
                                    t.matmul(
                                        bank[:, 0:256],
                                        lnq[:, kc * 512 + tl * 128:
                                            kc * 512 + tl * 128 + 128],
                                        buf[:, kc * 256:(kc + 1) * 256],
                                        start=(kc == 0), stop=(kc == 15))
                                sT.inc(t.last)
                            mmdone = sT.n
                            if tl == 3:
                                put(("wp", i2 % 2), sT)

                            @blk.vector
                            def _(v, i2=i2, tl=tl, bank=bank, mmdone=mmdone):
                                v = TrackEng(v)
                                v.wait_ge(sT.h, mmdone)
                                v.tensor_tensor(
                                    vtm[:, tl * 1024 + i2 * 256:
                                        tl * 1024 + i2 * 256 + 256],
                                    bank[:, 0:256],
                                    bvb[:, i2 * 256:(i2 + 1) * 256], Op.add)
                                sV.inc(v.last)
                            put(bk(bank), sV)

                    if SUB < 4:
                        continue
                    @blk.vector
                    def _(v, n2done=n2done):
                        v = TrackEng(v)
                        v.wait_ge(sT.h, n2done)
                        take(v, ("rinv",))
                        v.tensor_copy(rinv[:, :], banks[5][:, 0:64])
                        sV.inc(v.last)
                    ncp = sV.n
                    put(bk(banks[5]), sV)

                    @blk.gpsimd
                    def _(g, ncp=ncp):
                        g = TrackEng(g)
                        g.wait_ge(sV.h, ncp)
                        g.wait_ge(sV.h, consts_v)
                        g.tensor_tensor(rinv[:, :], rinv[:, :],
                                        nhalf[:, 0:64], Op.pow)
                        sG.inc(g.last)
                    ngp = sG.n

                    @blk.vector
                    def _(v, ngp=ngp):
                        v = TrackEng(v)
                        v.wait_ge(sG.h, ngp)
                        v.tensor_scalar(rinv[:, :], rinv[:, :], 1e6, None,
                                        Op.min)
                        sV.inc(v.last)
                    rinvdone = sV.n

                    nchunks = 4 if STAGE >= 2 else 1
                    if c + 1 < nchunks:
                        lnq_ready_next = emit_ln_chunk(c + 1)

                    for tl in range(4):
                        j = 4 * c + tl
                        pb_q, pb_k = banks[2], banks[3]

                        @blk.tensor
                        def _(t, tl=tl, first=(tl == 0)):
                            t = TrackEng(t)
                            if first:
                                t.wait_ge(sV.h, rope_done)
                                t.wait_ge(dW.h, consts_W)
                            take(t, bk(pb_q), bk(pb_k))
                            for h in range(HL):
                                t.matmul(pb_q[:, h * 64:h * 64 + 64],
                                         qf[:, h * 512 + tl * 128:
                                            h * 512 + tl * 128 + 128],
                                         rt[:, :], start=True, stop=True)
                            for h in range(HKV):
                                t.matmul(pb_k[:, h * 64:h * 64 + 64],
                                         kf[:, h * 512 + tl * 128:
                                            h * 512 + tl * 128 + 128],
                                         rt[:, :], start=True, stop=True)
                            sT.inc(t.last)
                        pdone = sT.n

                        @blk.vector
                        def _(v, tl=tl, pdone=pdone):
                            v = TrackEng(v)
                            v.wait_ge(sT.h, pdone)
                            v.wait_ge(sV.h, rinvdone)
                            take(v, ("pwrp",))
                            for h in range(16):
                                bank = pb_q if h < 8 else pb_k
                                v.tensor_scalar(
                                    pbuf[:, h * 64:(h + 1) * 64],
                                    bank[:, (h % 8) * 64:(h % 8) * 64 + 64],
                                    rinv[:, tl * 16 + h:tl * 16 + h + 1],
                                    None, Op.mult)
                            kick(v, sV)
                            v.tensor_scalar(wrp[:, :], pbuf[:, :], INV2PI,
                                            MAGIC, Op.mult, Op.add)
                            v.tensor_scalar(wrp[:, :], wrp[:, :], MAGIC,
                                            None, Op.subtract)
                            v.scalar_tensor_tensor(wrp[:, :], wrp[:, :],
                                                   -TWOPI, pbuf[:, :],
                                                   Op.mult, Op.add)
                            sV.inc(v.last)
                        wdone = sV.n
                        put(bk(pb_q), sV)
                        put(bk(pb_k), sV)
                        if tl == 3:
                            put(("rinv",), sV)

                        @blk.vector
                        def _(v, wdone=wdone):
                            v = TrackEng(v)
                            v.scalar_tensor_tensor(pbuf[:, :], wrp[:, :],
                                                   -1.0, wrp[:, :],
                                                   Op.mult, Op.max)
                            sV.inc(v.last)
                        absdone = sV.n

                        @blk.scalar
                        def _(sc, absdone=absdone):
                            sc = TrackEng(sc)
                            sc.wait_ge(sV.h, absdone)
                            take(sc, ("qpkp",))
                            qs = qpk[:, :].rearrange("p (h w) -> p h w",
                                                     h=16, w=128)
                            wq_ = wrp[:, :].rearrange(
                                "p (h w) -> p h w", h=16, w=64)
                            aq = pbuf[:, :].rearrange(
                                "p (h w) -> p h w", h=16, w=64)
                            sc.activation(qs[:, :, 0:64], wq_, AF.Sin)
                            sc.activation(qs[:, :, 64:128], aq, AF.Sin,
                                          scale=-1.0, bias=halfpi[:, :])
                            sA.inc(sc.last)
                        featdone = sA.n
                        put(("pwrp",), sA)

                        @blk.vector
                        def _(v, tl=tl, featdone=featdone):
                            v = TrackEng(v)
                            v.wait_ge(sA.h, featdone)
                            kp3 = qpk[:, 1024:2048].rearrange(
                                "p (a b) -> p a b", a=8, b=128)
                            t1k = term[:, :].rearrange("p (a b) -> p a b",
                                                       a=8, b=128)
                            for h in range(HL):
                                qph = qpk[:, h * 128:(h + 1) * 128]
                                v.tensor_tensor(
                                    t1k, kp3,
                                    qph.unsqueeze(1).to_broadcast(
                                        [128, 8, 128]), Op.mult)
                                v.tensor_reduce(sred[:, h * 8:(h + 1) * 8],
                                                t1k, AX.X, Op.add)
                            kick(v, sV)
                            v.tensor_scalar(sred[:, :], sred[:, :], SFRAC,
                                            None, Op.mult)
                            kick(v, sV)
                            s3 = sred[:, :].rearrange("p (a b) -> p a b",
                                                      a=HL, b=HKV)
                            t3 = term[:, :].rearrange("p (a b) -> p a b",
                                                      a=HL, b=128)
                            for hp in range(HKV):
                                vk = vtm[:, tl * 1024 + hp * 128:
                                         tl * 1024 + hp * 128 + 128]
                                v.tensor_tensor(
                                    t3,
                                    vk.unsqueeze(1).to_broadcast(
                                        [128, HL, 128]),
                                    s3[:, :, hp:hp + 1].to_broadcast(
                                        [128, HL, 128]), Op.mult)
                                if hp == 0:
                                    take(v, ("accb",))
                                    v.tensor_copy(acc[:, :], term[:, :])
                                else:
                                    v.tensor_tensor(acc[:, :], acc[:, :],
                                                    term[:, :], Op.add)
                            sV.inc(v.last)
                        put(("qpkp",), sV)
                        accdone = sV.n

                        pbx = psbf[tl % 2]

                        @blk.tensor
                        def _(t, pbx=pbx, tl=tl, accdone=accdone):
                            t = TrackEng(t)
                            t.wait_ge(sV.h, accdone)
                            take(t, ("psbf", tl % 2))
                            for h in range(HL):
                                t.transpose(pbx[:, h * 128:(h + 1) * 128],
                                            acc[:, h * 128:(h + 1) * 128],
                                            ident[:, :])
                            sT.inc(t.last)
                        tpd = sT.n
                        put(("accb",), sT)

                        @blk.vector
                        def _(v, pbx=pbx, j=j, tpd=tpd):
                            v = TrackEng(v)
                            v.wait_ge(sT.h, tpd)
                            base = afm[:, :].rearrange("p (h jj w) -> p h jj w",
                                                       h=HL, jj=16, w=128)
                            v.tensor_copy(
                                base[:, :, j, :],
                                pbx[:, :].rearrange("p (h w) -> p h w",
                                                    h=HL, w=128))
                            sV.inc(v.last)
                        put(("psbf", tl % 2), sV)

                        if DEBUG and c == 0 and tl == 0:
                            dbg_done = sV.n

                            @blk.gpsimd
                            def _(g):
                                g.wait_ge(sV.h, dbg_done)
                                dW.inc(g.dma_start(dbg["d_rinv"],
                                                   rinv[:, 0:64]), 16)
                                dW.inc(g.dma_start(dbg["d_wrp"], wrp[:, :]),
                                       16)
                                dW.inc(g.dma_start(dbg["d_abs"], pbuf[:, :]),
                                       16)
                                dW.inc(g.dma_start(dbg["d_qp"], qp[:, :]), 16)
                                dW.inc(g.dma_start(dbg["d_kp"], kp[:, :]), 16)
                                dW.inc(g.dma_start(dbg["d_sred"],
                                                   sred[:, :]), 16)
                                dW.inc(g.dma_start(dbg["d_acc"], acc[:, :]),
                                       16)
                                dW.inc(g.dma_start(dbg["d_qf"],
                                                   qf[:, 0:4096]), 16)
                                dW.inc(g.dma_start(dbg["d_kf"],
                                                   kf[:, 0:4096]), 16)
                                dW.inc(g.dma_start(dbg["d_vtm"],
                                                   vtm[:, 0:1024]), 16)
                                dW.inc(g.dma_start(dbg["d_lnq"],
                                                   lnq[:, 0:2048]), 16)
                            dbg_W = dW.n

                            @blk.vector
                            def _(v):
                                v = TrackEng(v)
                                v.wait_ge(dW.h, dbg_W)
                afm_ready = sV.n

            stkA.__exit__(None, None, None)
            otm4 = sbt(stkAb, "otm4", [128, 4 * D])
            wpC = sbt(stkAb, "wpC", [128, NKC * 256], BF16)
            wpD = sbt(stkAb, "wpD", [128, NKC * 256], BF16)
            bob = sbt(stkAb, "bob_sb", [128, D], BF16)
            wp4 = [wpA, wpB, wpC, wpD]

            with nc.Block() as blk:
                @blk.gpsimd
                def _(g):
                    g = TrackEng(g)
                    dW.inc(g.dma_start(bob[:, :], bob_in[:, :]), 16)
                bob_W = dW.n
                wo_li = 0

                def emit_wo_load(oc, s2b, woready):
                    nonlocal wo_li
                    buf = wp4[wo_li % 4]

                    @blk.gpsimd
                    def _(g, buf=buf, oc=oc, s2b=s2b, li=wo_li):
                        g = TrackEng(g)
                        take(g, ("wp4", li % 4))
                        dW.inc(load_pair(g, buf, wot,
                                         oc * 512 + s2b * 256), 16)
                    woready[(oc, s2b)] = (dW.n, wo_li % 4)
                    wo_li += 1

                for rh in range(2 if STAGE >= 2 else 0):
                    woready = {}
                    for oc0 in (0, 1):
                        for s2b in range(2):
                            emit_wo_load(oc0, s2b, woready)

                    for oc in range(4):
                        if oc >= 2 and rh == 0:
                            pass
                        for sub in range(4):
                            r = 4 * rh + sub
                            bank = banks[(oc * 4 + sub) % 2]

                            @blk.tensor
                            def _(t, r=r, sub=sub, oc=oc, bank=bank,
                                  first=(rh == 0 and oc == 0 and sub == 0)):
                                t = TrackEng(t)
                                if first:
                                    t.wait_ge(sV.h, afm_ready)
                                take(t, bk(bank))
                                for s2b in range(2):
                                    wr, bi = woready[(oc, s2b)]
                                    t.wait_ge(dW.h, wr)
                                    buf = wp4[bi]
                                    for kc in range(16):
                                        t.matmul(
                                            bank[:, s2b * 256:
                                                 s2b * 256 + 256],
                                            afm[:, r * 2048 + kc * 128:
                                                r * 2048 + kc * 128 + 128],
                                            buf[:, kc * 256:(kc + 1) * 256],
                                            start=(kc == 0), stop=(kc == 15))
                                sT.inc(t.last)
                            mmdone = sT.n
                            if sub == 3:
                                put(("wp4", woready[(oc, 0)][1]), sT)
                                put(("wp4", woready[(oc, 1)][1]), sT)

                            @blk.vector
                            def _(v, sub=sub, oc=oc, bank=bank,
                                  mmdone=mmdone):
                                v = TrackEng(v)
                                v.wait_ge(sT.h, mmdone)
                                v.wait_ge(dW.h, bob_W)
                                v.tensor_tensor(
                                    otm4[:, sub * D + oc * 512:
                                         sub * D + oc * 512 + 512],
                                    bank[:, :],
                                    bob[:, oc * 512:(oc + 1) * 512], Op.add)
                                sV.inc(v.last)
                            put(bk(bank), sV)
                        if oc < 2:
                            for s2b in range(2):
                                emit_wo_load(oc + 2, s2b, woready)
                    wo_done = sV.n

                    for sub in range(4):
                        r = 4 * rh + sub
                        oap = otm4[:, sub * D:(sub + 1) * D]
                        ln_stats(blk, oap,
                                 [(sV, wo_done), (dX, x_loaded)]
                                 if sub == 0 else [])

                        @blk.vector
                        def _(v, oap=oap, r=r):
                            v = TrackEng(v)
                            take(v, ("lntmp",))
                            v.tensor_scalar(lntmp[:, :], oap, mv2[:, 0:1],
                                            colA[:, :], Op.subtract, Op.mult)
                            v.tensor_tensor(xres[:, r * D:(r + 1) * D],
                                            xres[:, r * D:(r + 1) * D],
                                            lntmp[:, :], Op.add)
                            sV.inc(v.last)
                        put(("lntmp",), sV)
        attn_done = sV.n

        if STAGE < 5:
            with nc.Block() as blk:
                @blk.sync
                def _(s):
                    s.wait_ge(sV.h, sV.n)
                    for tt in range(8):
                        dO.inc(s.dma_start(
                            out_hbm[tt * 128:(tt + 1) * 128, :],
                            xres[:, tt * D:(tt + 1) * D]), 16)
                    s.wait_ge(dO.h, dO.n)

        stkY = ExitStack()
        ylb = sbt(stkY, "ylb", [128, 8 * P], BF16)
        stkE = ExitStack()
        sel32 = sbt(stkE, "sel32_s", [32, NE * 128], BF16)
        cwfm = sbt(stkE, "cwfm", [32, P], BF16)
        stkX = ExitStack()
        xlat = sbt(stkX, "xlat", [128, 8 * P], BF16)
        ysb = sbt(stkX, "ysb", [128, 8 * P])

        gate_ready = cwfm_ready = xlat_ready = 0

        stkL = ExitStack()
        with stkL:
            lnf = sbt(stkL, "lnf2", [128, NKC * P], BF16)
            wpAf = sbt(stkL, "wpAf", [128, NKC * 256], BF16)
            wpBf = sbt(stkL, "wpBf", [128, NKC * 256], BF16)
            gateb = sbt(stkL, "gateb", [128, NKC * NE], BF16)
            scs = sbt(stkL, "scs", [128, NE])
            sbb = sbt(stkL, "sbb", [128, NE])
            g8 = sbt(stkL, "g8", [128, 4 * 8])
            gpad = sbt(stkL, "gpad", [128, 12])
            gm8 = sbt(stkL, "gm8", [128, 8])
            repl8 = sbt(stkL, "repl8", [128, 8])
            gzap = sbt(stkL, "gzap", [128, 12])
            gmask = sbt(stkL, "gmask", [128, 4])
            msb = sbt(stkL, "msb", [128, NE])
            mzap = sbt(stkL, "mzap", [128, NE])
            selm = sbt(stkL, "selm", [128, NE])
            t8v = sbt(stkL, "t8v", [128, 8])
            cwb16 = sbt(stkL, "cwb16", [128, NE], BF16)

            with nc.Block() as blk:
                @blk.gpsimd
                def _(g):
                    g = TrackEng(g)
                    ap = gatet_in[:, :].rearrange("(kc p) o -> p kc o", p=128)
                    dW.inc(g.dma_start(gateb[:, :], ap), 16)
                    dW.inc(g.dma_start(sel32[:, :], sel32_in[:, :]), 16)
                gate_ready = dW.n

                for tt in range(8 if STAGE >= 3 else 0):
                    xap = xres[:, tt * D:(tt + 1) * D]
                    ln_stats(blk, xap,
                             [(sV, attn_done)] if tt == 0 else [])

                    @blk.vector
                    def _(v, xap=xap):
                        v = TrackEng(v)
                        take(v, ("lntmp",))
                        v.tensor_scalar(lntmp[:, :], xap, mv2[:, 0:1],
                                        colA[:, :], Op.subtract, Op.mult)
                        sV.inc(v.last)
                    lnv = sV.n

                    def lnf_dst(b, tt=tt):
                        base = lnf[:, :].rearrange("p (kc w) -> p kc w",
                                                   kc=16, w=P)
                        return base[:, b * 8:(b + 1) * 8,
                                    tt * 128:(tt + 1) * 128]
                    transpose_fm(blk, lntmp, lnf_dst, pe_waits=[(sV, lnv)])
                    put(("lntmp",), sT)
                ln2_ready = sV.n

                for tt in range(8 if STAGE >= 3 else 0):
                    bank = banks[tt % 2]

                    @blk.tensor
                    def _(t, tt=tt, bank=bank):
                        t = TrackEng(t)
                        t.wait_ge(dW.h, gate_ready)
                        if tt == 0:
                            t.wait_ge(sV.h, ln2_ready)
                        take(t, bk(bank))
                        for kc in range(16):
                            t.matmul(bank[:, 0:NE],
                                     lnf[:, kc * P + tt * 128:
                                         kc * P + tt * 128 + 128],
                                     gateb[:, kc * NE:(kc + 1) * NE],
                                     start=(kc == 0), stop=(kc == 15))
                        sT.inc(t.last)
                    gmm = sT.n

                    @blk.scalar
                    def _(sc, bank=bank, gmm=gmm):
                        sc = TrackEng(sc)
                        sc.wait_ge(sT.h, gmm)
                        take(sc, ("gatevec",))
                        sc.activation(scs[:, :], bank[:, 0:NE], AF.Sigmoid)
                        sA.inc(sc.last)
                    sigd = sA.n
                    put(bk(bank), sA)

                    @blk.vector
                    def _(v, sigd=sigd):
                        v = TrackEng(v)
                        v.wait_ge(sA.h, sigd)
                        v.tensor_tensor(sbb[:, :], scs[:, :], gbb[:, :],
                                        Op.add)
                        kick(v, sV)
                        for gi in range(4):
                            v.max(out=g8[:, gi * 8:(gi + 1) * 8],
                                  in_=sbb[:, gi * 8:(gi + 1) * 8])
                        kick(v, sV)
                        g3 = g8[:, :].rearrange("p (a b) -> p a b", a=4, b=8)
                        v.tensor_tensor(gpad[:, 0:4], g3[:, :, 0],
                                        g3[:, :, 1], Op.add)
                        v.memset(gpad[:, 4:12], NEG)
                        kick(v, sV)
                        v.max(out=gm8[:, :], in_=gpad[:, :])
                        kick(v, sV)
                        v.tensor_copy(repl8[:, 0:2], gm8[:, 0:2])
                        v.memset(repl8[:, 2:8], NEG)
                        kick(v, sV)
                        v.match_replace(out=gzap[:, :],
                                        in_to_replace=repl8[:, :],
                                        in_values=gpad[:, :], imm_value=NEG)
                        kick(v, sV)
                        v.tensor_tensor(gmask[:, :], gpad[:, 0:4],
                                        gzap[:, 0:4], Op.not_equal)
                        kick(v, sV)
                        v.tensor_scalar(gmask[:, :], gmask[:, :], 1.0, 1e30,
                                        Op.subtract, Op.mult)
                        kick(v, sV)
                        m3 = msb[:, :].rearrange("p (a b) -> p a b", a=4, b=8)
                        s3 = sbb[:, :].rearrange("p (a b) -> p a b", a=4, b=8)
                        v.tensor_tensor(m3, s3,
                                        gmask[:, :].unsqueeze(2)
                                        .to_broadcast([128, 4, 8]), Op.add)
                        kick(v, sV)
                        v.max(out=t8v[:, :], in_=msb[:, :])
                        kick(v, sV)
                        v.match_replace(out=mzap[:, :],
                                        in_to_replace=t8v[:, :],
                                        in_values=msb[:, :], imm_value=NEG2)
                        kick(v, sV)
                        v.tensor_tensor(selm[:, :], msb[:, :], mzap[:, :],
                                        Op.not_equal)
                        kick(v, sV)
                        v.tensor_tensor(selm[:, :], scs[:, :], selm[:, :],
                                        Op.mult)
                        kick(v, sV)
                        v.tensor_reduce(colA[:, :], selm[:, :], AX.X, Op.add)
                        kick(v, sV)
                        v.reciprocal(colA[:, :], colA[:, :])
                        kick(v, sV)
                        v.tensor_scalar(cwb16[:, :], selm[:, :], colA[:, :],
                                        2.5, Op.mult, Op.mult)
                        sV.inc(v.last)
                    cwd = sV.n
                    put(("gatevec",), sV)

                    @blk.tensor
                    def _(t, cwd=cwd):
                        t = TrackEng(t)
                        take(t, ("psbf", 0))
                        t.wait_ge(sV.h, cwd)
                        t.transpose(psbf[0][0:32, 0:128], cwb16[:, :],
                                    ident[:, :])
                        sT.inc(t.last)
                    cwt = sT.n

                    @blk.vector
                    def _(v, tt=tt, cwt=cwt):
                        v = TrackEng(v)
                        v.wait_ge(sT.h, cwt)
                        v.tensor_copy(cwfm[:, tt * 128:(tt + 1) * 128],
                                      psbf[0][0:32, 0:128])
                        sV.inc(v.last)
                    put(("psbf", 0), sV)
                cwfm_ready = sV.n

                for lc2 in range(4 if STAGE >= 3 else 0):
                    buf = [wpAf, wpBf][lc2 % 2]

                    @blk.gpsimd
                    def _(g, buf=buf, lc2=lc2):
                        g = TrackEng(g)
                        take(g, ("wpf", lc2 % 2))
                        dW.inc(load_pair(g, buf, fc1t, lc2 * 256), 16)
                    fcw = dW.n

                    for sub in range(2):
                        lc = lc2 * 2 + sub
                        for tc in range(2):
                            bank = banks[(lc * 2 + tc) % 2]

                            @blk.tensor
                            def _(t, buf=buf, sub=sub, tc=tc, bank=bank,
                                  fcw=fcw):
                                t = TrackEng(t)
                                t.wait_ge(dW.h, fcw)
                                take(t, bk(bank))
                                for kc in range(16):
                                    t.matmul(
                                        bank[:, :],
                                        buf[:, kc * 256 + sub * 128:
                                            kc * 256 + sub * 128 + 128],
                                        lnf[:, kc * P + tc * 512:
                                            kc * P + tc * 512 + 512],
                                        start=(kc == 0), stop=(kc == 15))
                                sT.inc(t.last)
                            mmd = sT.n
                            if sub == 1 and tc == 1:
                                put(("wpf", lc2 % 2), sT)

                            @blk.vector
                            def _(v, lc=lc, tc=tc, bank=bank, mmd=mmd):
                                v = TrackEng(v)
                                v.wait_ge(sT.h, mmd)
                                v.tensor_copy(
                                    xlat[:, lc * P + tc * 512:
                                         lc * P + tc * 512 + 512],
                                    bank[:, :])
                                sV.inc(v.last)
                            put(bk(bank), sV)
                xlat_ready = sV.n

                @blk.vector
                def _(v):
                    v = TrackEng(v)
                    v.memset(ysb[:, :], 0.0)
                    sV.inc(v.last)

        stkW = ExitStack()
        with stkW:
            w1b = [sbt(stkW, f"w1_{i}", [128, 8 * 512], BF16)
                   for i in range(2)]
            w2b = [sbt(stkW, f"w2_{i}", [128, 8 * 512], BF16)
                   for i in range(2)]
            he = sbt(stkW, "he", [128, 8 * 1024], BF16)
            gsb = sbt(stkW, "gsb", [128, 512], BF16)
            usb = sbt(stkW, "usb", [128, 512], BF16)
            cwsb = sbt(stkW, "cwsb", [128, 1024], BF16)

            def load_w1(g, buf, e, ih):
                ap = w1t[e, :, ih * 512:(ih + 1) * 512].rearrange(
                    "(lc p) i -> p lc i", p=128)
                return g.dma_start(buf[:, :], ap)

            def load_w2(g, buf, e, lh):
                ap = w2t[e, :, lh * 512:(lh + 1) * 512].rearrange(
                    "(ic p) l -> p ic l", p=128)
                return g.dma_start(buf[:, :], ap)

            with nc.Block() as blk:
                for e in range(NE if STAGE >= 4 else 0):
                    w1r = [0, 0]
                    for ih in range(2):
                        @blk.gpsimd
                        def _(g, ih=ih, e=e):
                            g = TrackEng(g)
                            take(g, ("w1", ih))
                            dW.inc(load_w1(g, w1b[ih], e, ih), 16)
                        w1r[ih] = dW.n

                    cb = banks[4]
                    for tc in range(2):
                        @blk.tensor
                        def _(t, e=e, tc=tc, first=(e == 0 and tc == 0)):
                            t = TrackEng(t)
                            if first:
                                t.wait_ge(sV.h, cwfm_ready)
                                t.wait_ge(dW.h, gate_ready)
                            take(t, bk(cb))
                            t.matmul(cb[:, :],
                                     sel32[:, e * 128:(e + 1) * 128],
                                     cwfm[:, tc * 512:(tc + 1) * 512],
                                     start=True, stop=True)
                            sT.inc(t.last)
                        cbd = sT.n

                        @blk.vector
                        def _(v, tc=tc, cbd=cbd):
                            v = TrackEng(v)
                            v.wait_ge(sT.h, cbd)
                            v.tensor_copy(cwsb[:, tc * 512:(tc + 1) * 512],
                                          banks[4][:, :])
                            sV.inc(v.last)
                        put(bk(cb), sV)
                    cwr = sV.n

                    for ih in range(2):
                        for icl in range(4):
                            ic = ih * 4 + icl
                            for tc in range(2):
                                bank = banks[(2 * ic + tc) % 2]

                                @blk.tensor
                                def _(t, ih=ih, icl=icl, tc=tc, bank=bank,
                                      first=(e == 0 and ih == 0
                                             and icl == 0 and tc == 0)):
                                    t = TrackEng(t)
                                    t.wait_ge(dW.h, w1r[ih])
                                    if first:
                                        t.wait_ge(sV.h, xlat_ready)
                                    take(t, bk(bank))
                                    for lc in range(8):
                                        t.matmul(
                                            bank[:, :],
                                            w1b[ih][:, lc * 512 + icl * 128:
                                                    lc * 512 + icl * 128
                                                    + 128],
                                            xlat[:, lc * P + tc * 512:
                                                 lc * P + tc * 512 + 512],
                                            start=(lc == 0), stop=(lc == 7))
                                    sT.inc(t.last)
                                hmm_ = sT.n
                                if icl == 3 and tc == 1:
                                    put(("w1", ih), sT)

                                @blk.vector
                                def _(v, ic=ic, tc=tc, bank=bank, hmm_=hmm_,
                                      cwr=cwr,
                                      firsth=(ih == 0 and icl == 0
                                              and tc == 0)):
                                    v = TrackEng(v)
                                    v.wait_ge(sT.h, hmm_)
                                    if firsth:
                                        v.wait_ge(sV.h, cwr)
                                        take(v, ("he",))
                                    v.tensor_scalar(gsb[:, :], bank[:, :],
                                                    0.0, None, Op.max)
                                    v.tensor_tensor(
                                        usb[:, :], gsb[:, :],
                                        cwsb[:, tc * 512:(tc + 1) * 512],
                                        Op.mult)
                                    v.tensor_tensor(
                                        he[:, ic * 1024 + tc * 512:
                                           ic * 1024 + tc * 512 + 512],
                                        usb[:, :], gsb[:, :], Op.mult)
                                    sV.inc(v.last)
                                put(bk(bank), sV)
                    hdone = sV.n

                    w2r = [0, 0]
                    for lh in range(2):
                        @blk.gpsimd
                        def _(g, lh=lh, e=e):
                            g = TrackEng(g)
                            take(g, ("w2", lh))
                            dW.inc(load_w2(g, w2b[lh], e, lh), 16)
                        w2r[lh] = dW.n

                    for tc in range(2):
                        for lc in range(8):
                            lh, lcl = lc // 4, lc % 4
                            bank = banks[2 + (lc + tc) % 2]

                            @blk.tensor
                            def _(t, tc=tc, lc=lc, lh=lh, lcl=lcl, bank=bank,
                                  first=(tc == 0 and lc == 0)):
                                t = TrackEng(t)
                                t.wait_ge(dW.h, w2r[lh])
                                if first:
                                    t.wait_ge(sV.h, hdone)
                                take(t, bk(bank))
                                for ic in range(8):
                                    t.matmul(
                                        bank[:, :],
                                        w2b[lh][:, ic * 512 + lcl * 128:
                                                ic * 512 + lcl * 128 + 128],
                                        he[:, ic * 1024 + tc * 512:
                                           ic * 1024 + tc * 512 + 512],
                                        start=(ic == 0), stop=(ic == 7))
                                sT.inc(t.last)
                            ymm = sT.n
                            if tc == 1 and lc == 3:
                                put(("w2", 0), sT)
                            if tc == 1 and lc == 7:
                                put(("w2", 1), sT)
                                put(("he",), sT)

                            @blk.vector
                            def _(v, tc=tc, lc=lc, bank=bank, ymm=ymm):
                                v = TrackEng(v)
                                v.wait_ge(sT.h, ymm)
                                yap = ysb[:, lc * P + tc * 512:
                                          lc * P + tc * 512 + 512]
                                v.tensor_tensor(yap, yap, bank[:, :], Op.add)
                                sV.inc(v.last)
                            put(bk(bank), sV)
        y_done = sV.n

        with nc.Block() as blk:
            @blk.vector
            def _(v):
                v = TrackEng(v)
                for i in range(8):
                    v.tensor_copy(ylb[:, i * P:(i + 1) * P],
                                  ysb[:, i * P:(i + 1) * P])
                sV.inc(v.last)
        ylb_ready = sV.n
        stkX.close()
        stkE.close()

        stkS = ExitStack()
        with stkS:
            shfm = sbt(stkS, "shfm", [128, 32 * 512], BF16)
            dwn3 = [sbt(stkS, f"dwn{i}", [128, 16 * 256], BF16)
                    for i in range(3)]
            fc2b = sbt(stkS, "fc2b", [128, 8 * 256], BF16)
            otmh = sbt(stkS, "otmh", [128, 4 * D])

            if True:
                for half in range(2 if STAGE >= 5 else 0):
                    stkH = ExitStack()
                    lnh2 = sbt(stkH, f"lnh2_{half}", [128, NKC * 512], BF16)
                    upwb = [sbt(stkH, f"upw{half}_{i}", [128, NKC * 128],
                                BF16) for i in range(2)]
                    gsh = sbt(stkH, f"gsh{half}", [128, 512], BF16)
                    blk_cm = nc.Block()
                    blk = blk_cm.__enter__()
                    for tl in range(4):
                        tt = half * 4 + tl
                        xap = xres[:, tt * D:(tt + 1) * D]
                        ln_stats(blk, xap)

                        @blk.vector
                        def _(v, xap=xap):
                            v = TrackEng(v)
                            take(v, ("lntmp",))
                            v.tensor_scalar(lntmp[:, :], xap, mv2[:, 0:1],
                                            colA[:, :], Op.subtract, Op.mult)
                            sV.inc(v.last)
                        lnv = sV.n

                        def lnh_dst(b, tl=tl):
                            base = lnh2[:, :].rearrange("p (kc w) -> p kc w",
                                                        kc=16, w=512)
                            return base[:, b * 8:(b + 1) * 8,
                                        tl * 128:(tl + 1) * 128]
                        transpose_fm(blk, lntmp, lnh_dst,
                                     pe_waits=[(sV, lnv)])
                        put(("lntmp",), sT)
                    lnh2_ready = sV.n

                    for icg in range(32):
                        upw = upwb[icg % 2]

                        @blk.gpsimd
                        def _(g, icg=icg, upw=upw):
                            g = TrackEng(g)
                            take(g, ("upw", icg % 2))
                            dW.inc(load_pair(g, upw, upt, icg * 128,
                                             width=128), 16)
                        upr = dW.n

                        if True:
                            bank = banks[icg % 2]

                            @blk.tensor
                            def _(t, bank=bank, upr=upr, upw=upw,
                                  first=(icg == 0)):
                                t = TrackEng(t)
                                t.wait_ge(dW.h, upr)
                                if first:
                                    t.wait_ge(sV.h, lnh2_ready)
                                take(t, bk(bank))
                                for kc in range(16):
                                    t.matmul(
                                        bank[:, :],
                                        upw[:, kc * 128:(kc + 1) * 128],
                                        lnh2[:, kc * 512:(kc + 1) * 512],
                                        start=(kc == 0), stop=(kc == 15))
                                sT.inc(t.last)
                            umm = sT.n
                            put(("upw", icg % 2), sT)

                            @blk.vector
                            def _(v, icg=icg, bank=bank, umm=umm):
                                v = TrackEng(v)
                                v.wait_ge(sT.h, umm)
                                take(v, ("gsh",))
                                v.tensor_scalar(gsh[:, :], bank[:, :],
                                                0.0, None, Op.max)
                                v.tensor_tensor(
                                    shfm[:, icg * 512:(icg + 1) * 512],
                                    gsh[:, :], gsh[:, :], Op.mult)
                                sV.inc(v.last)
                            put(bk(bank), sV)
                            put(("gsh",), sV)
                    sh_ready = sV.n

                    blk_cm.__exit__(None, None, None)
                    stkH.close()
                    blk_cm = nc.Block()
                    blk = blk_cm.__enter__()

                    for oc in range(4):
                        for s2b in range(2):
                            col0 = oc * 512 + s2b * 256
                            step = oc * 2 + s2b
                            dhalf = []
                            for h2 in range(2):
                                bi = (half * 16 + step * 2 + h2) % 3
                                buf = dwn3[bi]

                                @blk.gpsimd
                                def _(g, col0=col0, h2=h2, bi=bi, buf=buf):
                                    g = TrackEng(g)
                                    take(g, ("dwn3", bi))
                                    ap = downt[h2 * 2048:(h2 + 1) * 2048,
                                               col0:col0 + 256].rearrange(
                                        "(ic p) o -> p ic o", p=128)
                                    dW.inc(g.dma_start(buf[:, :], ap), 16)
                                dhalf.append((bi, dW.n))

                            @blk.gpsimd
                            def _(g, col0=col0):
                                g = TrackEng(g)
                                take(g, ("fc2b",))
                                apf = fc2t[:, col0:col0 + 256].rearrange(
                                    "(lc p) o -> p lc o", p=128)
                                dW.inc(g.dma_start(fc2b[:, :], apf), 16)
                            fcr = dW.n

                            for tl in range(4):
                                tt = half * 4 + tl
                                bank = banks[2 + tl % 2]

                                @blk.tensor
                                def _(t, tl=tl, tt=tt, bank=bank,
                                      dhalf=tuple(dhalf), fcr=fcr,
                                      first=(oc == 0 and s2b == 0
                                             and tl == 0)):
                                    t = TrackEng(t)
                                    if first:
                                        t.wait_ge(sV.h, sh_ready)
                                        t.wait_ge(sV.h, ylb_ready)
                                    for h2 in range(2):
                                        bi, dwr = dhalf[h2]
                                        t.wait_ge(dW.h, dwr)
                                        if h2 == 0:
                                            take(t, bk(bank))
                                        buf = dwn3[bi]
                                        for i2 in range(16):
                                            ic = h2 * 16 + i2
                                            t.matmul(
                                                bank[:, 0:256],
                                                shfm[:, ic * 512 + tl * 128:
                                                     ic * 512 + tl * 128
                                                     + 128],
                                                buf[:, i2 * 256:
                                                    (i2 + 1) * 256],
                                                start=(ic == 0), stop=False)
                                    t.wait_ge(dW.h, fcr)
                                    for lc in range(8):
                                        t.matmul(
                                            bank[:, 0:256],
                                            ylb[:, lc * P + tt * 128:
                                                lc * P + tt * 128 + 128],
                                            fc2b[:, lc * 256:(lc + 1) * 256],
                                            start=False, stop=(lc == 7))
                                    sT.inc(t.last)
                                dmm = sT.n
                                if tl == 3:
                                    put(("dwn3", dhalf[0][0]), sT)
                                    put(("dwn3", dhalf[1][0]), sT)
                                    put(("fc2b",), sT)

                                @blk.vector
                                def _(v, tl=tl, oc=oc, s2b=s2b, bank=bank,
                                      dmm=dmm):
                                    v = TrackEng(v)
                                    v.wait_ge(sT.h, dmm)
                                    v.tensor_copy(
                                        otmh[:, tl * D + oc * 512
                                             + s2b * 256:
                                             tl * D + oc * 512
                                             + s2b * 256 + 256],
                                        bank[:, 0:256])
                                    sV.inc(v.last)
                                put(bk(bank), sV)
                    ffn_done = sV.n

                    for tl in range(4):
                        tt = half * 4 + tl
                        oap = otmh[:, tl * D:(tl + 1) * D]
                        ln_stats(blk, oap,
                                 [(sV, ffn_done)] if tl == 0 else [])

                        @blk.vector
                        def _(v, oap=oap, tt=tt):
                            v = TrackEng(v)
                            take(v, ("lntmp",))
                            v.tensor_scalar(lntmp[:, :], oap, mv2[:, 0:1],
                                            colA[:, :], Op.subtract, Op.mult)
                            v.tensor_tensor(xres[:, tt * D:(tt + 1) * D],
                                            xres[:, tt * D:(tt + 1) * D],
                                            lntmp[:, :], Op.add)
                            sV.inc(v.last)
                        put(("lntmp",), sV)
                        xdone = sV.n

                        @blk.sync
                        def _(s, tt=tt, xdone=xdone):
                            s = TrackEng(s)
                            s.wait_ge(sV.h, xdone)
                            dO.inc(s.dma_start(
                                out_hbm[tt * 128:(tt + 1) * 128, :],
                                xres[:, tt * D:(tt + 1) * D]), 16)

                    blk_cm.__exit__(None, None, None)

            if STAGE >= 5:
                with nc.Block() as blk:
                    @blk.sync
                    def _(s):
                        s = TrackEng(s)
                        s.wait_ge(dO.h, dO.n)

        stkY.close()
    return nc


_CACHE = {}


def kernel(**inputs):
    import ml_dtypes

    x = np.asarray(inputs["L0_attn_hidden_states"], dtype=np.float32)
    B = x.shape[0]
    assert x.shape == (B, S2, D) and B == 4

    f = np.ascontiguousarray
    wq = np.asarray(inputs["L0_attn_Wq"], np.float32)
    wk = np.asarray(inputs["L0_attn_Wk"], np.float32)
    wv = np.asarray(inputs["L0_attn_Wv"], np.float32)
    wo = np.asarray(inputs["L0_attn_Wo"], np.float32)
    bq = np.asarray(inputs["L0_attn_bq"], np.float32)
    bkb = np.asarray(inputs["L0_attn_bk"], np.float32)
    bv = np.asarray(inputs["L0_attn_bv"], np.float32)
    bo = np.asarray(inputs["L0_attn_bo"], np.float32)
    R = np.asarray(inputs["L0_attn_rand_matrix"], np.float32)
    gate_w = np.asarray(inputs["L0_ffn_gate_w"], np.float32)
    gate_b = np.asarray(inputs["L0_ffn_gate_bias"], np.float32)
    fc1 = np.asarray(inputs["L0_ffn_fc1_w"], np.float32)
    fc2 = np.asarray(inputs["L0_ffn_fc2_w"], np.float32)
    w1 = np.asarray(inputs["L0_ffn_w1"], np.float32)
    w2 = np.asarray(inputs["L0_ffn_w2"], np.float32)
    up = np.asarray(inputs["L0_ffn_shared_up_w"], np.float32)
    down = np.asarray(inputs["L0_ffn_shared_down_w"], np.float32)

    wqT = f(wq.T)
    shared = {
        "wkt": f(wk.T), "wvt": f(wv.T), "wot": f(wo.T),
        "rt": f(R.T),
        "gatet": f(gate_w.T), "fc1t": f(fc1.T), "fc2t": f(fc2.T),
        "w1t": f(w1.transpose(0, 2, 1)), "w2t": f(w2.transpose(0, 2, 1)),
        "upt": f(up.T), "downt": f(down.T),
        "bkf": f(bkb.reshape(HKV, 128).T),
        "bvb": f(np.broadcast_to(bv, (128, HKV * 128))),
        "bob": f(np.broadcast_to(bo, (128, D))),
        "gbb": f(np.broadcast_to(gate_b, (128, NE))),
    }
    bqf_full = f(bq.reshape(16, 128).T)

    ident = np.eye(128, dtype=ml_dtypes.bfloat16)
    sel = np.zeros((32, NE * 128), dtype=ml_dtypes.bfloat16)
    for e in range(NE):
        sel[e, e * 128:(e + 1) * 128] = 1
    ones1 = np.ones((128, 1), dtype=ml_dtypes.bfloat16)
    hd = 128
    t32 = np.arange(0, hd, 2).astype(np.float32) / np.float32(hd)
    inv_freq = (np.float32(1.0) / (np.float32(10000.0) ** t32)) \
        .astype(np.float32)
    pos = np.arange(S2, dtype=np.float32)
    freqs = (pos[:, None] * inv_freq[None, :]).astype(np.float32)
    emb = np.concatenate([freqs, freqs], axis=1)
    cos_t = np.cos(emb).astype(np.float32)
    sin_t = np.sin(emb).astype(np.float32)
    i_idx = np.arange(S2)
    tperm = 16 * (i_idx % 128) + (i_idx // 128)
    cosf = f(cos_t[tperm, :].T)
    sinf = f(sin_t[tperm, :].T)
    consts = {"ident": ident, "sel32": sel, "ones1": ones1,
              "cosf": cosf, "sinf": sinf}

    in_maps = []
    for core in range(8):
        b, hh = core // 2, core % 2
        m = dict(shared)
        m.update(consts)
        m["xfull"] = f(x[b])
        m["xr"] = f(x[b, hh * P:(hh + 1) * P, :])
        m["wqt"] = f(wqT[:, hh * 1024:(hh + 1) * 1024])
        m["bqf"] = f(bqf_full[:, hh * 8:(hh + 1) * 8])
        in_maps.append(m)

    if "nc" not in _CACHE:
        _CACHE["nc"] = build()
    nc = _CACHE["nc"]
    res = run_bass_kernel_spmd(nc, in_maps, core_ids=list(range(8)))
    out = np.empty((B, S2, D), dtype=np.float32)
    for core in range(8):
        b, hh = core // 2, core % 2
        out[b, hh * P:(hh + 1) * P, :] = res.results[core]["out"]
    return out


# version 09
# speedup vs baseline: 1.1709x; 1.0459x over previous
import os
import numpy as np
import concourse.bass as bass
import concourse.mybir as mybir
from bass_rust import BassInstruction as _BassInstruction
from concourse.bass_utils import run_bass_kernel_spmd
from concourse.alu_op_type import AluOpType as Op
from contextlib import ExitStack

F32 = mybir.dt.float32
BF16 = mybir.dt.bfloat16
AF = mybir.ActivationFunctionType
AX = mybir.AxisListType

PI = float(np.pi)
INV2PI = float(1.0 / (2 * np.pi))
TWOPI = float(2 * np.pi)
MAGIC = float(1.5 * 2 ** 23)
NEG = -1e30
NEG2 = -2e30

P = 1024
S2 = 2048
D = 2048
NKC = 16
HL = 8
HKV = 8
LAT = 1024
NE = 32
SH = 4096
EPS_LN = 1e-5
SFRAC = float(2.0 / 128.0)


class Sem:
    __slots__ = ("h", "n")

    def __init__(self, h):
        self.h = h
        self.n = 0

    def inc(self, inst, k=1):
        inst.then_inc(self.h, k)
        self.n += k
        return self.n


_NOTRACK = {"wait_ge", "nop", "sem_inc", "sem_clear", "print", "breakpoint",
            "maybe_wait_ge", "drain"}


class TrackEng:

    def __init__(self, e):
        object.__setattr__(self, "_e", e)
        object.__setattr__(self, "last", None)

    def __getattr__(self, n):
        a = getattr(object.__getattribute__(self, "_e"), n)
        if callable(a) and n not in _NOTRACK:
            def w(*args, **kw):
                r = a(*args, **kw)
                if isinstance(r, _BassInstruction):
                    object.__setattr__(self, "last", r)
                return r
            return w
        return a


STAGE = int(os.environ.get("MK_STAGE", "5"))
SUB = int(os.environ.get("MK_SUB", "9"))
DEBUG = int(os.environ.get("MK_DEBUG", "0"))


def build():
    nc = bass.Bass("TRN2", target_bir_lowering=False)

    def din(name, shape, dt=F32):
        return nc.dram_tensor(name, shape, dt, kind="ExternalInput").ap()

    xfull = din("xfull", [S2, D])
    xr_in = din("xr", [P, D])
    out_hbm = nc.dram_tensor("out", [P, D], F32, kind="ExternalOutput").ap()
    dbg = {}
    if DEBUG:
        for nm, w in [("d_rinv", 64), ("d_wrp", 1024), ("d_abs", 1024),
                      ("d_qp", 1024), ("d_kp", 1024), ("d_sred", 64),
                      ("d_acc", 1024), ("d_qf", 4096), ("d_kf", 4096),
                      ("d_vtm", 1024), ("d_lnq", 2048), ("d_n2", 64)]:
            dbg[nm] = nc.dram_tensor(nm, [128, w], F32,
                                     kind="ExternalOutput").ap()

    wqt = din("wqt", [D, HL * 128])
    wkt = din("wkt", [D, HKV * 128])
    wvt = din("wvt", [D, HKV * 128])
    wot = din("wot", [D, D])
    rt_in = din("rt", [128, 64])
    gatet_in = din("gatet", [D, NE]); fc1t = din("fc1t", [D, LAT])
    fc2t = din("fc2t", [LAT, D])
    w1t = din("w1t", [NE, LAT, LAT]); w2t = din("w2t", [NE, LAT, LAT])
    upt = din("upt", [D, SH]); downt = din("downt", [SH, D])
    bqf_in = din("bqf", [128, HL]); bkf_in = din("bkf", [128, HKV])
    bvb_in = din("bvb", [128, HKV * 128]); bob_in = din("bob", [128, D])
    gbb_in = din("gbb", [128, NE])
    cosf_in = din("cosf", [128, S2]); sinf_in = din("sinf", [128, S2])
    ident_in = din("ident", [128, 128], BF16)
    sel32_in = din("sel32", [32, NE * 128], BF16)
    ones1_in = din("ones1", [128, 1], BF16)

    top = ExitStack()

    def sbt(stack, name, shape, dt=F32):
        return stack.enter_context(nc.sbuf_tensor(name, shape, dt))

    with top:
        xres = sbt(top, "xres", [128, 8 * D])
        ident = sbt(top, "ident_s", [128, 128], BF16)
        ones1 = sbt(top, "ones1_s", [128, 1], BF16)
        stat = sbt(top, "stat", [128, 24])
        mv2 = sbt(top, "mv2", [128, 2])
        colA = sbt(top, "colA", [128, 1])
        lntmp = sbt(top, "lntmp", [128, D], BF16)
        halfpi = sbt(top, "halfpi", [128, 1])
        nhalf = sbt(top, "nhalf", [128, 64])
        bqf = sbt(top, "bqf_s", [128, HL])
        bkf = sbt(top, "bkf_s", [128, HKV])
        gbb = sbt(top, "gbb_s", [128, NE])

        banks = [top.enter_context(nc.psum_tensor(f"bk{i}", [128, 512], F32))
                 for i in range(6)]
        psbf = [top.enter_context(nc.psum_tensor(f"pb{i}", [128, 1024], BF16))
                for i in range(2)]

        dW = Sem(top.enter_context(nc.semaphore("dW")))
        dX = Sem(top.enter_context(nc.semaphore("dX")))
        dO = Sem(top.enter_context(nc.semaphore("dO")))
        sT = Sem(top.enter_context(nc.semaphore("sT")))
        sV = Sem(top.enter_context(nc.semaphore("sV")))
        sA = Sem(top.enter_context(nc.semaphore("sA")))
        sG = Sem(top.enter_context(nc.semaphore("sG")))

        guard = {}

        def take(eng, *keys):
            for key in keys:
                g = guard.pop(key, None)
                if g is not None:
                    eng.wait_ge(g[0].h, g[1])

        def put(key, sem):
            guard[key] = (sem, sem.n)

        def bk(b):
            return ("bank", id(b))

        def kick(eng, sem):
            sem.inc(eng.last)
            eng.wait_ge(sem.h, sem.n)

        with nc.Block() as blk:
            @blk.sync
            def _(s):
                s = TrackEng(s)
                for tt in range(8):
                    dX.inc(s.dma_start(xres[:, tt * D:(tt + 1) * D],
                                       xr_in[tt * 128:(tt + 1) * 128, :]), 16)
                dX.inc(s.dma_start(bqf[:, :], bqf_in[:, :]), 16)
                dX.inc(s.dma_start(bkf[:, :], bkf_in[:, :]), 16)
                dX.inc(s.dma_start(gbb[:, :], gbb_in[:, :]), 16)
                dX.inc(s.dma_start(ident[:, :], ident_in[:, :]), 16)
                dX.inc(s.dma_start(ones1[:, :], ones1_in[:, :]), 16)

            @blk.vector
            def _(v):
                v = TrackEng(v)
                v.memset(halfpi[:, :], PI / 2)
                v.memset(nhalf[:, :], -0.5)
                sV.inc(v.last)
        x_loaded = dX.n
        consts_v = sV.n

        def load_pair(g, buf, src, col0, width=256):
            ap = src[:, col0:col0 + width].rearrange("(kc p) o -> p kc o",
                                                     p=128)
            return g.dma_start(buf[:, 0:NKC * width], ap)

        def ln_stats(blk, src_ap, wait_sems=()):
            @blk.vector
            def _(v):
                v = TrackEng(v)
                for sem, val in wait_sems:
                    v.wait_ge(sem.h, val)
                for c in range(4):
                    v.bn_stats(stat[:, c * 6:(c + 1) * 6],
                               src_ap[:, c * 512:(c + 1) * 512])
                kick(v, sV)
                v.bn_aggr(mv2[:, :], stat[:, :])
                kick(v, sV)
                v.tensor_scalar(colA[:, :], mv2[:, 1:2], EPS_LN, None, Op.add)
                sV.inc(v.last)
            vst = sV.n

            @blk.gpsimd
            def _(g):
                g = TrackEng(g)
                g.wait_ge(sV.h, vst)
                g.wait_ge(sV.h, consts_v)
                g.tensor_tensor(colA[:, :], colA[:, :], nhalf[:, 0:1], Op.pow)
                sG.inc(g.last)
            gst = sG.n

            @blk.vector
            def _(v):
                v = TrackEng(v)
                v.wait_ge(sG.h, gst)

        def transpose_fm(blk, src, dst_ap_fn, pe_waits=()):
            @blk.tensor
            def _(t):
                t = TrackEng(t)
                for sem, val in pe_waits:
                    t.wait_ge(sem.h, val)
                take(t, ("psbf", 0), ("psbf", 1))
                for kc in range(16):
                    b, j = kc // 8, kc % 8
                    t.transpose(psbf[b][:, j * 128:(j + 1) * 128],
                                src[:, kc * 128:(kc + 1) * 128], ident[:, :])
                sT.inc(t.last)
            tdone = sT.n

            @blk.vector
            def _(v):
                v = TrackEng(v)
                v.wait_ge(sT.h, tdone)
                for b in range(2):
                    v.tensor_copy(dst_ap_fn(b),
                                  psbf[b][:, :].rearrange("p (j c) -> p j c",
                                                          j=8, c=128))
                sV.inc(v.last)
            put(("psbf", 0), sV)
            put(("psbf", 1), sV)
            return sV.n

        stkAb = ExitStack()
        with stkAb:
            wpA = sbt(stkAb, "wpA", [128, NKC * 256], BF16)
            wpB = sbt(stkAb, "wpB", [128, NKC * 256], BF16)
            afm = sbt(stkAb, "afm", [128, HL * 2048], BF16)
            stkA = ExitStack()
            stkA.__enter__()
            cosf = sbt(stkA, "cosf_sb", [128, 512], BF16)
            sinf = sbt(stkA, "sinf_sb", [128, 512], BF16)
            bvb = sbt(stkA, "bvb_sb", [128, HKV * 128], BF16)
            rt = sbt(stkA, "rt_sb", [128, 64], BF16)
            evtmp = sbt(stkA, "evt", [128, 512], BF16)
            rotb = sbt(stkA, "rot", [128, 512], BF16)
            xtile = sbt(stkA, "xtile", [128, D])
            lnq2 = [sbt(stkA, f"lnq{i}", [128, NKC * 512], BF16)
                    for i in range(2)]
            qf = sbt(stkA, "qf", [128, HL * 512], BF16)
            kf = sbt(stkA, "kf", [128, HKV * 512], BF16)
            vtm = sbt(stkA, "vtm", [128, 4 * HKV * 128], BF16)
            sqb = sbt(stkA, "sqb", [128, 512], BF16)
            rinv = sbt(stkA, "rinv", [128, 64])
            pbuf = sbt(stkA, "pbuf", [128, 1024])
            wrp = sbt(stkA, "wrp", [128, 1024])
            qpk = sbt(stkA, "qpk", [128, 16 * 128], BF16)
            sred = sbt(stkA, "sred", [128, HL * HKV])
            term2 = [sbt(stkA, f"term{i}", [128, HL * 128], BF16)
                     for i in range(2)]
            term = term2[0]
            acc = sbt(stkA, "acc", [128, HL * 128], BF16)

            with nc.Block() as blk:
                @blk.gpsimd
                def _(g):
                    g = TrackEng(g)
                    dW.inc(g.dma_start(bvb[:, :], bvb_in[:, :]), 16)
                    dW.inc(g.dma_start(rt[:, :], rt_in[:, :]), 16)
                consts_W = dW.n

                xperm = xfull.rearrange("(p j) d -> j p d", j=16)

                for c in range(4 if STAGE >= 2 else 1):
                    @blk.gpsimd
                    def _(g, c=c):
                        g = TrackEng(g)
                        take(g, ("trig",))
                        dW.inc(g.dma_start(cosf[:, :],
                                           cosf_in[:, c * 512:(c + 1) * 512]),
                               16)
                        dW.inc(g.dma_start(sinf[:, :],
                                           sinf_in[:, c * 512:(c + 1) * 512]),
                               16)
                    trig_W = dW.n

                    def emit_ln_chunk(cc):
                        lnqd = lnq2[cc % 2]
                        for tl in range(4):
                            j = 4 * cc + tl

                            @blk.sync
                            def _(s, j=j):
                                s = TrackEng(s)
                                take(s, ("xtile",))
                                dX.inc(s.dma_start(xtile[:, :],
                                                   xperm[j, :, :]), 16)
                            xtl = dX.n

                            ln_stats(blk, xtile[:, :], [(dX, xtl)])

                            @blk.vector
                            def _(v):
                                v = TrackEng(v)
                                take(v, ("lntmp",))
                                v.tensor_scalar(lntmp[:, :], xtile[:, :],
                                                mv2[:, 0:1], colA[:, :],
                                                Op.subtract, Op.mult)
                                sV.inc(v.last)
                            lnv = sV.n
                            put(("xtile",), sV)

                            def lnq_dst(b, tl=tl, lnqd=lnqd):
                                base = lnqd[:, :].rearrange(
                                    "p (kc w) -> p kc w", kc=16, w=512)
                                return base[:, b * 8:(b + 1) * 8,
                                            tl * 128:(tl + 1) * 128]
                            transpose_fm(blk, lntmp, lnq_dst,
                                         pe_waits=[(sV, lnv)])
                            put(("lntmp",), sT)
                        return sV.n

                    if c == 0:
                        lnq_ready = emit_ln_chunk(0)
                    else:
                        lnq_ready = lnq_ready_next
                    lnq = lnq2[c % 2]

                    plans = ([("q", wqt, i) for i in range(4)]
                             + [("k", wkt, i) for i in range(4)])
                    if SUB < 2:
                        plans = []
                    for pi, (kind, src, i2) in enumerate(plans):
                        buf = [wpA, wpB][pi % 2]

                        @blk.gpsimd
                        def _(g, buf=buf, src=src, i2=i2, pi=pi):
                            g = TrackEng(g)
                            take(g, ("wp", pi % 2))
                            dW.inc(load_pair(g, buf, src, i2 * 256), 16)
                        wready = dW.n

                        for sub in range(2):
                            h = i2 * 2 + sub
                            dst, bias = (qf, bqf) if kind == "q" else (kf, bkf)
                            bank = banks[(pi * 2 + sub) % 2]

                            @blk.tensor
                            def _(t, buf=buf, sub=sub, bank=bank,
                                  wready=wready,
                                  first=(pi == 0 and sub == 0)):
                                t = TrackEng(t)
                                t.wait_ge(dW.h, wready)
                                if first:
                                    t.wait_ge(sV.h, lnq_ready)
                                take(t, bk(bank))
                                for kc in range(16):
                                    t.matmul(bank[:, :],
                                             buf[:, kc * 256 + sub * 128:
                                                 kc * 256 + sub * 128 + 128],
                                             lnq[:, kc * 512:(kc + 1) * 512],
                                             start=(kc == 0), stop=(kc == 15))
                                sT.inc(t.last)
                            mmdone = sT.n
                            if sub == 1:
                                put(("wp", pi % 2), sT)

                            @blk.vector
                            def _(v, bank=bank, h=h, dst=dst, bias=bias,
                                  mmdone=mmdone,
                                  firstc=(pi == 0 and sub == 0 and c == 0)):
                                v = TrackEng(v)
                                v.wait_ge(sT.h, mmdone)
                                if firstc:
                                    v.wait_ge(dW.h, consts_W)
                                v.wait_ge(dW.h, trig_W)
                                v.tensor_scalar(evtmp[:, :], bank[:, :],
                                                bias[:, h:h + 1], None, Op.add)
                                v.tensor_scalar(rotb[0:64, :],
                                                evtmp[64:128, :], -1.0, None,
                                                Op.mult)
                                v.tensor_copy(rotb[64:128, :], evtmp[0:64, :])
                                v.tensor_tensor(evtmp[:, :], evtmp[:, :],
                                                cosf[:, :], Op.mult)
                                v.tensor_tensor(rotb[:, :], rotb[:, :],
                                                sinf[:, :], Op.mult)
                                v.tensor_tensor(dst[:, h * 512:(h + 1) * 512],
                                                evtmp[:, :], rotb[:, :],
                                                Op.add)
                                sV.inc(v.last)
                            evdone = sV.n
                            put(bk(bank), sV)

                            @blk.gpsimd
                            def _(g, h=h, dst=dst, evdone=evdone):
                                g = TrackEng(g)
                                g.wait_ge(sV.h, evdone)
                                take(g, ("sqb",))
                                g.tensor_tensor(sqb[:, :],
                                                dst[:, h * 512:(h + 1) * 512],
                                                dst[:, h * 512:(h + 1) * 512],
                                                Op.mult)
                                sG.inc(g.last)
                            sqdone = sG.n

                            hidx = h if kind == "q" else HL + h
                            nb = banks[5]

                            @blk.tensor
                            def _(t, sqdone=sqdone, hidx=hidx, nb=nb):
                                t = TrackEng(t)
                                t.wait_ge(sG.h, sqdone)
                                take(t, bk(nb))
                                for tl in range(4):
                                    t.matmul(nb[:, tl * 16 + hidx:
                                                tl * 16 + hidx + 1],
                                             sqb[:, tl * 128:(tl + 1) * 128],
                                             ones1[:, :], start=True,
                                             stop=True)
                                sT.inc(t.last)
                            put(("sqb",), sT)
                    rope_done = sV.n
                    put(("trig",), sV)
                    n2done = sT.n

                    for i2 in range(4 if SUB >= 3 else 0):
                        buf = [wpA, wpB][i2 % 2]

                        @blk.gpsimd
                        def _(g, buf=buf, i2=i2):
                            g = TrackEng(g)
                            take(g, ("wp", i2 % 2))
                            dW.inc(load_pair(g, buf, wvt, i2 * 256), 16)
                        vwready = dW.n

                        for tl in range(4):
                            bank = banks[2 + (tl % 2)]

                            @blk.tensor
                            def _(t, buf=buf, i2=i2, tl=tl, bank=bank,
                                  vwready=vwready):
                                t = TrackEng(t)
                                t.wait_ge(dW.h, vwready)
                                take(t, bk(bank))
                                for kc in range(16):
                                    t.matmul(
                                        bank[:, 0:256],
                                        lnq[:, kc * 512 + tl * 128:
                                            kc * 512 + tl * 128 + 128],
                                        buf[:, kc * 256:(kc + 1) * 256],
                                        start=(kc == 0), stop=(kc == 15))
                                sT.inc(t.last)
                            mmdone = sT.n
                            if tl == 3:
                                put(("wp", i2 % 2), sT)

                            @blk.vector
                            def _(v, i2=i2, tl=tl, bank=bank, mmdone=mmdone):
                                v = TrackEng(v)
                                v.wait_ge(sT.h, mmdone)
                                v.tensor_tensor(
                                    vtm[:, tl * 1024 + i2 * 256:
                                        tl * 1024 + i2 * 256 + 256],
                                    bank[:, 0:256],
                                    bvb[:, i2 * 256:(i2 + 1) * 256], Op.add)
                                sV.inc(v.last)
                            put(bk(bank), sV)

                    if SUB < 4:
                        continue
                    @blk.vector
                    def _(v, n2done=n2done):
                        v = TrackEng(v)
                        v.wait_ge(sT.h, n2done)
                        take(v, ("rinv",))
                        v.tensor_copy(rinv[:, :], banks[5][:, 0:64])
                        sV.inc(v.last)
                    ncp = sV.n
                    put(bk(banks[5]), sV)

                    @blk.gpsimd
                    def _(g, ncp=ncp):
                        g = TrackEng(g)
                        g.wait_ge(sV.h, ncp)
                        g.wait_ge(sV.h, consts_v)
                        g.tensor_tensor(rinv[:, :], rinv[:, :],
                                        nhalf[:, 0:64], Op.pow)
                        sG.inc(g.last)
                    ngp = sG.n

                    @blk.vector
                    def _(v, ngp=ngp):
                        v = TrackEng(v)
                        v.wait_ge(sG.h, ngp)
                        v.tensor_scalar(rinv[:, :], rinv[:, :], 1e6, None,
                                        Op.min)
                        sV.inc(v.last)
                    rinvdone = sV.n

                    nchunks = 4 if STAGE >= 2 else 1
                    if c + 1 < nchunks:
                        lnq_ready_next = emit_ln_chunk(c + 1)

                    for tl in range(4):
                        j = 4 * c + tl
                        pb_q, pb_k = banks[2], banks[3]

                        @blk.tensor
                        def _(t, tl=tl, first=(tl == 0)):
                            t = TrackEng(t)
                            if first:
                                t.wait_ge(sV.h, rope_done)
                                t.wait_ge(dW.h, consts_W)
                            take(t, bk(pb_q), bk(pb_k))
                            for h in range(HL):
                                t.matmul(pb_q[:, h * 64:h * 64 + 64],
                                         qf[:, h * 512 + tl * 128:
                                            h * 512 + tl * 128 + 128],
                                         rt[:, :], start=True, stop=True)
                            for h in range(HKV):
                                t.matmul(pb_k[:, h * 64:h * 64 + 64],
                                         kf[:, h * 512 + tl * 128:
                                            h * 512 + tl * 128 + 128],
                                         rt[:, :], start=True, stop=True)
                            sT.inc(t.last)
                        pdone = sT.n

                        @blk.vector
                        def _(v, tl=tl, pdone=pdone):
                            v = TrackEng(v)
                            v.wait_ge(sT.h, pdone)
                            v.wait_ge(sV.h, rinvdone)
                            take(v, ("pwrp",))
                            for h in range(16):
                                bank = pb_q if h < 8 else pb_k
                                v.tensor_scalar(
                                    pbuf[:, h * 64:(h + 1) * 64],
                                    bank[:, (h % 8) * 64:(h % 8) * 64 + 64],
                                    rinv[:, tl * 16 + h:tl * 16 + h + 1],
                                    None, Op.mult)
                            kick(v, sV)
                            v.tensor_scalar(wrp[:, :], pbuf[:, :], INV2PI,
                                            MAGIC, Op.mult, Op.add)
                            v.tensor_scalar(wrp[:, :], wrp[:, :], MAGIC,
                                            None, Op.subtract)
                            v.scalar_tensor_tensor(wrp[:, :], wrp[:, :],
                                                   -TWOPI, pbuf[:, :],
                                                   Op.mult, Op.add)
                            sV.inc(v.last)
                        wdone = sV.n
                        put(bk(pb_q), sV)
                        put(bk(pb_k), sV)
                        if tl == 3:
                            put(("rinv",), sV)

                        @blk.vector
                        def _(v, wdone=wdone):
                            v = TrackEng(v)
                            v.scalar_tensor_tensor(pbuf[:, :], wrp[:, :],
                                                   -1.0, wrp[:, :],
                                                   Op.mult, Op.max)
                            sV.inc(v.last)
                        absdone = sV.n

                        @blk.scalar
                        def _(sc, absdone=absdone):
                            sc = TrackEng(sc)
                            sc.wait_ge(sV.h, absdone)
                            take(sc, ("qpkp",))
                            qs = qpk[:, :].rearrange("p (h w) -> p h w",
                                                     h=16, w=128)
                            wq_ = wrp[:, :].rearrange(
                                "p (h w) -> p h w", h=16, w=64)
                            aq = pbuf[:, :].rearrange(
                                "p (h w) -> p h w", h=16, w=64)
                            sc.activation(qs[:, :, 0:64], wq_, AF.Sin)
                            sc.activation(qs[:, :, 64:128], aq, AF.Sin,
                                          scale=-1.0, bias=halfpi[:, :])
                            sA.inc(sc.last)
                        featdone = sA.n
                        put(("pwrp",), sA)

                        term_v = {}

                        @blk.vector
                        def _(v, tl=tl, featdone=featdone):
                            v = TrackEng(v)
                            v.wait_ge(sA.h, featdone)
                            kp3 = qpk[:, 1024:2048].rearrange(
                                "p (a b) -> p a b", a=8, b=128)
                            t1k = term[:, :].rearrange("p (a b) -> p a b",
                                                       a=8, b=128)
                            for h in range(HL):
                                qph = qpk[:, h * 128:(h + 1) * 128]
                                v.tensor_tensor(
                                    t1k, kp3,
                                    qph.unsqueeze(1).to_broadcast(
                                        [128, 8, 128]), Op.mult)
                                v.tensor_reduce(sred[:, h * 8:(h + 1) * 8],
                                                t1k, AX.X, Op.add)
                            kick(v, sV)
                            v.tensor_scalar(sred[:, :], sred[:, :], SFRAC,
                                            None, Op.mult)
                            kick(v, sV)
                            s3 = sred[:, :].rearrange("p (a b) -> p a b",
                                                      a=HL, b=HKV)
                            for hp in range(HKV):
                                tb = term2[hp % 2]
                                t3 = tb[:, :].rearrange("p (a b) -> p a b",
                                                        a=HL, b=128)
                                vk = vtm[:, tl * 1024 + hp * 128:
                                         tl * 1024 + hp * 128 + 128]
                                take(v, ("term", hp % 2))
                                v.tensor_tensor(
                                    t3,
                                    vk.unsqueeze(1).to_broadcast(
                                        [128, HL, 128]),
                                    s3[:, :, hp:hp + 1].to_broadcast(
                                        [128, HL, 128]), Op.mult)
                                sV.inc(v.last)
                                term_v[hp] = sV.n
                        put(("qpkp",), sV)

                        @blk.gpsimd
                        def _(g, tl=tl):
                            g = TrackEng(g)
                            for hp in range(HKV):
                                tb = term2[hp % 2]
                                g.wait_ge(sV.h, term_v[hp])
                                if hp == 0:
                                    take(g, ("accb",))
                                    g.tensor_copy(acc[:, :], tb[:, :])
                                else:
                                    g.tensor_tensor(acc[:, :], acc[:, :],
                                                    tb[:, :], Op.add)
                                sG.inc(g.last)
                                put(("term", hp % 2), sG)
                        accdone = sG.n

                        pbx = psbf[tl % 2]

                        @blk.tensor
                        def _(t, pbx=pbx, tl=tl, accdone=accdone):
                            t = TrackEng(t)
                            t.wait_ge(sG.h, accdone)
                            take(t, ("psbf", tl % 2))
                            for h in range(HL):
                                t.transpose(pbx[:, h * 128:(h + 1) * 128],
                                            acc[:, h * 128:(h + 1) * 128],
                                            ident[:, :])
                            sT.inc(t.last)
                        tpd = sT.n
                        put(("accb",), sT)

                        @blk.vector
                        def _(v, pbx=pbx, j=j, tpd=tpd):
                            v = TrackEng(v)
                            v.wait_ge(sT.h, tpd)
                            base = afm[:, :].rearrange("p (h jj w) -> p h jj w",
                                                       h=HL, jj=16, w=128)
                            v.tensor_copy(
                                base[:, :, j, :],
                                pbx[:, :].rearrange("p (h w) -> p h w",
                                                    h=HL, w=128))
                            sV.inc(v.last)
                        put(("psbf", tl % 2), sV)

                        if DEBUG and c == 0 and tl == 0:
                            dbg_done = sV.n

                            @blk.gpsimd
                            def _(g):
                                g.wait_ge(sV.h, dbg_done)
                                dW.inc(g.dma_start(dbg["d_rinv"],
                                                   rinv[:, 0:64]), 16)
                                dW.inc(g.dma_start(dbg["d_wrp"], wrp[:, :]),
                                       16)
                                dW.inc(g.dma_start(dbg["d_abs"], pbuf[:, :]),
                                       16)
                                dW.inc(g.dma_start(dbg["d_qp"], qp[:, :]), 16)
                                dW.inc(g.dma_start(dbg["d_kp"], kp[:, :]), 16)
                                dW.inc(g.dma_start(dbg["d_sred"],
                                                   sred[:, :]), 16)
                                dW.inc(g.dma_start(dbg["d_acc"], acc[:, :]),
                                       16)
                                dW.inc(g.dma_start(dbg["d_qf"],
                                                   qf[:, 0:4096]), 16)
                                dW.inc(g.dma_start(dbg["d_kf"],
                                                   kf[:, 0:4096]), 16)
                                dW.inc(g.dma_start(dbg["d_vtm"],
                                                   vtm[:, 0:1024]), 16)
                                dW.inc(g.dma_start(dbg["d_lnq"],
                                                   lnq[:, 0:2048]), 16)
                            dbg_W = dW.n

                            @blk.vector
                            def _(v):
                                v = TrackEng(v)
                                v.wait_ge(dW.h, dbg_W)
                afm_ready = sV.n

            stkA.__exit__(None, None, None)
            otm4 = sbt(stkAb, "otm4", [128, 4 * D])
            wpC = sbt(stkAb, "wpC", [128, NKC * 256], BF16)
            wpD = sbt(stkAb, "wpD", [128, NKC * 256], BF16)
            bob = sbt(stkAb, "bob_sb", [128, D], BF16)
            wp4 = [wpA, wpB, wpC, wpD]

            with nc.Block() as blk:
                @blk.gpsimd
                def _(g):
                    g = TrackEng(g)
                    dW.inc(g.dma_start(bob[:, :], bob_in[:, :]), 16)
                bob_W = dW.n
                wo_li = 0

                def emit_wo_load(oc, s2b, woready):
                    nonlocal wo_li
                    buf = wp4[wo_li % 4]

                    @blk.gpsimd
                    def _(g, buf=buf, oc=oc, s2b=s2b, li=wo_li):
                        g = TrackEng(g)
                        take(g, ("wp4", li % 4))
                        dW.inc(load_pair(g, buf, wot,
                                         oc * 512 + s2b * 256), 16)
                    woready[(oc, s2b)] = (dW.n, wo_li % 4)
                    wo_li += 1

                for rh in range(2 if STAGE >= 2 else 0):
                    woready = {}
                    for oc0 in (0, 1):
                        for s2b in range(2):
                            emit_wo_load(oc0, s2b, woready)

                    for oc in range(4):
                        if oc >= 2 and rh == 0:
                            pass
                        for sub in range(4):
                            r = 4 * rh + sub
                            bank = banks[(oc * 4 + sub) % 2]

                            @blk.tensor
                            def _(t, r=r, sub=sub, oc=oc, bank=bank,
                                  first=(rh == 0 and oc == 0 and sub == 0)):
                                t = TrackEng(t)
                                if first:
                                    t.wait_ge(sV.h, afm_ready)
                                take(t, bk(bank))
                                for s2b in range(2):
                                    wr, bi = woready[(oc, s2b)]
                                    t.wait_ge(dW.h, wr)
                                    buf = wp4[bi]
                                    for kc in range(16):
                                        t.matmul(
                                            bank[:, s2b * 256:
                                                 s2b * 256 + 256],
                                            afm[:, r * 2048 + kc * 128:
                                                r * 2048 + kc * 128 + 128],
                                            buf[:, kc * 256:(kc + 1) * 256],
                                            start=(kc == 0), stop=(kc == 15))
                                sT.inc(t.last)
                            mmdone = sT.n
                            if sub == 3:
                                put(("wp4", woready[(oc, 0)][1]), sT)
                                put(("wp4", woready[(oc, 1)][1]), sT)

                            @blk.vector
                            def _(v, sub=sub, oc=oc, bank=bank,
                                  mmdone=mmdone):
                                v = TrackEng(v)
                                v.wait_ge(sT.h, mmdone)
                                v.wait_ge(dW.h, bob_W)
                                v.tensor_tensor(
                                    otm4[:, sub * D + oc * 512:
                                         sub * D + oc * 512 + 512],
                                    bank[:, :],
                                    bob[:, oc * 512:(oc + 1) * 512], Op.add)
                                sV.inc(v.last)
                            put(bk(bank), sV)
                        if oc < 2:
                            for s2b in range(2):
                                emit_wo_load(oc + 2, s2b, woready)
                    wo_done = sV.n

                    for sub in range(4):
                        r = 4 * rh + sub
                        oap = otm4[:, sub * D:(sub + 1) * D]
                        ln_stats(blk, oap,
                                 [(sV, wo_done), (dX, x_loaded)]
                                 if sub == 0 else [])

                        @blk.vector
                        def _(v, oap=oap, r=r):
                            v = TrackEng(v)
                            take(v, ("lntmp",))
                            v.tensor_scalar(lntmp[:, :], oap, mv2[:, 0:1],
                                            colA[:, :], Op.subtract, Op.mult)
                            v.tensor_tensor(xres[:, r * D:(r + 1) * D],
                                            xres[:, r * D:(r + 1) * D],
                                            lntmp[:, :], Op.add)
                            sV.inc(v.last)
                        put(("lntmp",), sV)
        attn_done = sV.n

        if STAGE < 5:
            with nc.Block() as blk:
                @blk.sync
                def _(s):
                    s.wait_ge(sV.h, sV.n)
                    for tt in range(8):
                        dO.inc(s.dma_start(
                            out_hbm[tt * 128:(tt + 1) * 128, :],
                            xres[:, tt * D:(tt + 1) * D]), 16)
                    s.wait_ge(dO.h, dO.n)

        stkY = ExitStack()
        ylb = sbt(stkY, "ylb", [128, 8 * P], BF16)
        stkE = ExitStack()
        sel32 = sbt(stkE, "sel32_s", [32, NE * 128], BF16)
        cwfm = sbt(stkE, "cwfm", [32, P], BF16)
        stkX = ExitStack()
        xlat = sbt(stkX, "xlat", [128, 8 * P], BF16)
        ysb = sbt(stkX, "ysb", [128, 8 * P])

        gate_ready = cwfm_ready = xlat_ready = 0

        stkL = ExitStack()
        with stkL:
            lnf = sbt(stkL, "lnf2", [128, NKC * P], BF16)
            wpAf = sbt(stkL, "wpAf", [128, NKC * 256], BF16)
            wpBf = sbt(stkL, "wpBf", [128, NKC * 256], BF16)
            gateb = sbt(stkL, "gateb", [128, NKC * NE], BF16)
            scs = sbt(stkL, "scs", [128, NE])
            sbb = sbt(stkL, "sbb", [128, NE])
            g8 = sbt(stkL, "g8", [128, 4 * 8])
            gpad = sbt(stkL, "gpad", [128, 12])
            gm8 = sbt(stkL, "gm8", [128, 8])
            repl8 = sbt(stkL, "repl8", [128, 8])
            gzap = sbt(stkL, "gzap", [128, 12])
            gmask = sbt(stkL, "gmask", [128, 4])
            msb = sbt(stkL, "msb", [128, NE])
            mzap = sbt(stkL, "mzap", [128, NE])
            selm = sbt(stkL, "selm", [128, NE])
            t8v = sbt(stkL, "t8v", [128, 8])
            cwb16 = sbt(stkL, "cwb16", [128, NE], BF16)

            with nc.Block() as blk:
                @blk.gpsimd
                def _(g):
                    g = TrackEng(g)
                    ap = gatet_in[:, :].rearrange("(kc p) o -> p kc o", p=128)
                    dW.inc(g.dma_start(gateb[:, :], ap), 16)
                    dW.inc(g.dma_start(sel32[:, :], sel32_in[:, :]), 16)
                gate_ready = dW.n

                for tt in range(8 if STAGE >= 3 else 0):
                    xap = xres[:, tt * D:(tt + 1) * D]
                    ln_stats(blk, xap,
                             [(sV, attn_done)] if tt == 0 else [])

                    @blk.vector
                    def _(v, xap=xap):
                        v = TrackEng(v)
                        take(v, ("lntmp",))
                        v.tensor_scalar(lntmp[:, :], xap, mv2[:, 0:1],
                                        colA[:, :], Op.subtract, Op.mult)
                        sV.inc(v.last)
                    lnv = sV.n

                    def lnf_dst(b, tt=tt):
                        base = lnf[:, :].rearrange("p (kc w) -> p kc w",
                                                   kc=16, w=P)
                        return base[:, b * 8:(b + 1) * 8,
                                    tt * 128:(tt + 1) * 128]
                    transpose_fm(blk, lntmp, lnf_dst, pe_waits=[(sV, lnv)])
                    put(("lntmp",), sT)
                ln2_ready = sV.n

                for tt in range(8 if STAGE >= 3 else 0):
                    bank = banks[tt % 2]

                    @blk.tensor
                    def _(t, tt=tt, bank=bank):
                        t = TrackEng(t)
                        t.wait_ge(dW.h, gate_ready)
                        if tt == 0:
                            t.wait_ge(sV.h, ln2_ready)
                        take(t, bk(bank))
                        for kc in range(16):
                            t.matmul(bank[:, 0:NE],
                                     lnf[:, kc * P + tt * 128:
                                         kc * P + tt * 128 + 128],
                                     gateb[:, kc * NE:(kc + 1) * NE],
                                     start=(kc == 0), stop=(kc == 15))
                        sT.inc(t.last)
                    gmm = sT.n

                    @blk.scalar
                    def _(sc, bank=bank, gmm=gmm):
                        sc = TrackEng(sc)
                        sc.wait_ge(sT.h, gmm)
                        take(sc, ("gatevec",))
                        sc.activation(scs[:, :], bank[:, 0:NE], AF.Sigmoid)
                        sA.inc(sc.last)
                    sigd = sA.n
                    put(bk(bank), sA)

                    @blk.vector
                    def _(v, sigd=sigd):
                        v = TrackEng(v)
                        v.wait_ge(sA.h, sigd)
                        v.tensor_tensor(sbb[:, :], scs[:, :], gbb[:, :],
                                        Op.add)
                        kick(v, sV)
                        for gi in range(4):
                            v.max(out=g8[:, gi * 8:(gi + 1) * 8],
                                  in_=sbb[:, gi * 8:(gi + 1) * 8])
                        kick(v, sV)
                        g3 = g8[:, :].rearrange("p (a b) -> p a b", a=4, b=8)
                        v.tensor_tensor(gpad[:, 0:4], g3[:, :, 0],
                                        g3[:, :, 1], Op.add)
                        v.memset(gpad[:, 4:12], NEG)
                        kick(v, sV)
                        v.max(out=gm8[:, :], in_=gpad[:, :])
                        kick(v, sV)
                        v.tensor_copy(repl8[:, 0:2], gm8[:, 0:2])
                        v.memset(repl8[:, 2:8], NEG)
                        kick(v, sV)
                        v.match_replace(out=gzap[:, :],
                                        in_to_replace=repl8[:, :],
                                        in_values=gpad[:, :], imm_value=NEG)
                        kick(v, sV)
                        v.tensor_tensor(gmask[:, :], gpad[:, 0:4],
                                        gzap[:, 0:4], Op.not_equal)
                        kick(v, sV)
                        v.tensor_scalar(gmask[:, :], gmask[:, :], 1.0, 1e30,
                                        Op.subtract, Op.mult)
                        kick(v, sV)
                        m3 = msb[:, :].rearrange("p (a b) -> p a b", a=4, b=8)
                        s3 = sbb[:, :].rearrange("p (a b) -> p a b", a=4, b=8)
                        v.tensor_tensor(m3, s3,
                                        gmask[:, :].unsqueeze(2)
                                        .to_broadcast([128, 4, 8]), Op.add)
                        kick(v, sV)
                        v.max(out=t8v[:, :], in_=msb[:, :])
                        kick(v, sV)
                        v.match_replace(out=mzap[:, :],
                                        in_to_replace=t8v[:, :],
                                        in_values=msb[:, :], imm_value=NEG2)
                        kick(v, sV)
                        v.tensor_tensor(selm[:, :], msb[:, :], mzap[:, :],
                                        Op.not_equal)
                        kick(v, sV)
                        v.tensor_tensor(selm[:, :], scs[:, :], selm[:, :],
                                        Op.mult)
                        kick(v, sV)
                        v.tensor_reduce(colA[:, :], selm[:, :], AX.X, Op.add)
                        kick(v, sV)
                        v.reciprocal(colA[:, :], colA[:, :])
                        kick(v, sV)
                        v.tensor_scalar(cwb16[:, :], selm[:, :], colA[:, :],
                                        2.5, Op.mult, Op.mult)
                        sV.inc(v.last)
                    cwd = sV.n
                    put(("gatevec",), sV)

                    @blk.tensor
                    def _(t, cwd=cwd):
                        t = TrackEng(t)
                        take(t, ("psbf", 0))
                        t.wait_ge(sV.h, cwd)
                        t.transpose(psbf[0][0:32, 0:128], cwb16[:, :],
                                    ident[:, :])
                        sT.inc(t.last)
                    cwt = sT.n

                    @blk.vector
                    def _(v, tt=tt, cwt=cwt):
                        v = TrackEng(v)
                        v.wait_ge(sT.h, cwt)
                        v.tensor_copy(cwfm[:, tt * 128:(tt + 1) * 128],
                                      psbf[0][0:32, 0:128])
                        sV.inc(v.last)
                    put(("psbf", 0), sV)
                cwfm_ready = sV.n

                for lc2 in range(4 if STAGE >= 3 else 0):
                    buf = [wpAf, wpBf][lc2 % 2]

                    @blk.gpsimd
                    def _(g, buf=buf, lc2=lc2):
                        g = TrackEng(g)
                        take(g, ("wpf", lc2 % 2))
                        dW.inc(load_pair(g, buf, fc1t, lc2 * 256), 16)
                    fcw = dW.n

                    for sub in range(2):
                        lc = lc2 * 2 + sub
                        for tc in range(2):
                            bank = banks[(lc * 2 + tc) % 2]

                            @blk.tensor
                            def _(t, buf=buf, sub=sub, tc=tc, bank=bank,
                                  fcw=fcw):
                                t = TrackEng(t)
                                t.wait_ge(dW.h, fcw)
                                take(t, bk(bank))
                                for kc in range(16):
                                    t.matmul(
                                        bank[:, :],
                                        buf[:, kc * 256 + sub * 128:
                                            kc * 256 + sub * 128 + 128],
                                        lnf[:, kc * P + tc * 512:
                                            kc * P + tc * 512 + 512],
                                        start=(kc == 0), stop=(kc == 15))
                                sT.inc(t.last)
                            mmd = sT.n
                            if sub == 1 and tc == 1:
                                put(("wpf", lc2 % 2), sT)

                            @blk.vector
                            def _(v, lc=lc, tc=tc, bank=bank, mmd=mmd):
                                v = TrackEng(v)
                                v.wait_ge(sT.h, mmd)
                                v.tensor_copy(
                                    xlat[:, lc * P + tc * 512:
                                         lc * P + tc * 512 + 512],
                                    bank[:, :])
                                sV.inc(v.last)
                            put(bk(bank), sV)
                xlat_ready = sV.n

                @blk.vector
                def _(v):
                    v = TrackEng(v)
                    v.memset(ysb[:, :], 0.0)
                    sV.inc(v.last)

        stkW = ExitStack()
        with stkW:
            w1b = [sbt(stkW, f"w1_{i}", [128, 8 * 512], BF16)
                   for i in range(2)]
            w2b = [sbt(stkW, f"w2_{i}", [128, 8 * 512], BF16)
                   for i in range(2)]
            he = sbt(stkW, "he", [128, 8 * 1024], BF16)
            gsb = sbt(stkW, "gsb", [128, 512], BF16)
            usb = sbt(stkW, "usb", [128, 512], BF16)
            cwsb = sbt(stkW, "cwsb", [128, 1024], BF16)

            def load_w1(g, buf, e, ih):
                ap = w1t[e, :, ih * 512:(ih + 1) * 512].rearrange(
                    "(lc p) i -> p lc i", p=128)
                return g.dma_start(buf[:, :], ap)

            def load_w2(g, buf, e, lh):
                ap = w2t[e, :, lh * 512:(lh + 1) * 512].rearrange(
                    "(ic p) l -> p ic l", p=128)
                return g.dma_start(buf[:, :], ap)

            with nc.Block() as blk:
                for e in range(NE if STAGE >= 4 else 0):
                    w1r = [0, 0]
                    for ih in range(2):
                        @blk.gpsimd
                        def _(g, ih=ih, e=e):
                            g = TrackEng(g)
                            take(g, ("w1", ih))
                            dW.inc(load_w1(g, w1b[ih], e, ih), 16)
                        w1r[ih] = dW.n

                    cb = banks[4]
                    for tc in range(2):
                        @blk.tensor
                        def _(t, e=e, tc=tc, first=(e == 0 and tc == 0)):
                            t = TrackEng(t)
                            if first:
                                t.wait_ge(sV.h, cwfm_ready)
                                t.wait_ge(dW.h, gate_ready)
                            take(t, bk(cb))
                            t.matmul(cb[:, :],
                                     sel32[:, e * 128:(e + 1) * 128],
                                     cwfm[:, tc * 512:(tc + 1) * 512],
                                     start=True, stop=True)
                            sT.inc(t.last)
                        cbd = sT.n

                        @blk.vector
                        def _(v, tc=tc, cbd=cbd):
                            v = TrackEng(v)
                            v.wait_ge(sT.h, cbd)
                            v.tensor_copy(cwsb[:, tc * 512:(tc + 1) * 512],
                                          banks[4][:, :])
                            sV.inc(v.last)
                        put(bk(cb), sV)
                    cwr = sV.n

                    for ih in range(2):
                        for icl in range(4):
                            ic = ih * 4 + icl
                            for tc in range(2):
                                bank = banks[(2 * ic + tc) % 2]

                                @blk.tensor
                                def _(t, ih=ih, icl=icl, tc=tc, bank=bank,
                                      first=(e == 0 and ih == 0
                                             and icl == 0 and tc == 0)):
                                    t = TrackEng(t)
                                    t.wait_ge(dW.h, w1r[ih])
                                    if first:
                                        t.wait_ge(sV.h, xlat_ready)
                                    take(t, bk(bank))
                                    for lc in range(8):
                                        t.matmul(
                                            bank[:, :],
                                            w1b[ih][:, lc * 512 + icl * 128:
                                                    lc * 512 + icl * 128
                                                    + 128],
                                            xlat[:, lc * P + tc * 512:
                                                 lc * P + tc * 512 + 512],
                                            start=(lc == 0), stop=(lc == 7))
                                    sT.inc(t.last)
                                hmm_ = sT.n
                                if icl == 3 and tc == 1:
                                    put(("w1", ih), sT)

                                @blk.vector
                                def _(v, ic=ic, tc=tc, bank=bank, hmm_=hmm_,
                                      cwr=cwr,
                                      firsth=(ih == 0 and icl == 0
                                              and tc == 0)):
                                    v = TrackEng(v)
                                    v.wait_ge(sT.h, hmm_)
                                    if firsth:
                                        v.wait_ge(sV.h, cwr)
                                        take(v, ("he",))
                                    v.tensor_scalar(gsb[:, :], bank[:, :],
                                                    0.0, None, Op.max)
                                    v.tensor_tensor(
                                        usb[:, :], gsb[:, :],
                                        cwsb[:, tc * 512:(tc + 1) * 512],
                                        Op.mult)
                                    v.tensor_tensor(
                                        he[:, ic * 1024 + tc * 512:
                                           ic * 1024 + tc * 512 + 512],
                                        usb[:, :], gsb[:, :], Op.mult)
                                    sV.inc(v.last)
                                put(bk(bank), sV)
                    hdone = sV.n

                    w2r = [0, 0]
                    for lh in range(2):
                        @blk.gpsimd
                        def _(g, lh=lh, e=e):
                            g = TrackEng(g)
                            take(g, ("w2", lh))
                            dW.inc(load_w2(g, w2b[lh], e, lh), 16)
                        w2r[lh] = dW.n

                    for tc in range(2):
                        for lc in range(8):
                            lh, lcl = lc // 4, lc % 4
                            bank = banks[2 + (lc + tc) % 2]

                            @blk.tensor
                            def _(t, tc=tc, lc=lc, lh=lh, lcl=lcl, bank=bank,
                                  first=(tc == 0 and lc == 0)):
                                t = TrackEng(t)
                                t.wait_ge(dW.h, w2r[lh])
                                if first:
                                    t.wait_ge(sV.h, hdone)
                                take(t, bk(bank))
                                for ic in range(8):
                                    t.matmul(
                                        bank[:, :],
                                        w2b[lh][:, ic * 512 + lcl * 128:
                                                ic * 512 + lcl * 128 + 128],
                                        he[:, ic * 1024 + tc * 512:
                                           ic * 1024 + tc * 512 + 512],
                                        start=(ic == 0), stop=(ic == 7))
                                sT.inc(t.last)
                            ymm = sT.n
                            if tc == 1 and lc == 3:
                                put(("w2", 0), sT)
                            if tc == 1 and lc == 7:
                                put(("w2", 1), sT)
                                put(("he",), sT)

                            @blk.vector
                            def _(v, tc=tc, lc=lc, bank=bank, ymm=ymm):
                                v = TrackEng(v)
                                v.wait_ge(sT.h, ymm)
                                yap = ysb[:, lc * P + tc * 512:
                                          lc * P + tc * 512 + 512]
                                v.tensor_tensor(yap, yap, bank[:, :], Op.add)
                                sV.inc(v.last)
                            put(bk(bank), sV)
        y_done = sV.n

        with nc.Block() as blk:
            @blk.vector
            def _(v):
                v = TrackEng(v)
                for i in range(8):
                    v.tensor_copy(ylb[:, i * P:(i + 1) * P],
                                  ysb[:, i * P:(i + 1) * P])
                sV.inc(v.last)
        ylb_ready = sV.n
        stkX.close()
        stkE.close()

        stkS = ExitStack()
        with stkS:
            shfm = sbt(stkS, "shfm", [128, 32 * 512], BF16)
            dwn3 = [sbt(stkS, f"dwn{i}", [128, 16 * 256], BF16)
                    for i in range(3)]
            fc2b = sbt(stkS, "fc2b", [128, 8 * 256], BF16)
            otmh = sbt(stkS, "otmh", [128, 4 * D])

            if True:
                for half in range(2 if STAGE >= 5 else 0):
                    stkH = ExitStack()
                    lnh2 = sbt(stkH, f"lnh2_{half}", [128, NKC * 512], BF16)
                    upwb = [sbt(stkH, f"upw{half}_{i}", [128, NKC * 128],
                                BF16) for i in range(2)]
                    gsh = sbt(stkH, f"gsh{half}", [128, 512], BF16)
                    blk_cm = nc.Block()
                    blk = blk_cm.__enter__()
                    for tl in range(4):
                        tt = half * 4 + tl
                        xap = xres[:, tt * D:(tt + 1) * D]
                        ln_stats(blk, xap)

                        @blk.vector
                        def _(v, xap=xap):
                            v = TrackEng(v)
                            take(v, ("lntmp",))
                            v.tensor_scalar(lntmp[:, :], xap, mv2[:, 0:1],
                                            colA[:, :], Op.subtract, Op.mult)
                            sV.inc(v.last)
                        lnv = sV.n

                        def lnh_dst(b, tl=tl):
                            base = lnh2[:, :].rearrange("p (kc w) -> p kc w",
                                                        kc=16, w=512)
                            return base[:, b * 8:(b + 1) * 8,
                                        tl * 128:(tl + 1) * 128]
                        transpose_fm(blk, lntmp, lnh_dst,
                                     pe_waits=[(sV, lnv)])
                        put(("lntmp",), sT)
                    lnh2_ready = sV.n

                    for icg in range(32):
                        upw = upwb[icg % 2]

                        @blk.gpsimd
                        def _(g, icg=icg, upw=upw):
                            g = TrackEng(g)
                            take(g, ("upw", icg % 2))
                            dW.inc(load_pair(g, upw, upt, icg * 128,
                                             width=128), 16)
                        upr = dW.n

                        if True:
                            bank = banks[icg % 2]

                            @blk.tensor
                            def _(t, bank=bank, upr=upr, upw=upw,
                                  first=(icg == 0)):
                                t = TrackEng(t)
                                t.wait_ge(dW.h, upr)
                                if first:
                                    t.wait_ge(sV.h, lnh2_ready)
                                take(t, bk(bank))
                                for kc in range(16):
                                    t.matmul(
                                        bank[:, :],
                                        upw[:, kc * 128:(kc + 1) * 128],
                                        lnh2[:, kc * 512:(kc + 1) * 512],
                                        start=(kc == 0), stop=(kc == 15))
                                sT.inc(t.last)
                            umm = sT.n
                            put(("upw", icg % 2), sT)

                            @blk.vector
                            def _(v, icg=icg, bank=bank, umm=umm):
                                v = TrackEng(v)
                                v.wait_ge(sT.h, umm)
                                take(v, ("gsh",))
                                v.tensor_scalar(gsh[:, :], bank[:, :],
                                                0.0, None, Op.max)
                                v.tensor_tensor(
                                    shfm[:, icg * 512:(icg + 1) * 512],
                                    gsh[:, :], gsh[:, :], Op.mult)
                                sV.inc(v.last)
                            put(bk(bank), sV)
                            put(("gsh",), sV)
                    sh_ready = sV.n

                    blk_cm.__exit__(None, None, None)
                    stkH.close()
                    blk_cm = nc.Block()
                    blk = blk_cm.__enter__()

                    for oc in range(4):
                        for s2b in range(2):
                            col0 = oc * 512 + s2b * 256
                            step = oc * 2 + s2b
                            dhalf = []
                            for h2 in range(2):
                                bi = (half * 16 + step * 2 + h2) % 3
                                buf = dwn3[bi]

                                @blk.gpsimd
                                def _(g, col0=col0, h2=h2, bi=bi, buf=buf):
                                    g = TrackEng(g)
                                    take(g, ("dwn3", bi))
                                    ap = downt[h2 * 2048:(h2 + 1) * 2048,
                                               col0:col0 + 256].rearrange(
                                        "(ic p) o -> p ic o", p=128)
                                    dW.inc(g.dma_start(buf[:, :], ap), 16)
                                dhalf.append((bi, dW.n))

                            @blk.gpsimd
                            def _(g, col0=col0):
                                g = TrackEng(g)
                                take(g, ("fc2b",))
                                apf = fc2t[:, col0:col0 + 256].rearrange(
                                    "(lc p) o -> p lc o", p=128)
                                dW.inc(g.dma_start(fc2b[:, :], apf), 16)
                            fcr = dW.n

                            for tl in range(4):
                                tt = half * 4 + tl
                                bank = banks[2 + tl % 2]

                                @blk.tensor
                                def _(t, tl=tl, tt=tt, bank=bank,
                                      dhalf=tuple(dhalf), fcr=fcr,
                                      first=(oc == 0 and s2b == 0
                                             and tl == 0)):
                                    t = TrackEng(t)
                                    if first:
                                        t.wait_ge(sV.h, sh_ready)
                                        t.wait_ge(sV.h, ylb_ready)
                                    for h2 in range(2):
                                        bi, dwr = dhalf[h2]
                                        t.wait_ge(dW.h, dwr)
                                        if h2 == 0:
                                            take(t, bk(bank))
                                        buf = dwn3[bi]
                                        for i2 in range(16):
                                            ic = h2 * 16 + i2
                                            t.matmul(
                                                bank[:, 0:256],
                                                shfm[:, ic * 512 + tl * 128:
                                                     ic * 512 + tl * 128
                                                     + 128],
                                                buf[:, i2 * 256:
                                                    (i2 + 1) * 256],
                                                start=(ic == 0), stop=False)
                                    t.wait_ge(dW.h, fcr)
                                    for lc in range(8):
                                        t.matmul(
                                            bank[:, 0:256],
                                            ylb[:, lc * P + tt * 128:
                                                lc * P + tt * 128 + 128],
                                            fc2b[:, lc * 256:(lc + 1) * 256],
                                            start=False, stop=(lc == 7))
                                    sT.inc(t.last)
                                dmm = sT.n
                                if tl == 3:
                                    put(("dwn3", dhalf[0][0]), sT)
                                    put(("dwn3", dhalf[1][0]), sT)
                                    put(("fc2b",), sT)

                                @blk.vector
                                def _(v, tl=tl, oc=oc, s2b=s2b, bank=bank,
                                      dmm=dmm):
                                    v = TrackEng(v)
                                    v.wait_ge(sT.h, dmm)
                                    v.tensor_copy(
                                        otmh[:, tl * D + oc * 512
                                             + s2b * 256:
                                             tl * D + oc * 512
                                             + s2b * 256 + 256],
                                        bank[:, 0:256])
                                    sV.inc(v.last)
                                put(bk(bank), sV)
                    ffn_done = sV.n

                    for tl in range(4):
                        tt = half * 4 + tl
                        oap = otmh[:, tl * D:(tl + 1) * D]
                        ln_stats(blk, oap,
                                 [(sV, ffn_done)] if tl == 0 else [])

                        @blk.vector
                        def _(v, oap=oap, tt=tt):
                            v = TrackEng(v)
                            take(v, ("lntmp",))
                            v.tensor_scalar(lntmp[:, :], oap, mv2[:, 0:1],
                                            colA[:, :], Op.subtract, Op.mult)
                            v.tensor_tensor(xres[:, tt * D:(tt + 1) * D],
                                            xres[:, tt * D:(tt + 1) * D],
                                            lntmp[:, :], Op.add)
                            sV.inc(v.last)
                        put(("lntmp",), sV)
                        xdone = sV.n

                        @blk.sync
                        def _(s, tt=tt, xdone=xdone):
                            s = TrackEng(s)
                            s.wait_ge(sV.h, xdone)
                            dO.inc(s.dma_start(
                                out_hbm[tt * 128:(tt + 1) * 128, :],
                                xres[:, tt * D:(tt + 1) * D]), 16)

                    blk_cm.__exit__(None, None, None)

            if STAGE >= 5:
                with nc.Block() as blk:
                    @blk.sync
                    def _(s):
                        s = TrackEng(s)
                        s.wait_ge(dO.h, dO.n)

        stkY.close()
    return nc


_CACHE = {}


def kernel(**inputs):
    import ml_dtypes

    x = np.asarray(inputs["L0_attn_hidden_states"], dtype=np.float32)
    B = x.shape[0]
    assert x.shape == (B, S2, D) and B == 4

    f = np.ascontiguousarray
    wq = np.asarray(inputs["L0_attn_Wq"], np.float32)
    wk = np.asarray(inputs["L0_attn_Wk"], np.float32)
    wv = np.asarray(inputs["L0_attn_Wv"], np.float32)
    wo = np.asarray(inputs["L0_attn_Wo"], np.float32)
    bq = np.asarray(inputs["L0_attn_bq"], np.float32)
    bkb = np.asarray(inputs["L0_attn_bk"], np.float32)
    bv = np.asarray(inputs["L0_attn_bv"], np.float32)
    bo = np.asarray(inputs["L0_attn_bo"], np.float32)
    R = np.asarray(inputs["L0_attn_rand_matrix"], np.float32)
    gate_w = np.asarray(inputs["L0_ffn_gate_w"], np.float32)
    gate_b = np.asarray(inputs["L0_ffn_gate_bias"], np.float32)
    fc1 = np.asarray(inputs["L0_ffn_fc1_w"], np.float32)
    fc2 = np.asarray(inputs["L0_ffn_fc2_w"], np.float32)
    w1 = np.asarray(inputs["L0_ffn_w1"], np.float32)
    w2 = np.asarray(inputs["L0_ffn_w2"], np.float32)
    up = np.asarray(inputs["L0_ffn_shared_up_w"], np.float32)
    down = np.asarray(inputs["L0_ffn_shared_down_w"], np.float32)

    wqT = f(wq.T)
    shared = {
        "wkt": f(wk.T), "wvt": f(wv.T), "wot": f(wo.T),
        "rt": f(R.T),
        "gatet": f(gate_w.T), "fc1t": f(fc1.T), "fc2t": f(fc2.T),
        "w1t": f(w1.transpose(0, 2, 1)), "w2t": f(w2.transpose(0, 2, 1)),
        "upt": f(up.T), "downt": f(down.T),
        "bkf": f(bkb.reshape(HKV, 128).T),
        "bvb": f(np.broadcast_to(bv, (128, HKV * 128))),
        "bob": f(np.broadcast_to(bo, (128, D))),
        "gbb": f(np.broadcast_to(gate_b, (128, NE))),
    }
    bqf_full = f(bq.reshape(16, 128).T)

    ident = np.eye(128, dtype=ml_dtypes.bfloat16)
    sel = np.zeros((32, NE * 128), dtype=ml_dtypes.bfloat16)
    for e in range(NE):
        sel[e, e * 128:(e + 1) * 128] = 1
    ones1 = np.ones((128, 1), dtype=ml_dtypes.bfloat16)
    hd = 128
    t32 = np.arange(0, hd, 2).astype(np.float32) / np.float32(hd)
    inv_freq = (np.float32(1.0) / (np.float32(10000.0) ** t32)) \
        .astype(np.float32)
    pos = np.arange(S2, dtype=np.float32)
    freqs = (pos[:, None] * inv_freq[None, :]).astype(np.float32)
    emb = np.concatenate([freqs, freqs], axis=1)
    cos_t = np.cos(emb).astype(np.float32)
    sin_t = np.sin(emb).astype(np.float32)
    i_idx = np.arange(S2)
    tperm = 16 * (i_idx % 128) + (i_idx // 128)
    cosf = f(cos_t[tperm, :].T)
    sinf = f(sin_t[tperm, :].T)
    consts = {"ident": ident, "sel32": sel, "ones1": ones1,
              "cosf": cosf, "sinf": sinf}

    in_maps = []
    for core in range(8):
        b, hh = core // 2, core % 2
        m = dict(shared)
        m.update(consts)
        m["xfull"] = f(x[b])
        m["xr"] = f(x[b, hh * P:(hh + 1) * P, :])
        m["wqt"] = f(wqT[:, hh * 1024:(hh + 1) * 1024])
        m["bqf"] = f(bqf_full[:, hh * 8:(hh + 1) * 8])
        in_maps.append(m)

    if "nc" not in _CACHE:
        _CACHE["nc"] = build()
    nc = _CACHE["nc"]
    res = run_bass_kernel_spmd(nc, in_maps, core_ids=list(range(8)))
    out = np.empty((B, S2, D), dtype=np.float32)
    for core in range(8):
        b, hh = core // 2, core % 2
        out[b, hh * P:(hh + 1) * P, :] = res.results[core]["out"]
    return out


# version 11
# speedup vs baseline: 1.2959x; 1.1068x over previous
import os
import numpy as np
import concourse.bass as bass
import concourse.mybir as mybir
from bass_rust import BassInstruction as _BassInstruction
from concourse.bass_utils import run_bass_kernel_spmd
from concourse.alu_op_type import AluOpType as Op
from contextlib import ExitStack

F32 = mybir.dt.float32
BF16 = mybir.dt.bfloat16
AF = mybir.ActivationFunctionType
AX = mybir.AxisListType

PI = float(np.pi)
INV2PI = float(1.0 / (2 * np.pi))
TWOPI = float(2 * np.pi)
MAGIC = float(1.5 * 2 ** 23)
NEG = -1e30
NEG2 = -2e30

P = 1024
S2 = 2048
D = 2048
NKC = 16
HL = 8
HKV = 8
LAT = 1024
NE = 32
SH = 4096
EPS_LN = 1e-5
SFRAC = float(2.0 / 128.0)


class Sem:
    __slots__ = ("h", "n")

    def __init__(self, h):
        self.h = h
        self.n = 0

    def inc(self, inst, k=1):
        inst.then_inc(self.h, k)
        self.n += k
        return self.n


_NOTRACK = {"wait_ge", "nop", "sem_inc", "sem_clear", "print", "breakpoint",
            "maybe_wait_ge", "drain"}


class TrackEng:

    def __init__(self, e):
        object.__setattr__(self, "_e", e)
        object.__setattr__(self, "last", None)

    def __getattr__(self, n):
        a = getattr(object.__getattribute__(self, "_e"), n)
        if callable(a) and n not in _NOTRACK:
            def w(*args, **kw):
                r = a(*args, **kw)
                if isinstance(r, _BassInstruction):
                    object.__setattr__(self, "last", r)
                return r
            return w
        return a


STAGE = int(os.environ.get("MK_STAGE", "5"))
SUB = int(os.environ.get("MK_SUB", "9"))
DEBUG = int(os.environ.get("MK_DEBUG", "0"))


def build():
    nc = bass.Bass("TRN2", target_bir_lowering=False)

    def din(name, shape, dt=F32):
        return nc.dram_tensor(name, shape, dt, kind="ExternalInput").ap()

    xfull = din("xfull", [S2, D])
    xr_in = din("xr", [P, D])
    out_hbm = nc.dram_tensor("out", [P, D], F32, kind="ExternalOutput").ap()
    dbg = {}
    if DEBUG:
        for nm, w in [("d_rinv", 64), ("d_wrp", 1024), ("d_abs", 1024),
                      ("d_qp", 1024), ("d_kp", 1024), ("d_sred", 64),
                      ("d_acc", 1024), ("d_qf", 4096), ("d_kf", 4096),
                      ("d_vtm", 1024), ("d_lnq", 2048), ("d_n2", 64)]:
            dbg[nm] = nc.dram_tensor(nm, [128, w], F32,
                                     kind="ExternalOutput").ap()

    wqt = din("wqt", [D, HL * 128])
    wkt = din("wkt", [D, HKV * 128])
    wvt = din("wvt", [D, HKV * 128])
    wot = din("wot", [D, D])
    rt_in = din("rt", [128, 64])
    gatet_in = din("gatet", [D, NE]); fc1t = din("fc1t", [D, LAT])
    fc2t = din("fc2t", [LAT, D])
    w1t = din("w1t", [NE, LAT, LAT]); w2t = din("w2t", [NE, LAT, LAT])
    upt = din("upt", [D, SH]); downt = din("downt", [SH, D])
    bqf_in = din("bqf", [128, HL]); bkf_in = din("bkf", [128, HKV])
    bvb_in = din("bvb", [128, HKV * 128]); bob_in = din("bob", [128, D])
    gbb_in = din("gbb", [128, NE])
    cosf_in = din("cosf", [128, S2]); sinf_in = din("sinf", [128, S2])
    ident_in = din("ident", [128, 128], BF16)
    sel32_in = din("sel32", [32, NE * 128], BF16)
    ones1_in = din("ones1", [128, 1], BF16)

    top = ExitStack()

    def sbt(stack, name, shape, dt=F32):
        return stack.enter_context(nc.sbuf_tensor(name, shape, dt))

    with top:
        xres = sbt(top, "xres", [128, 8 * D])
        ident = sbt(top, "ident_s", [128, 128], BF16)
        ones1 = sbt(top, "ones1_s", [128, 1], BF16)
        stat = sbt(top, "stat", [128, 24])
        mv2 = sbt(top, "mv2", [128, 2])
        colA = sbt(top, "colA", [128, 1])
        lntmp = sbt(top, "lntmp", [128, D], BF16)
        halfpi = sbt(top, "halfpi", [128, 1])
        nhalf = sbt(top, "nhalf", [128, 64])
        bqf = sbt(top, "bqf_s", [128, HL])
        bkf = sbt(top, "bkf_s", [128, HKV])
        gbb = sbt(top, "gbb_s", [128, NE])

        banks = [top.enter_context(nc.psum_tensor(f"bk{i}", [128, 512], F32))
                 for i in range(6)]
        psbf = [top.enter_context(nc.psum_tensor(f"pb{i}", [128, 1024], BF16))
                for i in range(2)]

        dW = Sem(top.enter_context(nc.semaphore("dW")))
        dX = Sem(top.enter_context(nc.semaphore("dX")))
        dO = Sem(top.enter_context(nc.semaphore("dO")))
        sT = Sem(top.enter_context(nc.semaphore("sT")))
        sV = Sem(top.enter_context(nc.semaphore("sV")))
        sA = Sem(top.enter_context(nc.semaphore("sA")))
        sG = Sem(top.enter_context(nc.semaphore("sG")))

        guard = {}

        def take(eng, *keys):
            for key in keys:
                g = guard.pop(key, None)
                if g is not None:
                    eng.wait_ge(g[0].h, g[1])

        def put(key, sem):
            guard[key] = (sem, sem.n)

        def bk(b):
            return ("bank", id(b))

        def kick(eng, sem):
            sem.inc(eng.last)
            eng.wait_ge(sem.h, sem.n)

        with nc.Block() as blk:
            @blk.sync
            def _(s):
                s = TrackEng(s)
                dX.inc(s.dma_start(bqf[:, :], bqf_in[:, :]), 16)
                dX.inc(s.dma_start(bkf[:, :], bkf_in[:, :]), 16)
                dX.inc(s.dma_start(gbb[:, :], gbb_in[:, :]), 16)
                dX.inc(s.dma_start(ident[:, :], ident_in[:, :]), 16)
                dX.inc(s.dma_start(ones1[:, :], ones1_in[:, :]), 16)

            @blk.vector
            def _(v):
                v = TrackEng(v)
                v.memset(halfpi[:, :], PI / 2)
                v.memset(nhalf[:, :], -0.5)
                sV.inc(v.last)
        x_loaded = dX.n
        consts_v = sV.n

        def load_pair(g, buf, src, col0, width=256):
            ap = src[:, col0:col0 + width].rearrange("(kc p) o -> p kc o",
                                                     p=128)
            return g.dma_start(buf[:, 0:NKC * width], ap)

        def ln_stats(blk, src_ap, wait_sems=()):
            @blk.vector
            def _(v):
                v = TrackEng(v)
                for sem, val in wait_sems:
                    v.wait_ge(sem.h, val)
                for c in range(4):
                    v.bn_stats(stat[:, c * 6:(c + 1) * 6],
                               src_ap[:, c * 512:(c + 1) * 512])
                kick(v, sV)
                v.bn_aggr(mv2[:, :], stat[:, :])
                kick(v, sV)
                v.tensor_scalar(colA[:, :], mv2[:, 1:2], EPS_LN, None, Op.add)
                sV.inc(v.last)
            vst = sV.n

            @blk.gpsimd
            def _(g):
                g = TrackEng(g)
                g.wait_ge(sV.h, vst)
                g.wait_ge(sV.h, consts_v)
                g.tensor_tensor(colA[:, :], colA[:, :], nhalf[:, 0:1], Op.pow)
                sG.inc(g.last)
            gst = sG.n

            @blk.vector
            def _(v):
                v = TrackEng(v)
                v.wait_ge(sG.h, gst)

        def transpose_fm(blk, src, dst_ap_fn, pe_waits=()):
            @blk.tensor
            def _(t):
                t = TrackEng(t)
                for sem, val in pe_waits:
                    t.wait_ge(sem.h, val)
                take(t, ("psbf", 0), ("psbf", 1))
                for kc in range(16):
                    b, j = kc // 8, kc % 8
                    t.transpose(psbf[b][:, j * 128:(j + 1) * 128],
                                src[:, kc * 128:(kc + 1) * 128], ident[:, :])
                sT.inc(t.last)
            tdone = sT.n

            @blk.vector
            def _(v):
                v = TrackEng(v)
                v.wait_ge(sT.h, tdone)
                for b in range(2):
                    v.tensor_copy(dst_ap_fn(b),
                                  psbf[b][:, :].rearrange("p (j c) -> p j c",
                                                          j=8, c=128))
                sV.inc(v.last)
            put(("psbf", 0), sV)
            put(("psbf", 1), sV)
            return sV.n

        stkAb = ExitStack()
        with stkAb:
            wpA = sbt(stkAb, "wpA", [128, NKC * 256], BF16)
            wpB = sbt(stkAb, "wpB", [128, NKC * 256], BF16)
            afm = sbt(stkAb, "afm", [128, HL * 2048], BF16)
            stkA = ExitStack()
            stkA.__enter__()
            cosf = sbt(stkA, "cosf_sb", [128, 512], BF16)
            sinf = sbt(stkA, "sinf_sb", [128, 512], BF16)
            bvb = sbt(stkA, "bvb_sb", [128, HKV * 128], BF16)
            rt = sbt(stkA, "rt_sb", [128, 64], BF16)
            evtmp = sbt(stkA, "evt", [128, 512], BF16)
            rotb = sbt(stkA, "rot", [128, 512], BF16)
            xtile = sbt(stkA, "xtile", [128, D])
            lnq2 = [sbt(stkA, f"lnq{i}", [128, NKC * 512], BF16)
                    for i in range(2)]
            qf = sbt(stkA, "qf", [128, HL * 512], BF16)
            kf = sbt(stkA, "kf", [128, HKV * 512], BF16)
            vtm = sbt(stkA, "vtm", [128, 4 * HKV * 128], BF16)
            sqb = sbt(stkA, "sqb", [128, 512], BF16)
            rinv = sbt(stkA, "rinv", [128, 64])
            pbuf = sbt(stkA, "pbuf", [128, 1024])
            wrp = sbt(stkA, "wrp", [128, 1024])
            qpk = sbt(stkA, "qpk", [128, 16 * 128], BF16)
            sred = sbt(stkA, "sred", [128, HL * HKV])
            term2 = [sbt(stkA, f"term{i}", [128, HL * 128], BF16)
                     for i in range(2)]
            term = term2[0]
            acc = sbt(stkA, "acc", [128, HL * 128], BF16)

            with nc.Block() as blk:
                @blk.gpsimd
                def _(g):
                    g = TrackEng(g)
                    dW.inc(g.dma_start(bvb[:, :], bvb_in[:, :]), 16)
                    dW.inc(g.dma_start(rt[:, :], rt_in[:, :]), 16)
                consts_W = dW.n

                xperm = xfull.rearrange("(p j) d -> j p d", j=16)

                for c in range(4 if STAGE >= 2 else 1):
                    @blk.gpsimd
                    def _(g, c=c):
                        g = TrackEng(g)
                        take(g, ("trig",))
                        dW.inc(g.dma_start(cosf[:, :],
                                           cosf_in[:, c * 512:(c + 1) * 512]),
                               16)
                        dW.inc(g.dma_start(sinf[:, :],
                                           sinf_in[:, c * 512:(c + 1) * 512]),
                               16)
                    trig_W = dW.n

                    def emit_ln_chunk(cc):
                        lnqd = lnq2[cc % 2]
                        for tl in range(4):
                            j = 4 * cc + tl

                            @blk.sync
                            def _(s, j=j):
                                s = TrackEng(s)
                                take(s, ("xtile",))
                                dX.inc(s.dma_start(xtile[:, :],
                                                   xperm[j, :, :]), 16)
                            xtl = dX.n

                            ln_stats(blk, xtile[:, :], [(dX, xtl)])

                            @blk.vector
                            def _(v):
                                v = TrackEng(v)
                                take(v, ("lntmp",))
                                v.tensor_scalar(lntmp[:, :], xtile[:, :],
                                                mv2[:, 0:1], colA[:, :],
                                                Op.subtract, Op.mult)
                                sV.inc(v.last)
                            lnv = sV.n
                            put(("xtile",), sV)

                            def lnq_dst(b, tl=tl, lnqd=lnqd):
                                base = lnqd[:, :].rearrange(
                                    "p (kc w) -> p kc w", kc=16, w=512)
                                return base[:, b * 8:(b + 1) * 8,
                                            tl * 128:(tl + 1) * 128]
                            transpose_fm(blk, lntmp, lnq_dst,
                                         pe_waits=[(sV, lnv)])
                            put(("lntmp",), sT)
                        return sV.n

                    if c == 0:
                        lnq_ready = emit_ln_chunk(0)
                    else:
                        lnq_ready = lnq_ready_next
                    lnq = lnq2[c % 2]

                    plans = ([("q", wqt, i) for i in range(4)]
                             + [("k", wkt, i) for i in range(4)])
                    if SUB < 2:
                        plans = []
                    for pi, (kind, src, i2) in enumerate(plans):
                        buf = [wpA, wpB][pi % 2]

                        @blk.gpsimd
                        def _(g, buf=buf, src=src, i2=i2, pi=pi):
                            g = TrackEng(g)
                            take(g, ("wp", pi % 2))
                            dW.inc(load_pair(g, buf, src, i2 * 256), 16)
                        wready = dW.n

                        for sub in range(2):
                            h = i2 * 2 + sub
                            dst, bias = (qf, bqf) if kind == "q" else (kf, bkf)
                            bank = banks[(pi * 2 + sub) % 2]

                            @blk.tensor
                            def _(t, buf=buf, sub=sub, bank=bank,
                                  wready=wready,
                                  first=(pi == 0 and sub == 0)):
                                t = TrackEng(t)
                                t.wait_ge(dW.h, wready)
                                if first:
                                    t.wait_ge(sV.h, lnq_ready)
                                take(t, bk(bank))
                                for kc in range(16):
                                    t.matmul(bank[:, :],
                                             buf[:, kc * 256 + sub * 128:
                                                 kc * 256 + sub * 128 + 128],
                                             lnq[:, kc * 512:(kc + 1) * 512],
                                             start=(kc == 0), stop=(kc == 15))
                                sT.inc(t.last)
                            mmdone = sT.n
                            if sub == 1:
                                put(("wp", pi % 2), sT)

                            @blk.vector
                            def _(v, bank=bank, h=h, dst=dst, bias=bias,
                                  mmdone=mmdone,
                                  firstc=(pi == 0 and sub == 0 and c == 0)):
                                v = TrackEng(v)
                                v.wait_ge(sT.h, mmdone)
                                if firstc:
                                    v.wait_ge(dW.h, consts_W)
                                v.wait_ge(dW.h, trig_W)
                                v.tensor_scalar(evtmp[:, :], bank[:, :],
                                                bias[:, h:h + 1], None, Op.add)
                                v.tensor_scalar(rotb[0:64, :],
                                                evtmp[64:128, :], -1.0, None,
                                                Op.mult)
                                v.tensor_copy(rotb[64:128, :], evtmp[0:64, :])
                                v.tensor_tensor(evtmp[:, :], evtmp[:, :],
                                                cosf[:, :], Op.mult)
                                v.tensor_tensor(rotb[:, :], rotb[:, :],
                                                sinf[:, :], Op.mult)
                                v.tensor_tensor(dst[:, h * 512:(h + 1) * 512],
                                                evtmp[:, :], rotb[:, :],
                                                Op.add)
                                sV.inc(v.last)
                            evdone = sV.n
                            put(bk(bank), sV)

                            @blk.vector
                            def _(v, h=h, dst=dst):
                                v = TrackEng(v)
                                take(v, ("sqb",))
                                v.tensor_tensor(sqb[:, :],
                                                dst[:, h * 512:(h + 1) * 512],
                                                dst[:, h * 512:(h + 1) * 512],
                                                Op.mult)
                                sV.inc(v.last)
                            sqdone = sV.n

                            hidx = h if kind == "q" else HL + h
                            nb = banks[5]

                            @blk.tensor
                            def _(t, sqdone=sqdone, hidx=hidx, nb=nb):
                                t = TrackEng(t)
                                t.wait_ge(sV.h, sqdone)
                                take(t, bk(nb))
                                for tl in range(4):
                                    t.matmul(nb[:, tl * 16 + hidx:
                                                tl * 16 + hidx + 1],
                                             sqb[:, tl * 128:(tl + 1) * 128],
                                             ones1[:, :], start=True,
                                             stop=True)
                                sT.inc(t.last)
                            put(("sqb",), sT)
                    rope_done = sV.n
                    put(("trig",), sV)
                    n2done = sT.n

                    for i2 in range(4 if SUB >= 3 else 0):
                        buf = [wpA, wpB][i2 % 2]

                        @blk.gpsimd
                        def _(g, buf=buf, i2=i2):
                            g = TrackEng(g)
                            take(g, ("wp", i2 % 2))
                            dW.inc(load_pair(g, buf, wvt, i2 * 256), 16)
                        vwready = dW.n

                        for tl in range(4):
                            bank = banks[2 + (tl % 2)]

                            @blk.tensor
                            def _(t, buf=buf, i2=i2, tl=tl, bank=bank,
                                  vwready=vwready):
                                t = TrackEng(t)
                                t.wait_ge(dW.h, vwready)
                                take(t, bk(bank))
                                for kc in range(16):
                                    t.matmul(
                                        bank[:, 0:256],
                                        lnq[:, kc * 512 + tl * 128:
                                            kc * 512 + tl * 128 + 128],
                                        buf[:, kc * 256:(kc + 1) * 256],
                                        start=(kc == 0), stop=(kc == 15))
                                sT.inc(t.last)
                            mmdone = sT.n
                            if tl == 3:
                                put(("wp", i2 % 2), sT)

                            @blk.vector
                            def _(v, i2=i2, tl=tl, bank=bank, mmdone=mmdone):
                                v = TrackEng(v)
                                v.wait_ge(sT.h, mmdone)
                                v.tensor_tensor(
                                    vtm[:, tl * 1024 + i2 * 256:
                                        tl * 1024 + i2 * 256 + 256],
                                    bank[:, 0:256],
                                    bvb[:, i2 * 256:(i2 + 1) * 256], Op.add)
                                sV.inc(v.last)
                            put(bk(bank), sV)

                    if SUB < 4:
                        continue
                    @blk.vector
                    def _(v, n2done=n2done):
                        v = TrackEng(v)
                        v.wait_ge(sT.h, n2done)
                        take(v, ("rinv",))
                        v.tensor_copy(rinv[:, :], banks[5][:, 0:64])
                        sV.inc(v.last)
                    ncp = sV.n
                    put(bk(banks[5]), sV)

                    @blk.gpsimd
                    def _(g, ncp=ncp):
                        g = TrackEng(g)
                        g.wait_ge(sV.h, ncp)
                        g.wait_ge(sV.h, consts_v)
                        g.tensor_tensor(rinv[:, :], rinv[:, :],
                                        nhalf[:, 0:64], Op.pow)
                        sG.inc(g.last)
                    ngp = sG.n

                    @blk.vector
                    def _(v, ngp=ngp):
                        v = TrackEng(v)
                        v.wait_ge(sG.h, ngp)
                        v.tensor_scalar(rinv[:, :], rinv[:, :], 1e6, None,
                                        Op.min)
                        sV.inc(v.last)
                    rinvdone = sV.n

                    nchunks = 4 if STAGE >= 2 else 1
                    if c + 1 < nchunks:
                        lnq_ready_next = emit_ln_chunk(c + 1)

                    for tl in range(4):
                        j = 4 * c + tl
                        pb_q, pb_k = banks[2], banks[3]

                        @blk.tensor
                        def _(t, tl=tl, first=(tl == 0)):
                            t = TrackEng(t)
                            if first:
                                t.wait_ge(sV.h, rope_done)
                                t.wait_ge(dW.h, consts_W)
                            take(t, bk(pb_q), bk(pb_k))
                            for h in range(HL):
                                t.matmul(pb_q[:, h * 64:h * 64 + 64],
                                         qf[:, h * 512 + tl * 128:
                                            h * 512 + tl * 128 + 128],
                                         rt[:, :], start=True, stop=True)
                            for h in range(HKV):
                                t.matmul(pb_k[:, h * 64:h * 64 + 64],
                                         kf[:, h * 512 + tl * 128:
                                            h * 512 + tl * 128 + 128],
                                         rt[:, :], start=True, stop=True)
                            sT.inc(t.last)
                        pdone = sT.n

                        @blk.vector
                        def _(v, tl=tl, pdone=pdone):
                            v = TrackEng(v)
                            v.wait_ge(sT.h, pdone)
                            v.wait_ge(sV.h, rinvdone)
                            take(v, ("pwrp",))
                            for h in range(16):
                                bank = pb_q if h < 8 else pb_k
                                v.tensor_scalar(
                                    pbuf[:, h * 64:(h + 1) * 64],
                                    bank[:, (h % 8) * 64:(h % 8) * 64 + 64],
                                    rinv[:, tl * 16 + h:tl * 16 + h + 1],
                                    None, Op.mult)
                            kick(v, sV)
                            v.tensor_scalar(wrp[:, :], pbuf[:, :], INV2PI,
                                            MAGIC, Op.mult, Op.add)
                            v.tensor_scalar(wrp[:, :], wrp[:, :], MAGIC,
                                            None, Op.subtract)
                            v.scalar_tensor_tensor(wrp[:, :], wrp[:, :],
                                                   -TWOPI, pbuf[:, :],
                                                   Op.mult, Op.add)
                            sV.inc(v.last)
                        wdone = sV.n
                        put(bk(pb_q), sV)
                        put(bk(pb_k), sV)
                        if tl == 3:
                            put(("rinv",), sV)

                        @blk.vector
                        def _(v, wdone=wdone):
                            v = TrackEng(v)
                            v.scalar_tensor_tensor(pbuf[:, :], wrp[:, :],
                                                   -1.0, wrp[:, :],
                                                   Op.mult, Op.max)
                            sV.inc(v.last)
                        absdone = sV.n

                        @blk.scalar
                        def _(sc, absdone=absdone):
                            sc = TrackEng(sc)
                            sc.wait_ge(sV.h, absdone)
                            take(sc, ("qpkp",))
                            qs = qpk[:, :].rearrange("p (h w) -> p h w",
                                                     h=16, w=128)
                            wq_ = wrp[:, :].rearrange(
                                "p (h w) -> p h w", h=16, w=64)
                            aq = pbuf[:, :].rearrange(
                                "p (h w) -> p h w", h=16, w=64)
                            sc.activation(qs[:, :, 0:64], wq_, AF.Sin)
                            sc.activation(qs[:, :, 64:128], aq, AF.Sin,
                                          scale=-1.0, bias=halfpi[:, :])
                            sA.inc(sc.last)
                        featdone = sA.n
                        put(("pwrp",), sA)

                        term_v = {}

                        @blk.vector
                        def _(v, tl=tl, featdone=featdone):
                            v = TrackEng(v)
                            v.wait_ge(sA.h, featdone)
                            kp3 = qpk[:, 1024:2048].rearrange(
                                "p (a b) -> p a b", a=8, b=128)
                            t1k = term[:, :].rearrange("p (a b) -> p a b",
                                                       a=8, b=128)
                            for h in range(HL):
                                qph = qpk[:, h * 128:(h + 1) * 128]
                                v.tensor_tensor(
                                    t1k, kp3,
                                    qph.unsqueeze(1).to_broadcast(
                                        [128, 8, 128]), Op.mult)
                                v.tensor_reduce(sred[:, h * 8:(h + 1) * 8],
                                                t1k, AX.X, Op.add)
                            kick(v, sV)
                            v.tensor_scalar(sred[:, :], sred[:, :], SFRAC,
                                            None, Op.mult)
                            kick(v, sV)
                            s3 = sred[:, :].rearrange("p (a b) -> p a b",
                                                      a=HL, b=HKV)
                            for hp in range(HKV):
                                tb = term2[hp % 2]
                                t3 = tb[:, :].rearrange("p (a b) -> p a b",
                                                        a=HL, b=128)
                                vk = vtm[:, tl * 1024 + hp * 128:
                                         tl * 1024 + hp * 128 + 128]
                                v.tensor_tensor(
                                    t3,
                                    vk.unsqueeze(1).to_broadcast(
                                        [128, HL, 128]),
                                    s3[:, :, hp:hp + 1].to_broadcast(
                                        [128, HL, 128]), Op.mult)
                                if hp == 0:
                                    take(v, ("accb",))
                                    v.tensor_copy(acc[:, :], tb[:, :])
                                else:
                                    v.tensor_tensor(acc[:, :], acc[:, :],
                                                    tb[:, :], Op.add)
                            sV.inc(v.last)
                        put(("qpkp",), sV)
                        accdone = sV.n

                        pbx = psbf[tl % 2]

                        @blk.tensor
                        def _(t, pbx=pbx, tl=tl, accdone=accdone):
                            t = TrackEng(t)
                            t.wait_ge(sV.h, accdone)
                            take(t, ("psbf", tl % 2))
                            for h in range(HL):
                                t.transpose(pbx[:, h * 128:(h + 1) * 128],
                                            acc[:, h * 128:(h + 1) * 128],
                                            ident[:, :])
                            sT.inc(t.last)
                        tpd = sT.n
                        put(("accb",), sT)

                        @blk.vector
                        def _(v, pbx=pbx, j=j, tpd=tpd):
                            v = TrackEng(v)
                            v.wait_ge(sT.h, tpd)
                            base = afm[:, :].rearrange("p (h jj w) -> p h jj w",
                                                       h=HL, jj=16, w=128)
                            v.tensor_copy(
                                base[:, :, j, :],
                                pbx[:, :].rearrange("p (h w) -> p h w",
                                                    h=HL, w=128))
                            sV.inc(v.last)
                        put(("psbf", tl % 2), sV)

                        if DEBUG and c == 0 and tl == 0:
                            dbg_done = sV.n

                            @blk.gpsimd
                            def _(g):
                                g.wait_ge(sV.h, dbg_done)
                                dW.inc(g.dma_start(dbg["d_rinv"],
                                                   rinv[:, 0:64]), 16)
                                dW.inc(g.dma_start(dbg["d_wrp"], wrp[:, :]),
                                       16)
                                dW.inc(g.dma_start(dbg["d_abs"], pbuf[:, :]),
                                       16)
                                dW.inc(g.dma_start(dbg["d_qp"], qp[:, :]), 16)
                                dW.inc(g.dma_start(dbg["d_kp"], kp[:, :]), 16)
                                dW.inc(g.dma_start(dbg["d_sred"],
                                                   sred[:, :]), 16)
                                dW.inc(g.dma_start(dbg["d_acc"], acc[:, :]),
                                       16)
                                dW.inc(g.dma_start(dbg["d_qf"],
                                                   qf[:, 0:4096]), 16)
                                dW.inc(g.dma_start(dbg["d_kf"],
                                                   kf[:, 0:4096]), 16)
                                dW.inc(g.dma_start(dbg["d_vtm"],
                                                   vtm[:, 0:1024]), 16)
                                dW.inc(g.dma_start(dbg["d_lnq"],
                                                   lnq[:, 0:2048]), 16)
                            dbg_W = dW.n

                            @blk.vector
                            def _(v):
                                v = TrackEng(v)
                                v.wait_ge(dW.h, dbg_W)
                afm_ready = sV.n

            stkA.__exit__(None, None, None)
            otm4 = sbt(stkAb, "otm4", [128, 4 * D])
            wpC = sbt(stkAb, "wpC", [128, NKC * 256], BF16)
            wpD = sbt(stkAb, "wpD", [128, NKC * 256], BF16)
            bob = sbt(stkAb, "bob_sb", [128, D], BF16)
            wp4 = [wpA, wpB, wpC, wpD]

            with nc.Block() as blk:
                @blk.gpsimd
                def _(g):
                    g = TrackEng(g)
                    dW.inc(g.dma_start(bob[:, :], bob_in[:, :]), 16)
                bob_W = dW.n

                @blk.sync
                def _(s):
                    s = TrackEng(s)
                    for tt in range(8):
                        dX.inc(s.dma_start(xres[:, tt * D:(tt + 1) * D],
                                           xr_in[tt * 128:(tt + 1) * 128, :]),
                               16)
                x_loaded2 = dX.n
                globals()["_XL2"] = x_loaded2
                wo_li = 0

                def emit_wo_load(oc, s2b, woready):
                    nonlocal wo_li
                    buf = wp4[wo_li % 4]

                    @blk.gpsimd
                    def _(g, buf=buf, oc=oc, s2b=s2b, li=wo_li):
                        g = TrackEng(g)
                        take(g, ("wp4", li % 4))
                        dW.inc(load_pair(g, buf, wot,
                                         oc * 512 + s2b * 256), 16)
                    woready[(oc, s2b)] = (dW.n, wo_li % 4)
                    wo_li += 1

                for rh in range(2 if STAGE >= 2 else 0):
                    woready = {}
                    for oc0 in (0, 1):
                        for s2b in range(2):
                            emit_wo_load(oc0, s2b, woready)

                    for oc in range(4):
                        if oc >= 2 and rh == 0:
                            pass
                        for sub in range(4):
                            r = 4 * rh + sub
                            bank = banks[(oc * 4 + sub) % 2]

                            @blk.tensor
                            def _(t, r=r, sub=sub, oc=oc, bank=bank,
                                  first=(rh == 0 and oc == 0 and sub == 0)):
                                t = TrackEng(t)
                                if first:
                                    t.wait_ge(sV.h, afm_ready)
                                take(t, bk(bank))
                                for s2b in range(2):
                                    wr, bi = woready[(oc, s2b)]
                                    t.wait_ge(dW.h, wr)
                                    buf = wp4[bi]
                                    for kc in range(16):
                                        t.matmul(
                                            bank[:, s2b * 256:
                                                 s2b * 256 + 256],
                                            afm[:, r * 2048 + kc * 128:
                                                r * 2048 + kc * 128 + 128],
                                            buf[:, kc * 256:(kc + 1) * 256],
                                            start=(kc == 0), stop=(kc == 15))
                                sT.inc(t.last)
                            mmdone = sT.n
                            if sub == 3:
                                put(("wp4", woready[(oc, 0)][1]), sT)
                                put(("wp4", woready[(oc, 1)][1]), sT)

                            @blk.vector
                            def _(v, sub=sub, oc=oc, bank=bank,
                                  mmdone=mmdone):
                                v = TrackEng(v)
                                v.wait_ge(sT.h, mmdone)
                                v.wait_ge(dW.h, bob_W)
                                v.tensor_tensor(
                                    otm4[:, sub * D + oc * 512:
                                         sub * D + oc * 512 + 512],
                                    bank[:, :],
                                    bob[:, oc * 512:(oc + 1) * 512], Op.add)
                                sV.inc(v.last)
                            put(bk(bank), sV)
                        if oc < 2:
                            for s2b in range(2):
                                emit_wo_load(oc + 2, s2b, woready)
                    wo_done = sV.n

                    for sub in range(4):
                        r = 4 * rh + sub
                        oap = otm4[:, sub * D:(sub + 1) * D]
                        ln_stats(blk, oap,
                                 [(sV, wo_done), (dX, x_loaded2)]
                                 if sub == 0 else [])

                        @blk.vector
                        def _(v, oap=oap, r=r):
                            v = TrackEng(v)
                            take(v, ("lntmp",))
                            v.tensor_scalar(lntmp[:, :], oap, mv2[:, 0:1],
                                            colA[:, :], Op.subtract, Op.mult)
                            v.tensor_tensor(xres[:, r * D:(r + 1) * D],
                                            xres[:, r * D:(r + 1) * D],
                                            lntmp[:, :], Op.add)
                            sV.inc(v.last)
                        put(("lntmp",), sV)
        attn_done = sV.n

        if STAGE < 5:
            with nc.Block() as blk:
                @blk.sync
                def _(s):
                    s.wait_ge(sV.h, sV.n)
                    for tt in range(8):
                        dO.inc(s.dma_start(
                            out_hbm[tt * 128:(tt + 1) * 128, :],
                            xres[:, tt * D:(tt + 1) * D]), 16)
                    s.wait_ge(dO.h, dO.n)

        stkY = ExitStack()
        ylb = sbt(stkY, "ylb", [128, 8 * P], BF16)
        stkE = ExitStack()
        sel32 = sbt(stkE, "sel32_s", [32, NE * 128], BF16)
        cwfm = sbt(stkE, "cwfm", [32, P], BF16)
        stkX = ExitStack()
        xlat = sbt(stkX, "xlat", [128, 8 * P], BF16)
        ysb = sbt(stkX, "ysb", [128, 8 * P])

        gate_ready = cwfm_ready = xlat_ready = 0

        stkL = ExitStack()
        with stkL:
            lnf = sbt(stkL, "lnf2", [128, NKC * P], BF16)
            wpAf = sbt(stkL, "wpAf", [128, NKC * 256], BF16)
            wpBf = sbt(stkL, "wpBf", [128, NKC * 256], BF16)
            gateb = sbt(stkL, "gateb", [128, NKC * NE], BF16)
            scs = sbt(stkL, "scs", [128, NE])
            sbb = sbt(stkL, "sbb", [128, NE])
            g8 = sbt(stkL, "g8", [128, 4 * 8])
            gpad = sbt(stkL, "gpad", [128, 12])
            gm8 = sbt(stkL, "gm8", [128, 8])
            repl8 = sbt(stkL, "repl8", [128, 8])
            gzap = sbt(stkL, "gzap", [128, 12])
            gmask = sbt(stkL, "gmask", [128, 4])
            msb = sbt(stkL, "msb", [128, NE])
            mzap = sbt(stkL, "mzap", [128, NE])
            selm = sbt(stkL, "selm", [128, NE])
            t8v = sbt(stkL, "t8v", [128, 8])
            cwb16 = sbt(stkL, "cwb16", [128, NE], BF16)

            with nc.Block() as blk:
                @blk.gpsimd
                def _(g):
                    g = TrackEng(g)
                    ap = gatet_in[:, :].rearrange("(kc p) o -> p kc o", p=128)
                    dW.inc(g.dma_start(gateb[:, :], ap), 16)
                    dW.inc(g.dma_start(sel32[:, :], sel32_in[:, :]), 16)
                gate_ready = dW.n

                for tt in range(8 if STAGE >= 3 else 0):
                    xap = xres[:, tt * D:(tt + 1) * D]
                    ln_stats(blk, xap,
                             [(sV, attn_done)] if tt == 0 else [])

                    @blk.vector
                    def _(v, xap=xap):
                        v = TrackEng(v)
                        take(v, ("lntmp",))
                        v.tensor_scalar(lntmp[:, :], xap, mv2[:, 0:1],
                                        colA[:, :], Op.subtract, Op.mult)
                        sV.inc(v.last)
                    lnv = sV.n

                    def lnf_dst(b, tt=tt):
                        base = lnf[:, :].rearrange("p (kc w) -> p kc w",
                                                   kc=16, w=P)
                        return base[:, b * 8:(b + 1) * 8,
                                    tt * 128:(tt + 1) * 128]
                    transpose_fm(blk, lntmp, lnf_dst, pe_waits=[(sV, lnv)])
                    put(("lntmp",), sT)
                ln2_ready = sV.n

                for tt in range(8 if STAGE >= 3 else 0):
                    bank = banks[tt % 2]

                    @blk.tensor
                    def _(t, tt=tt, bank=bank):
                        t = TrackEng(t)
                        t.wait_ge(dW.h, gate_ready)
                        if tt == 0:
                            t.wait_ge(sV.h, ln2_ready)
                        take(t, bk(bank))
                        for kc in range(16):
                            t.matmul(bank[:, 0:NE],
                                     lnf[:, kc * P + tt * 128:
                                         kc * P + tt * 128 + 128],
                                     gateb[:, kc * NE:(kc + 1) * NE],
                                     start=(kc == 0), stop=(kc == 15))
                        sT.inc(t.last)
                    gmm = sT.n

                    @blk.scalar
                    def _(sc, bank=bank, gmm=gmm):
                        sc = TrackEng(sc)
                        sc.wait_ge(sT.h, gmm)
                        take(sc, ("gatevec",))
                        sc.activation(scs[:, :], bank[:, 0:NE], AF.Sigmoid)
                        sA.inc(sc.last)
                    sigd = sA.n
                    put(bk(bank), sA)

                    @blk.vector
                    def _(v, sigd=sigd):
                        v = TrackEng(v)
                        v.wait_ge(sA.h, sigd)
                        v.tensor_tensor(sbb[:, :], scs[:, :], gbb[:, :],
                                        Op.add)
                        kick(v, sV)
                        for gi in range(4):
                            v.max(out=g8[:, gi * 8:(gi + 1) * 8],
                                  in_=sbb[:, gi * 8:(gi + 1) * 8])
                        kick(v, sV)
                        g3 = g8[:, :].rearrange("p (a b) -> p a b", a=4, b=8)
                        v.tensor_tensor(gpad[:, 0:4], g3[:, :, 0],
                                        g3[:, :, 1], Op.add)
                        v.memset(gpad[:, 4:12], NEG)
                        kick(v, sV)
                        v.max(out=gm8[:, :], in_=gpad[:, :])
                        kick(v, sV)
                        v.tensor_copy(repl8[:, 0:2], gm8[:, 0:2])
                        v.memset(repl8[:, 2:8], NEG)
                        kick(v, sV)
                        v.match_replace(out=gzap[:, :],
                                        in_to_replace=repl8[:, :],
                                        in_values=gpad[:, :], imm_value=NEG)
                        kick(v, sV)
                        v.tensor_tensor(gmask[:, :], gpad[:, 0:4],
                                        gzap[:, 0:4], Op.not_equal)
                        kick(v, sV)
                        v.tensor_scalar(gmask[:, :], gmask[:, :], 1.0, 1e30,
                                        Op.subtract, Op.mult)
                        kick(v, sV)
                        m3 = msb[:, :].rearrange("p (a b) -> p a b", a=4, b=8)
                        s3 = sbb[:, :].rearrange("p (a b) -> p a b", a=4, b=8)
                        v.tensor_tensor(m3, s3,
                                        gmask[:, :].unsqueeze(2)
                                        .to_broadcast([128, 4, 8]), Op.add)
                        kick(v, sV)
                        v.max(out=t8v[:, :], in_=msb[:, :])
                        kick(v, sV)
                        v.match_replace(out=mzap[:, :],
                                        in_to_replace=t8v[:, :],
                                        in_values=msb[:, :], imm_value=NEG2)
                        kick(v, sV)
                        v.tensor_tensor(selm[:, :], msb[:, :], mzap[:, :],
                                        Op.not_equal)
                        kick(v, sV)
                        v.tensor_tensor(selm[:, :], scs[:, :], selm[:, :],
                                        Op.mult)
                        kick(v, sV)
                        v.tensor_reduce(colA[:, :], selm[:, :], AX.X, Op.add)
                        kick(v, sV)
                        v.reciprocal(colA[:, :], colA[:, :])
                        kick(v, sV)
                        v.tensor_scalar(cwb16[:, :], selm[:, :], colA[:, :],
                                        2.5, Op.mult, Op.mult)
                        sV.inc(v.last)
                    cwd = sV.n
                    put(("gatevec",), sV)

                    @blk.tensor
                    def _(t, cwd=cwd):
                        t = TrackEng(t)
                        take(t, ("psbf", 0))
                        t.wait_ge(sV.h, cwd)
                        t.transpose(psbf[0][0:32, 0:128], cwb16[:, :],
                                    ident[:, :])
                        sT.inc(t.last)
                    cwt = sT.n

                    @blk.vector
                    def _(v, tt=tt, cwt=cwt):
                        v = TrackEng(v)
                        v.wait_ge(sT.h, cwt)
                        v.tensor_copy(cwfm[:, tt * 128:(tt + 1) * 128],
                                      psbf[0][0:32, 0:128])
                        sV.inc(v.last)
                    put(("psbf", 0), sV)
                cwfm_ready = sV.n

                for lc2 in range(4 if STAGE >= 3 else 0):
                    buf = [wpAf, wpBf][lc2 % 2]

                    @blk.gpsimd
                    def _(g, buf=buf, lc2=lc2):
                        g = TrackEng(g)
                        take(g, ("wpf", lc2 % 2))
                        dW.inc(load_pair(g, buf, fc1t, lc2 * 256), 16)
                    fcw = dW.n

                    for sub in range(2):
                        lc = lc2 * 2 + sub
                        for tc in range(2):
                            bank = banks[(lc * 2 + tc) % 2]

                            @blk.tensor
                            def _(t, buf=buf, sub=sub, tc=tc, bank=bank,
                                  fcw=fcw):
                                t = TrackEng(t)
                                t.wait_ge(dW.h, fcw)
                                take(t, bk(bank))
                                for kc in range(16):
                                    t.matmul(
                                        bank[:, :],
                                        buf[:, kc * 256 + sub * 128:
                                            kc * 256 + sub * 128 + 128],
                                        lnf[:, kc * P + tc * 512:
                                            kc * P + tc * 512 + 512],
                                        start=(kc == 0), stop=(kc == 15))
                                sT.inc(t.last)
                            mmd = sT.n
                            if sub == 1 and tc == 1:
                                put(("wpf", lc2 % 2), sT)

                            @blk.vector
                            def _(v, lc=lc, tc=tc, bank=bank, mmd=mmd):
                                v = TrackEng(v)
                                v.wait_ge(sT.h, mmd)
                                v.tensor_copy(
                                    xlat[:, lc * P + tc * 512:
                                         lc * P + tc * 512 + 512],
                                    bank[:, :])
                                sV.inc(v.last)
                            put(bk(bank), sV)
                xlat_ready = sV.n

                @blk.vector
                def _(v):
                    v = TrackEng(v)
                    v.memset(ysb[:, :], 0.0)
                    sV.inc(v.last)

        stkW = ExitStack()
        with stkW:
            w1b = [sbt(stkW, f"w1_{i}", [128, 8 * 512], BF16)
                   for i in range(2)]
            w2b = [sbt(stkW, f"w2_{i}", [128, 8 * 512], BF16)
                   for i in range(2)]
            he = sbt(stkW, "he", [128, 8 * 1024], BF16)
            gsb = sbt(stkW, "gsb", [128, 512], BF16)
            usb = sbt(stkW, "usb", [128, 512], BF16)
            cwsb = sbt(stkW, "cwsb", [128, 1024], BF16)

            def load_w1(g, buf, e, ih):
                ap = w1t[e, :, ih * 512:(ih + 1) * 512].rearrange(
                    "(lc p) i -> p lc i", p=128)
                return g.dma_start(buf[:, :], ap)

            def load_w2(g, buf, e, lh):
                ap = w2t[e, :, lh * 512:(lh + 1) * 512].rearrange(
                    "(ic p) l -> p ic l", p=128)
                return g.dma_start(buf[:, :], ap)

            with nc.Block() as blk:
                for e in range(NE if STAGE >= 4 else 0):
                    w1r = [0, 0]
                    for ih in range(2):
                        @blk.gpsimd
                        def _(g, ih=ih, e=e):
                            g = TrackEng(g)
                            take(g, ("w1", ih))
                            dW.inc(load_w1(g, w1b[ih], e, ih), 16)
                        w1r[ih] = dW.n

                    cb = banks[4]
                    for tc in range(2):
                        @blk.tensor
                        def _(t, e=e, tc=tc, first=(e == 0 and tc == 0)):
                            t = TrackEng(t)
                            if first:
                                t.wait_ge(sV.h, cwfm_ready)
                                t.wait_ge(dW.h, gate_ready)
                            take(t, bk(cb))
                            t.matmul(cb[:, :],
                                     sel32[:, e * 128:(e + 1) * 128],
                                     cwfm[:, tc * 512:(tc + 1) * 512],
                                     start=True, stop=True)
                            sT.inc(t.last)
                        cbd = sT.n

                        @blk.vector
                        def _(v, tc=tc, cbd=cbd):
                            v = TrackEng(v)
                            v.wait_ge(sT.h, cbd)
                            v.tensor_copy(cwsb[:, tc * 512:(tc + 1) * 512],
                                          banks[4][:, :])
                            sV.inc(v.last)
                        put(bk(cb), sV)
                    cwr = sV.n

                    for ih in range(2):
                        for icl in range(4):
                            ic = ih * 4 + icl
                            for tc in range(2):
                                bank = banks[(2 * ic + tc) % 2]

                                @blk.tensor
                                def _(t, ih=ih, icl=icl, tc=tc, bank=bank,
                                      first=(e == 0 and ih == 0
                                             and icl == 0 and tc == 0)):
                                    t = TrackEng(t)
                                    t.wait_ge(dW.h, w1r[ih])
                                    if first:
                                        t.wait_ge(sV.h, xlat_ready)
                                    take(t, bk(bank))
                                    for lc in range(8):
                                        t.matmul(
                                            bank[:, :],
                                            w1b[ih][:, lc * 512 + icl * 128:
                                                    lc * 512 + icl * 128
                                                    + 128],
                                            xlat[:, lc * P + tc * 512:
                                                 lc * P + tc * 512 + 512],
                                            start=(lc == 0), stop=(lc == 7))
                                    sT.inc(t.last)
                                hmm_ = sT.n
                                if icl == 3 and tc == 1:
                                    put(("w1", ih), sT)

                                @blk.vector
                                def _(v, ic=ic, tc=tc, bank=bank, hmm_=hmm_,
                                      cwr=cwr,
                                      firsth=(ih == 0 and icl == 0
                                              and tc == 0)):
                                    v = TrackEng(v)
                                    v.wait_ge(sT.h, hmm_)
                                    if firsth:
                                        v.wait_ge(sV.h, cwr)
                                        take(v, ("he",))
                                    v.tensor_scalar(gsb[:, :], bank[:, :],
                                                    0.0, None, Op.max)
                                    v.tensor_tensor(
                                        usb[:, :], gsb[:, :],
                                        cwsb[:, tc * 512:(tc + 1) * 512],
                                        Op.mult)
                                    v.tensor_tensor(
                                        he[:, ic * 1024 + tc * 512:
                                           ic * 1024 + tc * 512 + 512],
                                        usb[:, :], gsb[:, :], Op.mult)
                                    sV.inc(v.last)
                                put(bk(bank), sV)
                    hdone = sV.n

                    w2r = [0, 0]
                    for lh in range(2):
                        @blk.gpsimd
                        def _(g, lh=lh, e=e):
                            g = TrackEng(g)
                            take(g, ("w2", lh))
                            dW.inc(load_w2(g, w2b[lh], e, lh), 16)
                        w2r[lh] = dW.n

                    for tc in range(2):
                        for lc in range(8):
                            lh, lcl = lc // 4, lc % 4
                            bank = banks[2 + (lc + tc) % 2]

                            @blk.tensor
                            def _(t, tc=tc, lc=lc, lh=lh, lcl=lcl, bank=bank,
                                  first=(tc == 0 and lc == 0)):
                                t = TrackEng(t)
                                t.wait_ge(dW.h, w2r[lh])
                                if first:
                                    t.wait_ge(sV.h, hdone)
                                take(t, bk(bank))
                                for ic in range(8):
                                    t.matmul(
                                        bank[:, :],
                                        w2b[lh][:, ic * 512 + lcl * 128:
                                                ic * 512 + lcl * 128 + 128],
                                        he[:, ic * 1024 + tc * 512:
                                           ic * 1024 + tc * 512 + 512],
                                        start=(ic == 0), stop=(ic == 7))
                                sT.inc(t.last)
                            ymm = sT.n
                            if tc == 1 and lc == 3:
                                put(("w2", 0), sT)
                            if tc == 1 and lc == 7:
                                put(("w2", 1), sT)
                                put(("he",), sT)

                            @blk.vector
                            def _(v, tc=tc, lc=lc, bank=bank, ymm=ymm):
                                v = TrackEng(v)
                                v.wait_ge(sT.h, ymm)
                                yap = ysb[:, lc * P + tc * 512:
                                          lc * P + tc * 512 + 512]
                                v.tensor_tensor(yap, yap, bank[:, :], Op.add)
                                sV.inc(v.last)
                            put(bk(bank), sV)
        y_done = sV.n

        with nc.Block() as blk:
            @blk.vector
            def _(v):
                v = TrackEng(v)
                for i in range(8):
                    v.tensor_copy(ylb[:, i * P:(i + 1) * P],
                                  ysb[:, i * P:(i + 1) * P])
                sV.inc(v.last)
        ylb_ready = sV.n
        stkX.close()
        stkE.close()

        stkS = ExitStack()
        with stkS:
            shfm = sbt(stkS, "shfm", [128, 32 * 512], BF16)
            dwn3 = [sbt(stkS, f"dwn{i}", [128, 8 * 256], BF16)
                    for i in range(4)]
            fc2b = sbt(stkS, "fc2b", [128, 8 * 256], BF16)
            otmh = sbt(stkS, "otmh", [128, 4 * D])

            if True:
                for half in range(2 if STAGE >= 5 else 0):
                    stkH = ExitStack()
                    lnh2 = sbt(stkH, f"lnh2_{half}", [128, NKC * 512], BF16)
                    upwb = [sbt(stkH, f"upw{half}_{i}", [128, NKC * 128],
                                BF16) for i in range(2)]
                    gsh = sbt(stkH, f"gsh{half}", [128, 512], BF16)
                    blk_cm = nc.Block()
                    blk = blk_cm.__enter__()
                    for tl in range(4):
                        tt = half * 4 + tl
                        xap = xres[:, tt * D:(tt + 1) * D]
                        ln_stats(blk, xap)

                        @blk.vector
                        def _(v, xap=xap):
                            v = TrackEng(v)
                            take(v, ("lntmp",))
                            v.tensor_scalar(lntmp[:, :], xap, mv2[:, 0:1],
                                            colA[:, :], Op.subtract, Op.mult)
                            sV.inc(v.last)
                        lnv = sV.n

                        def lnh_dst(b, tl=tl):
                            base = lnh2[:, :].rearrange("p (kc w) -> p kc w",
                                                        kc=16, w=512)
                            return base[:, b * 8:(b + 1) * 8,
                                        tl * 128:(tl + 1) * 128]
                        transpose_fm(blk, lntmp, lnh_dst,
                                     pe_waits=[(sV, lnv)])
                        put(("lntmp",), sT)
                    lnh2_ready = sV.n

                    for icg in range(32):
                        upw = upwb[icg % 2]

                        @blk.gpsimd
                        def _(g, icg=icg, upw=upw):
                            g = TrackEng(g)
                            take(g, ("upw", icg % 2))
                            dW.inc(load_pair(g, upw, upt, icg * 128,
                                             width=128), 16)
                        upr = dW.n

                        if True:
                            bank = banks[icg % 2]

                            @blk.tensor
                            def _(t, bank=bank, upr=upr, upw=upw,
                                  first=(icg == 0)):
                                t = TrackEng(t)
                                t.wait_ge(dW.h, upr)
                                if first:
                                    t.wait_ge(sV.h, lnh2_ready)
                                take(t, bk(bank))
                                for kc in range(16):
                                    t.matmul(
                                        bank[:, :],
                                        upw[:, kc * 128:(kc + 1) * 128],
                                        lnh2[:, kc * 512:(kc + 1) * 512],
                                        start=(kc == 0), stop=(kc == 15))
                                sT.inc(t.last)
                            umm = sT.n
                            put(("upw", icg % 2), sT)

                            @blk.vector
                            def _(v, icg=icg, bank=bank, umm=umm):
                                v = TrackEng(v)
                                v.wait_ge(sT.h, umm)
                                take(v, ("gsh",))
                                v.tensor_scalar(gsh[:, :], bank[:, :],
                                                0.0, None, Op.max)
                                v.tensor_tensor(
                                    shfm[:, icg * 512:(icg + 1) * 512],
                                    gsh[:, :], gsh[:, :], Op.mult)
                                sV.inc(v.last)
                            put(bk(bank), sV)
                            put(("gsh",), sV)
                    sh_ready = sV.n

                    blk_cm.__exit__(None, None, None)
                    stkH.close()
                    blk_cm = nc.Block()
                    blk = blk_cm.__enter__()

                    for oc in range(4):
                        for s2b in range(2):
                            col0 = oc * 512 + s2b * 256
                            step = oc * 2 + s2b
                            dhalf = []
                            for h2 in range(4):
                                bi = (half * 32 + step * 4 + h2) % 4
                                buf = dwn3[bi]

                                @blk.gpsimd
                                def _(g, col0=col0, h2=h2, bi=bi, buf=buf):
                                    g = TrackEng(g)
                                    take(g, ("dwn3", bi))
                                    ap = downt[h2 * 1024:(h2 + 1) * 1024,
                                               col0:col0 + 256].rearrange(
                                        "(ic p) o -> p ic o", p=128)
                                    dW.inc(g.dma_start(buf[:, :], ap), 16)
                                dhalf.append((bi, dW.n))

                            @blk.gpsimd
                            def _(g, col0=col0, fc2b=fc2b):
                                g = TrackEng(g)
                                take(g, ("fc2b",))
                                apf = fc2t[:, col0:col0 + 256].rearrange(
                                    "(lc p) o -> p lc o", p=128)
                                dW.inc(g.dma_start(fc2b[:, :], apf), 16)
                            fcr = dW.n

                            for tl in range(4):
                                tt = half * 4 + tl
                                bank = banks[2 + tl % 2]

                                @blk.tensor
                                def _(t, tl=tl, tt=tt, bank=bank,
                                      dhalf=tuple(dhalf), fcr=fcr, fc2b=fc2b,
                                      first=(oc == 0 and s2b == 0
                                             and tl == 0)):
                                    t = TrackEng(t)
                                    if first:
                                        t.wait_ge(sV.h, sh_ready)
                                        t.wait_ge(sV.h, ylb_ready)
                                    for h2 in range(4):
                                        bi, dwr = dhalf[h2]
                                        t.wait_ge(dW.h, dwr)
                                        if h2 == 0:
                                            take(t, bk(bank))
                                        buf = dwn3[bi]
                                        for i2 in range(8):
                                            ic = h2 * 8 + i2
                                            t.matmul(
                                                bank[:, 0:256],
                                                shfm[:, ic * 512 + tl * 128:
                                                     ic * 512 + tl * 128
                                                     + 128],
                                                buf[:, i2 * 256:
                                                    (i2 + 1) * 256],
                                                start=(ic == 0), stop=False)
                                    t.wait_ge(dW.h, fcr)
                                    for lc in range(8):
                                        t.matmul(
                                            bank[:, 0:256],
                                            ylb[:, lc * P + tt * 128:
                                                lc * P + tt * 128 + 128],
                                            fc2b[:, lc * 256:(lc + 1) * 256],
                                            start=False, stop=(lc == 7))
                                    sT.inc(t.last)
                                dmm = sT.n
                                if tl == 3:
                                    for _bi, _ in dhalf:
                                        put(("dwn3", _bi), sT)
                                    put(("fc2b",), sT)

                                @blk.vector
                                def _(v, tl=tl, oc=oc, s2b=s2b, bank=bank,
                                      dmm=dmm):
                                    v = TrackEng(v)
                                    v.wait_ge(sT.h, dmm)
                                    v.tensor_copy(
                                        otmh[:, tl * D + oc * 512
                                             + s2b * 256:
                                             tl * D + oc * 512
                                             + s2b * 256 + 256],
                                        bank[:, 0:256])
                                    sV.inc(v.last)
                                put(bk(bank), sV)
                    ffn_done = sV.n

                    for tl in range(4):
                        tt = half * 4 + tl
                        oap = otmh[:, tl * D:(tl + 1) * D]
                        ln_stats(blk, oap,
                                 [(sV, ffn_done)] if tl == 0 else [])

                        @blk.vector
                        def _(v, oap=oap, tt=tt):
                            v = TrackEng(v)
                            take(v, ("lntmp",))
                            v.tensor_scalar(lntmp[:, :], oap, mv2[:, 0:1],
                                            colA[:, :], Op.subtract, Op.mult)
                            v.tensor_tensor(xres[:, tt * D:(tt + 1) * D],
                                            xres[:, tt * D:(tt + 1) * D],
                                            lntmp[:, :], Op.add)
                            sV.inc(v.last)
                        put(("lntmp",), sV)
                        xdone = sV.n

                        @blk.sync
                        def _(s, tt=tt, xdone=xdone):
                            s = TrackEng(s)
                            s.wait_ge(sV.h, xdone)
                            dO.inc(s.dma_start(
                                out_hbm[tt * 128:(tt + 1) * 128, :],
                                xres[:, tt * D:(tt + 1) * D]), 16)

                    blk_cm.__exit__(None, None, None)

            if STAGE >= 5:
                with nc.Block() as blk:
                    @blk.sync
                    def _(s):
                        s = TrackEng(s)
                        s.wait_ge(dO.h, dO.n)

        stkY.close()
    return nc


_CACHE = {}


def kernel(**inputs):
    import ml_dtypes

    x = np.asarray(inputs["L0_attn_hidden_states"], dtype=np.float32)
    B = x.shape[0]
    assert x.shape == (B, S2, D) and B == 4

    f = np.ascontiguousarray
    wq = np.asarray(inputs["L0_attn_Wq"], np.float32)
    wk = np.asarray(inputs["L0_attn_Wk"], np.float32)
    wv = np.asarray(inputs["L0_attn_Wv"], np.float32)
    wo = np.asarray(inputs["L0_attn_Wo"], np.float32)
    bq = np.asarray(inputs["L0_attn_bq"], np.float32)
    bkb = np.asarray(inputs["L0_attn_bk"], np.float32)
    bv = np.asarray(inputs["L0_attn_bv"], np.float32)
    bo = np.asarray(inputs["L0_attn_bo"], np.float32)
    R = np.asarray(inputs["L0_attn_rand_matrix"], np.float32)
    gate_w = np.asarray(inputs["L0_ffn_gate_w"], np.float32)
    gate_b = np.asarray(inputs["L0_ffn_gate_bias"], np.float32)
    fc1 = np.asarray(inputs["L0_ffn_fc1_w"], np.float32)
    fc2 = np.asarray(inputs["L0_ffn_fc2_w"], np.float32)
    w1 = np.asarray(inputs["L0_ffn_w1"], np.float32)
    w2 = np.asarray(inputs["L0_ffn_w2"], np.float32)
    up = np.asarray(inputs["L0_ffn_shared_up_w"], np.float32)
    down = np.asarray(inputs["L0_ffn_shared_down_w"], np.float32)

    wqT = f(wq.T)
    shared = {
        "wkt": f(wk.T), "wvt": f(wv.T), "wot": f(wo.T),
        "rt": f(R.T),
        "gatet": f(gate_w.T), "fc1t": f(fc1.T), "fc2t": f(fc2.T),
        "w1t": f(w1.transpose(0, 2, 1)), "w2t": f(w2.transpose(0, 2, 1)),
        "upt": f(up.T), "downt": f(down.T),
        "bkf": f(bkb.reshape(HKV, 128).T),
        "bvb": f(np.broadcast_to(bv, (128, HKV * 128))),
        "bob": f(np.broadcast_to(bo, (128, D))),
        "gbb": f(np.broadcast_to(gate_b, (128, NE))),
    }
    bqf_full = f(bq.reshape(16, 128).T)

    ident = np.eye(128, dtype=ml_dtypes.bfloat16)
    sel = np.zeros((32, NE * 128), dtype=ml_dtypes.bfloat16)
    for e in range(NE):
        sel[e, e * 128:(e + 1) * 128] = 1
    ones1 = np.ones((128, 1), dtype=ml_dtypes.bfloat16)
    hd = 128
    t32 = np.arange(0, hd, 2).astype(np.float32) / np.float32(hd)
    inv_freq = (np.float32(1.0) / (np.float32(10000.0) ** t32)) \
        .astype(np.float32)
    pos = np.arange(S2, dtype=np.float32)
    freqs = (pos[:, None] * inv_freq[None, :]).astype(np.float32)
    emb = np.concatenate([freqs, freqs], axis=1)
    cos_t = np.cos(emb).astype(np.float32)
    sin_t = np.sin(emb).astype(np.float32)
    i_idx = np.arange(S2)
    tperm = 16 * (i_idx % 128) + (i_idx // 128)
    cosf = f(cos_t[tperm, :].T)
    sinf = f(sin_t[tperm, :].T)
    consts = {"ident": ident, "sel32": sel, "ones1": ones1,
              "cosf": cosf, "sinf": sinf}

    in_maps = []
    for core in range(8):
        b, hh = core // 2, core % 2
        m = dict(shared)
        m.update(consts)
        m["xfull"] = f(x[b])
        m["xr"] = f(x[b, hh * P:(hh + 1) * P, :])
        m["wqt"] = f(wqT[:, hh * 1024:(hh + 1) * 1024])
        m["bqf"] = f(bqf_full[:, hh * 8:(hh + 1) * 8])
        in_maps.append(m)

    if "nc" not in _CACHE:
        _CACHE["nc"] = build()
    nc = _CACHE["nc"]
    res = run_bass_kernel_spmd(nc, in_maps, core_ids=list(range(8)))
    out = np.empty((B, S2, D), dtype=np.float32)
    for core in range(8):
        b, hh = core // 2, core % 2
        out[b, hh * P:(hh + 1) * P, :] = res.results[core]["out"]
    return out


# version 12 (final)
# speedup vs baseline: 1.3120x; 1.0124x over previous
import os
import numpy as np
import concourse.bass as bass
import concourse.mybir as mybir
from bass_rust import BassInstruction as _BassInstruction
from concourse.bass_utils import run_bass_kernel_spmd
from concourse.alu_op_type import AluOpType as Op
from contextlib import ExitStack

F32 = mybir.dt.float32
BF16 = mybir.dt.bfloat16
AF = mybir.ActivationFunctionType
AX = mybir.AxisListType

PI = float(np.pi)
INV2PI = float(1.0 / (2 * np.pi))
TWOPI = float(2 * np.pi)
MAGIC = float(1.5 * 2 ** 23)
NEG = -1e30
NEG2 = -2e30

P = 1024
S2 = 2048
D = 2048
NKC = 16
HL = 8
HKV = 8
LAT = 1024
NE = 32
SH = 4096
EPS_LN = 1e-5
SFRAC = float(2.0 / 128.0)


class Sem:
    __slots__ = ("h", "n")

    def __init__(self, h):
        self.h = h
        self.n = 0

    def inc(self, inst, k=1):
        inst.then_inc(self.h, k)
        self.n += k
        return self.n


_NOTRACK = {"wait_ge", "nop", "sem_inc", "sem_clear", "print", "breakpoint",
            "maybe_wait_ge", "drain"}


class TrackEng:

    def __init__(self, e):
        object.__setattr__(self, "_e", e)
        object.__setattr__(self, "last", None)

    def __getattr__(self, n):
        a = getattr(object.__getattribute__(self, "_e"), n)
        if callable(a) and n not in _NOTRACK:
            def w(*args, **kw):
                r = a(*args, **kw)
                if isinstance(r, _BassInstruction):
                    object.__setattr__(self, "last", r)
                return r
            return w
        return a


STAGE = int(os.environ.get("MK_STAGE", "5"))
SUB = int(os.environ.get("MK_SUB", "9"))
DEBUG = int(os.environ.get("MK_DEBUG", "0"))


def build():
    nc = bass.Bass("TRN2", target_bir_lowering=False)

    def din(name, shape, dt=F32):
        return nc.dram_tensor(name, shape, dt, kind="ExternalInput").ap()

    xfull = din("xfull", [S2, D])
    xr_in = din("xr", [P, D])
    out_hbm = nc.dram_tensor("out", [P, D], F32, kind="ExternalOutput").ap()
    dbg = {}
    if DEBUG:
        for nm, w in [("d_rinv", 64), ("d_wrp", 1024), ("d_abs", 1024),
                      ("d_qp", 1024), ("d_kp", 1024), ("d_sred", 64),
                      ("d_acc", 1024), ("d_qf", 4096), ("d_kf", 4096),
                      ("d_vtm", 1024), ("d_lnq", 2048), ("d_n2", 64)]:
            dbg[nm] = nc.dram_tensor(nm, [128, w], F32,
                                     kind="ExternalOutput").ap()

    wqt = din("wqt", [D, HL * 128])
    wkt = din("wkt", [D, HKV * 128])
    wvt = din("wvt", [D, HKV * 128])
    wot = din("wot", [D, D])
    rt_in = din("rt", [128, 64])
    gatet_in = din("gatet", [D, NE]); fc1t = din("fc1t", [D, LAT])
    fc2t = din("fc2t", [LAT, D])
    w1t = din("w1t", [NE, LAT, LAT]); w2t = din("w2t", [NE, LAT, LAT])
    upt = din("upt", [D, SH]); downt = din("downt", [SH, D])
    bqf_in = din("bqf", [128, HL]); bkf_in = din("bkf", [128, HKV])
    bvb_in = din("bvb", [128, HKV * 128]); bob_in = din("bob", [128, D])
    gbb_in = din("gbb", [128, NE])
    cosf_in = din("cosf", [128, S2]); sinf_in = din("sinf", [128, S2])
    ident_in = din("ident", [128, 128], BF16)
    sel32_in = din("sel32", [32, NE * 128], BF16)
    ones1_in = din("ones1", [128, 1], BF16)

    top = ExitStack()

    def sbt(stack, name, shape, dt=F32):
        return stack.enter_context(nc.sbuf_tensor(name, shape, dt))

    with top:
        xres = sbt(top, "xres", [128, 8 * D])
        ident = sbt(top, "ident_s", [128, 128], BF16)
        ones1 = sbt(top, "ones1_s", [128, 1], BF16)
        stat = sbt(top, "stat", [128, 24])
        mv2 = sbt(top, "mv2", [128, 2])
        colA = sbt(top, "colA", [128, 1])
        lntmp = sbt(top, "lntmp", [128, D], BF16)
        halfpi = sbt(top, "halfpi", [128, 1])
        nhalf = sbt(top, "nhalf", [128, 64])
        bqf = sbt(top, "bqf_s", [128, HL])
        bkf = sbt(top, "bkf_s", [128, HKV])
        gbb = sbt(top, "gbb_s", [128, NE])

        banks = [top.enter_context(nc.psum_tensor(f"bk{i}", [128, 512], F32))
                 for i in range(6)]
        psbf = [top.enter_context(nc.psum_tensor(f"pb{i}", [128, 1024], BF16))
                for i in range(2)]

        dW = Sem(top.enter_context(nc.semaphore("dW")))
        dX = Sem(top.enter_context(nc.semaphore("dX")))
        dO = Sem(top.enter_context(nc.semaphore("dO")))
        sT = Sem(top.enter_context(nc.semaphore("sT")))
        sV = Sem(top.enter_context(nc.semaphore("sV")))
        sA = Sem(top.enter_context(nc.semaphore("sA")))
        sG = Sem(top.enter_context(nc.semaphore("sG")))

        guard = {}

        def take(eng, *keys):
            for key in keys:
                g = guard.pop(key, None)
                if g is not None:
                    eng.wait_ge(g[0].h, g[1])

        def put(key, sem):
            guard[key] = (sem, sem.n)

        def bk(b):
            return ("bank", id(b))

        def kick(eng, sem):
            sem.inc(eng.last)
            eng.wait_ge(sem.h, sem.n)

        with nc.Block() as blk:
            @blk.sync
            def _(s):
                s = TrackEng(s)
                dX.inc(s.dma_start(bqf[:, :], bqf_in[:, :]), 16)
                dX.inc(s.dma_start(bkf[:, :], bkf_in[:, :]), 16)
                dX.inc(s.dma_start(gbb[:, :], gbb_in[:, :]), 16)
                dX.inc(s.dma_start(ident[:, :], ident_in[:, :]), 16)
                dX.inc(s.dma_start(ones1[:, :], ones1_in[:, :]), 16)

            @blk.vector
            def _(v):
                v = TrackEng(v)
                v.memset(halfpi[:, :], PI / 2)
                v.memset(nhalf[:, :], -0.5)
                sV.inc(v.last)
        x_loaded = dX.n
        consts_v = sV.n

        def load_pair(g, buf, src, col0, width=256):
            ap = src[:, col0:col0 + width].rearrange("(kc p) o -> p kc o",
                                                     p=128)
            return g.dma_start(buf[:, 0:NKC * width], ap)

        def ln_stats(blk, src_ap, wait_sems=()):
            @blk.vector
            def _(v):
                v = TrackEng(v)
                for sem, val in wait_sems:
                    v.wait_ge(sem.h, val)
                for c in range(4):
                    v.bn_stats(stat[:, c * 6:(c + 1) * 6],
                               src_ap[:, c * 512:(c + 1) * 512])
                kick(v, sV)
                v.bn_aggr(mv2[:, :], stat[:, :])
                kick(v, sV)
                v.tensor_scalar(colA[:, :], mv2[:, 1:2], EPS_LN, None, Op.add)
                sV.inc(v.last)
            vst = sV.n

            @blk.gpsimd
            def _(g):
                g = TrackEng(g)
                g.wait_ge(sV.h, vst)
                g.wait_ge(sV.h, consts_v)
                g.tensor_tensor(colA[:, :], colA[:, :], nhalf[:, 0:1], Op.pow)
                sG.inc(g.last)
            gst = sG.n

            @blk.vector
            def _(v):
                v = TrackEng(v)
                v.wait_ge(sG.h, gst)

        def transpose_fm(blk, src, dst_ap_fn, pe_waits=()):
            @blk.tensor
            def _(t):
                t = TrackEng(t)
                for sem, val in pe_waits:
                    t.wait_ge(sem.h, val)
                take(t, ("psbf", 0), ("psbf", 1))
                for kc in range(16):
                    b, j = kc // 8, kc % 8
                    t.transpose(psbf[b][:, j * 128:(j + 1) * 128],
                                src[:, kc * 128:(kc + 1) * 128], ident[:, :])
                sT.inc(t.last)
            tdone = sT.n

            @blk.vector
            def _(v):
                v = TrackEng(v)
                v.wait_ge(sT.h, tdone)
                for b in range(2):
                    v.tensor_copy(dst_ap_fn(b),
                                  psbf[b][:, :].rearrange("p (j c) -> p j c",
                                                          j=8, c=128))
                sV.inc(v.last)
            put(("psbf", 0), sV)
            put(("psbf", 1), sV)
            return sV.n

        stkAb = ExitStack()
        with stkAb:
            wpA = sbt(stkAb, "wpA", [128, NKC * 256], BF16)
            wpB = sbt(stkAb, "wpB", [128, NKC * 256], BF16)
            afm = sbt(stkAb, "afm", [128, HL * 2048], BF16)
            stkA = ExitStack()
            stkA.__enter__()
            cosf = sbt(stkA, "cosf_sb", [128, 512], BF16)
            sinf = sbt(stkA, "sinf_sb", [128, 512], BF16)
            bvb = sbt(stkA, "bvb_sb", [128, HKV * 128], BF16)
            rt = sbt(stkA, "rt_sb", [128, 64], BF16)
            evtmp = sbt(stkA, "evt", [128, 512], BF16)
            rotb = sbt(stkA, "rot", [128, 512], BF16)
            xtile = sbt(stkA, "xtile", [128, D])
            lnq2 = [sbt(stkA, f"lnq{i}", [128, NKC * 512], BF16)
                    for i in range(2)]
            qf = sbt(stkA, "qf", [128, HL * 512], BF16)
            kf = sbt(stkA, "kf", [128, HKV * 512], BF16)
            vtm = sbt(stkA, "vtm", [128, 4 * HKV * 128], BF16)
            sqb = sbt(stkA, "sqb", [128, 512], BF16)
            rinv = sbt(stkA, "rinv", [128, 64])
            pbuf = sbt(stkA, "pbuf", [128, 1024])
            wrp = sbt(stkA, "wrp", [128, 1024])
            qpk = sbt(stkA, "qpk", [128, 16 * 128], BF16)
            sred = sbt(stkA, "sred", [128, HL * HKV])
            term2 = [sbt(stkA, f"term{i}", [128, HL * 128], BF16)
                     for i in range(2)]
            term = term2[0]
            acc = sbt(stkA, "acc", [128, HL * 128], BF16)

            with nc.Block() as blk:
                @blk.gpsimd
                def _(g):
                    g = TrackEng(g)
                    dW.inc(g.dma_start(bvb[:, :], bvb_in[:, :]), 16)
                    dW.inc(g.dma_start(rt[:, :], rt_in[:, :]), 16)
                consts_W = dW.n

                xperm = xfull.rearrange("(p j) d -> j p d", j=16)

                for c in range(4 if STAGE >= 2 else 1):
                    @blk.gpsimd
                    def _(g, c=c):
                        g = TrackEng(g)
                        take(g, ("trig",))
                        dW.inc(g.dma_start(cosf[:, :],
                                           cosf_in[:, c * 512:(c + 1) * 512]),
                               16)
                        dW.inc(g.dma_start(sinf[:, :],
                                           sinf_in[:, c * 512:(c + 1) * 512]),
                               16)
                    trig_W = dW.n

                    def emit_ln_chunk(cc):
                        lnqd = lnq2[cc % 2]
                        for tl in range(4):
                            j = 4 * cc + tl

                            @blk.sync
                            def _(s, j=j):
                                s = TrackEng(s)
                                take(s, ("xtile",))
                                dX.inc(s.dma_start(xtile[:, :],
                                                   xperm[j, :, :]), 16)
                            xtl = dX.n

                            ln_stats(blk, xtile[:, :], [(dX, xtl)])

                            @blk.vector
                            def _(v):
                                v = TrackEng(v)
                                take(v, ("lntmp",))
                                v.tensor_scalar(lntmp[:, :], xtile[:, :],
                                                mv2[:, 0:1], colA[:, :],
                                                Op.subtract, Op.mult)
                                sV.inc(v.last)
                            lnv = sV.n
                            put(("xtile",), sV)

                            def lnq_dst(b, tl=tl, lnqd=lnqd):
                                base = lnqd[:, :].rearrange(
                                    "p (kc w) -> p kc w", kc=16, w=512)
                                return base[:, b * 8:(b + 1) * 8,
                                            tl * 128:(tl + 1) * 128]
                            transpose_fm(blk, lntmp, lnq_dst,
                                         pe_waits=[(sV, lnv)])
                            put(("lntmp",), sT)
                        return sV.n

                    if c == 0:
                        lnq_ready = emit_ln_chunk(0)
                    else:
                        lnq_ready = lnq_ready_next
                    lnq = lnq2[c % 2]

                    plans = ([("q", wqt, i) for i in range(4)]
                             + [("k", wkt, i) for i in range(4)])
                    if SUB < 2:
                        plans = []
                    for pi, (kind, src, i2) in enumerate(plans):
                        buf = [wpA, wpB][pi % 2]

                        @blk.gpsimd
                        def _(g, buf=buf, src=src, i2=i2, pi=pi):
                            g = TrackEng(g)
                            take(g, ("wp", pi % 2))
                            dW.inc(load_pair(g, buf, src, i2 * 256), 16)
                        wready = dW.n

                        for sub in range(2):
                            h = i2 * 2 + sub
                            dst, bias = (qf, bqf) if kind == "q" else (kf, bkf)
                            bank = banks[(pi * 2 + sub) % 2]

                            @blk.tensor
                            def _(t, buf=buf, sub=sub, bank=bank,
                                  wready=wready,
                                  first=(pi == 0 and sub == 0)):
                                t = TrackEng(t)
                                t.wait_ge(dW.h, wready)
                                if first:
                                    t.wait_ge(sV.h, lnq_ready)
                                take(t, bk(bank))
                                for kc in range(16):
                                    t.matmul(bank[:, :],
                                             buf[:, kc * 256 + sub * 128:
                                                 kc * 256 + sub * 128 + 128],
                                             lnq[:, kc * 512:(kc + 1) * 512],
                                             start=(kc == 0), stop=(kc == 15))
                                sT.inc(t.last)
                            mmdone = sT.n
                            if sub == 1:
                                put(("wp", pi % 2), sT)

                            @blk.vector
                            def _(v, bank=bank, h=h, dst=dst, bias=bias,
                                  mmdone=mmdone,
                                  firstc=(pi == 0 and sub == 0 and c == 0)):
                                v = TrackEng(v)
                                v.wait_ge(sT.h, mmdone)
                                if firstc:
                                    v.wait_ge(dW.h, consts_W)
                                v.wait_ge(dW.h, trig_W)
                                v.tensor_scalar(evtmp[:, :], bank[:, :],
                                                bias[:, h:h + 1], None, Op.add)
                                v.tensor_scalar(rotb[0:64, :],
                                                evtmp[64:128, :], -1.0, None,
                                                Op.mult)
                                v.tensor_copy(rotb[64:128, :], evtmp[0:64, :])
                                v.tensor_tensor(evtmp[:, :], evtmp[:, :],
                                                cosf[:, :], Op.mult)
                                v.tensor_tensor(rotb[:, :], rotb[:, :],
                                                sinf[:, :], Op.mult)
                                v.tensor_tensor(dst[:, h * 512:(h + 1) * 512],
                                                evtmp[:, :], rotb[:, :],
                                                Op.add)
                                sV.inc(v.last)
                            evdone = sV.n
                            put(bk(bank), sV)

                            @blk.vector
                            def _(v, h=h, dst=dst):
                                v = TrackEng(v)
                                take(v, ("sqb",))
                                v.tensor_tensor(sqb[:, :],
                                                dst[:, h * 512:(h + 1) * 512],
                                                dst[:, h * 512:(h + 1) * 512],
                                                Op.mult)
                                sV.inc(v.last)
                            sqdone = sV.n

                            hidx = h if kind == "q" else HL + h
                            nb = banks[5]

                            @blk.tensor
                            def _(t, sqdone=sqdone, hidx=hidx, nb=nb):
                                t = TrackEng(t)
                                t.wait_ge(sV.h, sqdone)
                                take(t, bk(nb))
                                for tl in range(4):
                                    t.matmul(nb[:, tl * 16 + hidx:
                                                tl * 16 + hidx + 1],
                                             sqb[:, tl * 128:(tl + 1) * 128],
                                             ones1[:, :], start=True,
                                             stop=True)
                                sT.inc(t.last)
                            put(("sqb",), sT)
                    rope_done = sV.n
                    put(("trig",), sV)
                    n2done = sT.n

                    for i2 in range(4 if SUB >= 3 else 0):
                        buf = [wpA, wpB][i2 % 2]

                        @blk.gpsimd
                        def _(g, buf=buf, i2=i2):
                            g = TrackEng(g)
                            take(g, ("wp", i2 % 2))
                            dW.inc(load_pair(g, buf, wvt, i2 * 256), 16)
                        vwready = dW.n

                        for tl in range(4):
                            bank = banks[2 + (tl % 2)]

                            @blk.tensor
                            def _(t, buf=buf, i2=i2, tl=tl, bank=bank,
                                  vwready=vwready):
                                t = TrackEng(t)
                                t.wait_ge(dW.h, vwready)
                                take(t, bk(bank))
                                for kc in range(16):
                                    t.matmul(
                                        bank[:, 0:256],
                                        lnq[:, kc * 512 + tl * 128:
                                            kc * 512 + tl * 128 + 128],
                                        buf[:, kc * 256:(kc + 1) * 256],
                                        start=(kc == 0), stop=(kc == 15))
                                sT.inc(t.last)
                            mmdone = sT.n
                            if tl == 3:
                                put(("wp", i2 % 2), sT)

                            @blk.vector
                            def _(v, i2=i2, tl=tl, bank=bank, mmdone=mmdone):
                                v = TrackEng(v)
                                v.wait_ge(sT.h, mmdone)
                                v.tensor_tensor(
                                    vtm[:, tl * 1024 + i2 * 256:
                                        tl * 1024 + i2 * 256 + 256],
                                    bank[:, 0:256],
                                    bvb[:, i2 * 256:(i2 + 1) * 256], Op.add)
                                sV.inc(v.last)
                            put(bk(bank), sV)

                    if SUB < 4:
                        continue
                    @blk.vector
                    def _(v, n2done=n2done):
                        v = TrackEng(v)
                        v.wait_ge(sT.h, n2done)
                        take(v, ("rinv",))
                        v.tensor_copy(rinv[:, :], banks[5][:, 0:64])
                        sV.inc(v.last)
                    ncp = sV.n
                    put(bk(banks[5]), sV)

                    @blk.gpsimd
                    def _(g, ncp=ncp):
                        g = TrackEng(g)
                        g.wait_ge(sV.h, ncp)
                        g.wait_ge(sV.h, consts_v)
                        g.tensor_tensor(rinv[:, :], rinv[:, :],
                                        nhalf[:, 0:64], Op.pow)
                        sG.inc(g.last)
                    ngp = sG.n

                    @blk.vector
                    def _(v, ngp=ngp):
                        v = TrackEng(v)
                        v.wait_ge(sG.h, ngp)
                        v.tensor_scalar(rinv[:, :], rinv[:, :], 1e6, None,
                                        Op.min)
                        sV.inc(v.last)
                    rinvdone = sV.n

                    nchunks = 4 if STAGE >= 2 else 1
                    if c + 1 < nchunks:
                        lnq_ready_next = emit_ln_chunk(c + 1)

                    for tl in range(4):
                        j = 4 * c + tl
                        pb_q, pb_k = banks[2], banks[3]

                        @blk.tensor
                        def _(t, tl=tl, first=(tl == 0)):
                            t = TrackEng(t)
                            if first:
                                t.wait_ge(sV.h, rope_done)
                                t.wait_ge(dW.h, consts_W)
                            take(t, bk(pb_q), bk(pb_k))
                            for h in range(HL):
                                t.matmul(pb_q[:, h * 64:h * 64 + 64],
                                         qf[:, h * 512 + tl * 128:
                                            h * 512 + tl * 128 + 128],
                                         rt[:, :], start=True, stop=True)
                            for h in range(HKV):
                                t.matmul(pb_k[:, h * 64:h * 64 + 64],
                                         kf[:, h * 512 + tl * 128:
                                            h * 512 + tl * 128 + 128],
                                         rt[:, :], start=True, stop=True)
                            sT.inc(t.last)
                        pdone = sT.n

                        @blk.vector
                        def _(v, tl=tl, pdone=pdone):
                            v = TrackEng(v)
                            v.wait_ge(sT.h, pdone)
                            v.wait_ge(sV.h, rinvdone)
                            take(v, ("pwrp",))
                            for h in range(16):
                                bank = pb_q if h < 8 else pb_k
                                v.tensor_scalar(
                                    pbuf[:, h * 64:(h + 1) * 64],
                                    bank[:, (h % 8) * 64:(h % 8) * 64 + 64],
                                    rinv[:, tl * 16 + h:tl * 16 + h + 1],
                                    None, Op.mult)
                            kick(v, sV)
                            v.tensor_scalar(wrp[:, :], pbuf[:, :], INV2PI,
                                            MAGIC, Op.mult, Op.add)
                            v.tensor_scalar(wrp[:, :], wrp[:, :], MAGIC,
                                            None, Op.subtract)
                            v.scalar_tensor_tensor(wrp[:, :], wrp[:, :],
                                                   -TWOPI, pbuf[:, :],
                                                   Op.mult, Op.add)
                            sV.inc(v.last)
                        wdone = sV.n
                        put(bk(pb_q), sV)
                        put(bk(pb_k), sV)
                        if tl == 3:
                            put(("rinv",), sV)

                        @blk.vector
                        def _(v, wdone=wdone):
                            v = TrackEng(v)
                            v.scalar_tensor_tensor(pbuf[:, :], wrp[:, :],
                                                   -1.0, wrp[:, :],
                                                   Op.mult, Op.max)
                            sV.inc(v.last)
                        absdone = sV.n

                        @blk.scalar
                        def _(sc, absdone=absdone):
                            sc = TrackEng(sc)
                            sc.wait_ge(sV.h, absdone)
                            take(sc, ("qpkp",))
                            qs = qpk[:, :].rearrange("p (h w) -> p h w",
                                                     h=16, w=128)
                            wq_ = wrp[:, :].rearrange(
                                "p (h w) -> p h w", h=16, w=64)
                            aq = pbuf[:, :].rearrange(
                                "p (h w) -> p h w", h=16, w=64)
                            sc.activation(qs[:, :, 0:64], wq_, AF.Sin)
                            sc.activation(qs[:, :, 64:128], aq, AF.Sin,
                                          scale=-1.0, bias=halfpi[:, :])
                            sA.inc(sc.last)
                        featdone = sA.n
                        put(("pwrp",), sA)

                        term_v = {}

                        @blk.vector
                        def _(v, tl=tl, featdone=featdone):
                            v = TrackEng(v)
                            v.wait_ge(sA.h, featdone)
                            kp3 = qpk[:, 1024:2048].rearrange(
                                "p (a b) -> p a b", a=8, b=128)
                            t1k = term[:, :].rearrange("p (a b) -> p a b",
                                                       a=8, b=128)
                            for h in range(HL):
                                qph = qpk[:, h * 128:(h + 1) * 128]
                                v.tensor_tensor(
                                    t1k, kp3,
                                    qph.unsqueeze(1).to_broadcast(
                                        [128, 8, 128]), Op.mult)
                                v.tensor_reduce(sred[:, h * 8:(h + 1) * 8],
                                                t1k, AX.X, Op.add)
                            kick(v, sV)
                            v.tensor_scalar(sred[:, :], sred[:, :], SFRAC,
                                            None, Op.mult)
                            kick(v, sV)
                            s3 = sred[:, :].rearrange("p (a b) -> p a b",
                                                      a=HL, b=HKV)
                            for hp in range(HKV):
                                tb = term2[hp % 2]
                                t3 = tb[:, :].rearrange("p (a b) -> p a b",
                                                        a=HL, b=128)
                                vk = vtm[:, tl * 1024 + hp * 128:
                                         tl * 1024 + hp * 128 + 128]
                                v.tensor_tensor(
                                    t3,
                                    vk.unsqueeze(1).to_broadcast(
                                        [128, HL, 128]),
                                    s3[:, :, hp:hp + 1].to_broadcast(
                                        [128, HL, 128]), Op.mult)
                                if hp == 0:
                                    take(v, ("accb",))
                                    v.tensor_copy(acc[:, :], tb[:, :])
                                else:
                                    v.tensor_tensor(acc[:, :], acc[:, :],
                                                    tb[:, :], Op.add)
                            sV.inc(v.last)
                        put(("qpkp",), sV)
                        accdone = sV.n

                        pbx = psbf[tl % 2]

                        @blk.tensor
                        def _(t, pbx=pbx, tl=tl, accdone=accdone):
                            t = TrackEng(t)
                            t.wait_ge(sV.h, accdone)
                            take(t, ("psbf", tl % 2))
                            for h in range(HL):
                                t.transpose(pbx[:, h * 128:(h + 1) * 128],
                                            acc[:, h * 128:(h + 1) * 128],
                                            ident[:, :])
                            sT.inc(t.last)
                        tpd = sT.n
                        put(("accb",), sT)

                        @blk.vector
                        def _(v, pbx=pbx, j=j, tpd=tpd):
                            v = TrackEng(v)
                            v.wait_ge(sT.h, tpd)
                            base = afm[:, :].rearrange("p (h jj w) -> p h jj w",
                                                       h=HL, jj=16, w=128)
                            v.tensor_copy(
                                base[:, :, j, :],
                                pbx[:, :].rearrange("p (h w) -> p h w",
                                                    h=HL, w=128))
                            sV.inc(v.last)
                        put(("psbf", tl % 2), sV)

                        if DEBUG and c == 0 and tl == 0:
                            dbg_done = sV.n

                            @blk.gpsimd
                            def _(g):
                                g.wait_ge(sV.h, dbg_done)
                                dW.inc(g.dma_start(dbg["d_rinv"],
                                                   rinv[:, 0:64]), 16)
                                dW.inc(g.dma_start(dbg["d_wrp"], wrp[:, :]),
                                       16)
                                dW.inc(g.dma_start(dbg["d_abs"], pbuf[:, :]),
                                       16)
                                dW.inc(g.dma_start(dbg["d_qp"], qp[:, :]), 16)
                                dW.inc(g.dma_start(dbg["d_kp"], kp[:, :]), 16)
                                dW.inc(g.dma_start(dbg["d_sred"],
                                                   sred[:, :]), 16)
                                dW.inc(g.dma_start(dbg["d_acc"], acc[:, :]),
                                       16)
                                dW.inc(g.dma_start(dbg["d_qf"],
                                                   qf[:, 0:4096]), 16)
                                dW.inc(g.dma_start(dbg["d_kf"],
                                                   kf[:, 0:4096]), 16)
                                dW.inc(g.dma_start(dbg["d_vtm"],
                                                   vtm[:, 0:1024]), 16)
                                dW.inc(g.dma_start(dbg["d_lnq"],
                                                   lnq[:, 0:2048]), 16)
                            dbg_W = dW.n

                            @blk.vector
                            def _(v):
                                v = TrackEng(v)
                                v.wait_ge(dW.h, dbg_W)
                afm_ready = sV.n

            stkA.__exit__(None, None, None)
            otm4 = sbt(stkAb, "otm4", [128, 4 * D])
            wpC = sbt(stkAb, "wpC", [128, NKC * 256], BF16)
            wpD = sbt(stkAb, "wpD", [128, NKC * 256], BF16)
            bob = sbt(stkAb, "bob_sb", [128, D], BF16)
            wp4 = [wpA, wpB, wpC, wpD]

            with nc.Block() as blk:
                @blk.gpsimd
                def _(g):
                    g = TrackEng(g)
                    dW.inc(g.dma_start(bob[:, :], bob_in[:, :]), 16)
                bob_W = dW.n

                @blk.sync
                def _(s):
                    s = TrackEng(s)
                    for tt in range(8):
                        dX.inc(s.dma_start(xres[:, tt * D:(tt + 1) * D],
                                           xr_in[tt * 128:(tt + 1) * 128, :]),
                               16)
                x_loaded2 = dX.n
                globals()["_XL2"] = x_loaded2
                wo_li = 0

                def emit_wo_load(oc, s2b, woready):
                    nonlocal wo_li
                    buf = wp4[wo_li % 4]

                    @blk.gpsimd
                    def _(g, buf=buf, oc=oc, s2b=s2b, li=wo_li):
                        g = TrackEng(g)
                        take(g, ("wp4", li % 4))
                        dW.inc(load_pair(g, buf, wot,
                                         oc * 512 + s2b * 256), 16)
                    woready[(oc, s2b)] = (dW.n, wo_li % 4)
                    wo_li += 1

                for rh in range(2 if STAGE >= 2 else 0):
                    woready = {}
                    for oc0 in (0, 1):
                        for s2b in range(2):
                            emit_wo_load(oc0, s2b, woready)

                    for oc in range(4):
                        if oc >= 2 and rh == 0:
                            pass
                        for sub in range(4):
                            r = 4 * rh + sub
                            bank = banks[(oc * 4 + sub) % 2]

                            @blk.tensor
                            def _(t, r=r, sub=sub, oc=oc, bank=bank,
                                  first=(rh == 0 and oc == 0 and sub == 0)):
                                t = TrackEng(t)
                                if first:
                                    t.wait_ge(sV.h, afm_ready)
                                take(t, bk(bank))
                                for s2b in range(2):
                                    wr, bi = woready[(oc, s2b)]
                                    t.wait_ge(dW.h, wr)
                                    buf = wp4[bi]
                                    for kc in range(16):
                                        t.matmul(
                                            bank[:, s2b * 256:
                                                 s2b * 256 + 256],
                                            afm[:, r * 2048 + kc * 128:
                                                r * 2048 + kc * 128 + 128],
                                            buf[:, kc * 256:(kc + 1) * 256],
                                            start=(kc == 0), stop=(kc == 15))
                                sT.inc(t.last)
                            mmdone = sT.n
                            if sub == 3:
                                put(("wp4", woready[(oc, 0)][1]), sT)
                                put(("wp4", woready[(oc, 1)][1]), sT)

                            @blk.vector
                            def _(v, sub=sub, oc=oc, bank=bank,
                                  mmdone=mmdone):
                                v = TrackEng(v)
                                v.wait_ge(sT.h, mmdone)
                                v.wait_ge(dW.h, bob_W)
                                v.tensor_tensor(
                                    otm4[:, sub * D + oc * 512:
                                         sub * D + oc * 512 + 512],
                                    bank[:, :],
                                    bob[:, oc * 512:(oc + 1) * 512], Op.add)
                                sV.inc(v.last)
                            put(bk(bank), sV)
                        if oc < 2:
                            for s2b in range(2):
                                emit_wo_load(oc + 2, s2b, woready)
                    wo_done = sV.n

                    for sub in range(4):
                        r = 4 * rh + sub
                        oap = otm4[:, sub * D:(sub + 1) * D]
                        ln_stats(blk, oap,
                                 [(sV, wo_done), (dX, x_loaded2)]
                                 if sub == 0 else [])

                        @blk.vector
                        def _(v, oap=oap, r=r):
                            v = TrackEng(v)
                            take(v, ("lntmp",))
                            v.tensor_scalar(lntmp[:, :], oap, mv2[:, 0:1],
                                            colA[:, :], Op.subtract, Op.mult)
                            v.tensor_tensor(xres[:, r * D:(r + 1) * D],
                                            xres[:, r * D:(r + 1) * D],
                                            lntmp[:, :], Op.add)
                            sV.inc(v.last)
                        put(("lntmp",), sV)
        attn_done = sV.n

        if STAGE < 5:
            with nc.Block() as blk:
                @blk.sync
                def _(s):
                    s.wait_ge(sV.h, sV.n)
                    for tt in range(8):
                        dO.inc(s.dma_start(
                            out_hbm[tt * 128:(tt + 1) * 128, :],
                            xres[:, tt * D:(tt + 1) * D]), 16)
                    s.wait_ge(dO.h, dO.n)

        stkY = ExitStack()
        ylb = sbt(stkY, "ylb", [128, 8 * P], BF16)
        stkE = ExitStack()
        sel32 = sbt(stkE, "sel32_s", [32, NE * 128], BF16)
        cwfm = sbt(stkE, "cwfm", [32, P], BF16)
        stkX = ExitStack()
        xlat = sbt(stkX, "xlat", [128, 8 * P], BF16)
        ysb = sbt(stkX, "ysb", [128, 8 * P])

        gate_ready = cwfm_ready = xlat_ready = 0

        stkL = ExitStack()
        with stkL:
            lnf = sbt(stkL, "lnf2", [128, NKC * P], BF16)
            wpAf = sbt(stkL, "wpAf", [128, NKC * 256], BF16)
            wpBf = sbt(stkL, "wpBf", [128, NKC * 256], BF16)
            gateb = sbt(stkL, "gateb", [128, NKC * NE], BF16)
            scs = sbt(stkL, "scs", [128, NE])
            sbb = sbt(stkL, "sbb", [128, NE])
            g8 = sbt(stkL, "g8", [128, 4 * 8])
            gpad = sbt(stkL, "gpad", [128, 12])
            gm8 = sbt(stkL, "gm8", [128, 8])
            repl8 = sbt(stkL, "repl8", [128, 8])
            gzap = sbt(stkL, "gzap", [128, 12])
            gmask = sbt(stkL, "gmask", [128, 4])
            msb = sbt(stkL, "msb", [128, NE])
            mzap = sbt(stkL, "mzap", [128, NE])
            selm = sbt(stkL, "selm", [128, NE])
            t8v = sbt(stkL, "t8v", [128, 8])
            cwb16 = sbt(stkL, "cwb16", [128, NE], BF16)

            with nc.Block() as blk:
                @blk.gpsimd
                def _(g):
                    g = TrackEng(g)
                    ap = gatet_in[:, :].rearrange("(kc p) o -> p kc o", p=128)
                    dW.inc(g.dma_start(gateb[:, :], ap), 16)
                    dW.inc(g.dma_start(sel32[:, :], sel32_in[:, :]), 16)
                gate_ready = dW.n

                for tt in range(8 if STAGE >= 3 else 0):
                    xap = xres[:, tt * D:(tt + 1) * D]
                    ln_stats(blk, xap,
                             [(sV, attn_done)] if tt == 0 else [])

                    @blk.vector
                    def _(v, xap=xap):
                        v = TrackEng(v)
                        take(v, ("lntmp",))
                        v.tensor_scalar(lntmp[:, :], xap, mv2[:, 0:1],
                                        colA[:, :], Op.subtract, Op.mult)
                        sV.inc(v.last)
                    lnv = sV.n

                    def lnf_dst(b, tt=tt):
                        base = lnf[:, :].rearrange("p (kc w) -> p kc w",
                                                   kc=16, w=P)
                        return base[:, b * 8:(b + 1) * 8,
                                    tt * 128:(tt + 1) * 128]
                    transpose_fm(blk, lntmp, lnf_dst, pe_waits=[(sV, lnv)])
                    put(("lntmp",), sT)
                ln2_ready = sV.n

                for tt in range(8 if STAGE >= 3 else 0):
                    bank = banks[tt % 2]

                    @blk.tensor
                    def _(t, tt=tt, bank=bank):
                        t = TrackEng(t)
                        t.wait_ge(dW.h, gate_ready)
                        if tt == 0:
                            t.wait_ge(sV.h, ln2_ready)
                        take(t, bk(bank))
                        for kc in range(16):
                            t.matmul(bank[:, 0:NE],
                                     lnf[:, kc * P + tt * 128:
                                         kc * P + tt * 128 + 128],
                                     gateb[:, kc * NE:(kc + 1) * NE],
                                     start=(kc == 0), stop=(kc == 15))
                        sT.inc(t.last)
                    gmm = sT.n

                    @blk.scalar
                    def _(sc, bank=bank, gmm=gmm):
                        sc = TrackEng(sc)
                        sc.wait_ge(sT.h, gmm)
                        take(sc, ("gatevec",))
                        sc.activation(scs[:, :], bank[:, 0:NE], AF.Sigmoid)
                        sA.inc(sc.last)
                    sigd = sA.n
                    put(bk(bank), sA)

                    @blk.vector
                    def _(v, sigd=sigd):
                        v = TrackEng(v)
                        v.wait_ge(sA.h, sigd)
                        v.tensor_tensor(sbb[:, :], scs[:, :], gbb[:, :],
                                        Op.add)
                        kick(v, sV)
                        for gi in range(4):
                            v.max(out=g8[:, gi * 8:(gi + 1) * 8],
                                  in_=sbb[:, gi * 8:(gi + 1) * 8])
                        kick(v, sV)
                        g3 = g8[:, :].rearrange("p (a b) -> p a b", a=4, b=8)
                        v.tensor_tensor(gpad[:, 0:4], g3[:, :, 0],
                                        g3[:, :, 1], Op.add)
                        v.memset(gpad[:, 4:12], NEG)
                        kick(v, sV)
                        v.max(out=gm8[:, :], in_=gpad[:, :])
                        kick(v, sV)
                        v.tensor_copy(repl8[:, 0:2], gm8[:, 0:2])
                        v.memset(repl8[:, 2:8], NEG)
                        kick(v, sV)
                        v.match_replace(out=gzap[:, :],
                                        in_to_replace=repl8[:, :],
                                        in_values=gpad[:, :], imm_value=NEG)
                        kick(v, sV)
                        v.tensor_tensor(gmask[:, :], gpad[:, 0:4],
                                        gzap[:, 0:4], Op.not_equal)
                        kick(v, sV)
                        v.tensor_scalar(gmask[:, :], gmask[:, :], 1.0, 1e30,
                                        Op.subtract, Op.mult)
                        kick(v, sV)
                        m3 = msb[:, :].rearrange("p (a b) -> p a b", a=4, b=8)
                        s3 = sbb[:, :].rearrange("p (a b) -> p a b", a=4, b=8)
                        v.tensor_tensor(m3, s3,
                                        gmask[:, :].unsqueeze(2)
                                        .to_broadcast([128, 4, 8]), Op.add)
                        kick(v, sV)
                        v.max(out=t8v[:, :], in_=msb[:, :])
                        kick(v, sV)
                        v.match_replace(out=mzap[:, :],
                                        in_to_replace=t8v[:, :],
                                        in_values=msb[:, :], imm_value=NEG2)
                        kick(v, sV)
                        v.tensor_tensor(selm[:, :], msb[:, :], mzap[:, :],
                                        Op.not_equal)
                        kick(v, sV)
                        v.tensor_tensor(selm[:, :], scs[:, :], selm[:, :],
                                        Op.mult)
                        kick(v, sV)
                        v.tensor_reduce(colA[:, :], selm[:, :], AX.X, Op.add)
                        kick(v, sV)
                        v.reciprocal(colA[:, :], colA[:, :])
                        kick(v, sV)
                        v.tensor_scalar(cwb16[:, :], selm[:, :], colA[:, :],
                                        2.5, Op.mult, Op.mult)
                        sV.inc(v.last)
                    cwd = sV.n
                    put(("gatevec",), sV)

                    @blk.tensor
                    def _(t, cwd=cwd):
                        t = TrackEng(t)
                        take(t, ("psbf", 0))
                        t.wait_ge(sV.h, cwd)
                        t.transpose(psbf[0][0:32, 0:128], cwb16[:, :],
                                    ident[:, :])
                        sT.inc(t.last)
                    cwt = sT.n

                    @blk.vector
                    def _(v, tt=tt, cwt=cwt):
                        v = TrackEng(v)
                        v.wait_ge(sT.h, cwt)
                        v.tensor_copy(cwfm[:, tt * 128:(tt + 1) * 128],
                                      psbf[0][0:32, 0:128])
                        sV.inc(v.last)
                    put(("psbf", 0), sV)
                cwfm_ready = sV.n

                for lc2 in range(4 if STAGE >= 3 else 0):
                    buf = [wpAf, wpBf][lc2 % 2]

                    @blk.gpsimd
                    def _(g, buf=buf, lc2=lc2):
                        g = TrackEng(g)
                        take(g, ("wpf", lc2 % 2))
                        dW.inc(load_pair(g, buf, fc1t, lc2 * 256), 16)
                    fcw = dW.n

                    for sub in range(2):
                        lc = lc2 * 2 + sub
                        for tc in range(2):
                            bank = banks[(lc * 2 + tc) % 2]

                            @blk.tensor
                            def _(t, buf=buf, sub=sub, tc=tc, bank=bank,
                                  fcw=fcw):
                                t = TrackEng(t)
                                t.wait_ge(dW.h, fcw)
                                take(t, bk(bank))
                                for kc in range(16):
                                    t.matmul(
                                        bank[:, :],
                                        buf[:, kc * 256 + sub * 128:
                                            kc * 256 + sub * 128 + 128],
                                        lnf[:, kc * P + tc * 512:
                                            kc * P + tc * 512 + 512],
                                        start=(kc == 0), stop=(kc == 15))
                                sT.inc(t.last)
                            mmd = sT.n
                            if sub == 1 and tc == 1:
                                put(("wpf", lc2 % 2), sT)

                            @blk.vector
                            def _(v, lc=lc, tc=tc, bank=bank, mmd=mmd):
                                v = TrackEng(v)
                                v.wait_ge(sT.h, mmd)
                                v.tensor_copy(
                                    xlat[:, lc * P + tc * 512:
                                         lc * P + tc * 512 + 512],
                                    bank[:, :])
                                sV.inc(v.last)
                            put(bk(bank), sV)
                xlat_ready = sV.n

                @blk.vector
                def _(v):
                    v = TrackEng(v)
                    v.memset(ysb[:, :], 0.0)
                    sV.inc(v.last)

        stkW = ExitStack()
        with stkW:
            w1b = [sbt(stkW, f"w1_{i}", [128, 8 * 512], BF16)
                   for i in range(2)]
            w2b = [sbt(stkW, f"w2_{i}", [128, 8 * 512], BF16)
                   for i in range(2)]
            he = sbt(stkW, "he", [128, 8 * 1024], BF16)
            gsb = sbt(stkW, "gsb", [128, 512], BF16)
            usb = sbt(stkW, "usb", [128, 512], BF16)
            cwsb = sbt(stkW, "cwsb", [128, 1024], BF16)

            def load_w1(g, buf, e, ih):
                ap = w1t[e, :, ih * 512:(ih + 1) * 512].rearrange(
                    "(lc p) i -> p lc i", p=128)
                return g.dma_start(buf[:, :], ap)

            def load_w2(g, buf, e, lh):
                ap = w2t[e, :, lh * 512:(lh + 1) * 512].rearrange(
                    "(ic p) l -> p ic l", p=128)
                return g.dma_start(buf[:, :], ap)

            with nc.Block() as blk:
                for e in range(NE if STAGE >= 4 else 0):
                    w1r = [0, 0]
                    for ih in range(2):
                        @blk.gpsimd
                        def _(g, ih=ih, e=e):
                            g = TrackEng(g)
                            take(g, ("w1", ih))
                            dW.inc(load_w1(g, w1b[ih], e, ih), 16)
                        w1r[ih] = dW.n

                    cb = banks[4]
                    for tc in range(2):
                        @blk.tensor
                        def _(t, e=e, tc=tc, first=(e == 0 and tc == 0)):
                            t = TrackEng(t)
                            if first:
                                t.wait_ge(sV.h, cwfm_ready)
                                t.wait_ge(dW.h, gate_ready)
                            take(t, bk(cb))
                            t.matmul(cb[:, :],
                                     sel32[:, e * 128:(e + 1) * 128],
                                     cwfm[:, tc * 512:(tc + 1) * 512],
                                     start=True, stop=True)
                            sT.inc(t.last)
                        cbd = sT.n

                        @blk.vector
                        def _(v, tc=tc, cbd=cbd):
                            v = TrackEng(v)
                            v.wait_ge(sT.h, cbd)
                            v.tensor_copy(cwsb[:, tc * 512:(tc + 1) * 512],
                                          banks[4][:, :])
                            sV.inc(v.last)
                        put(bk(cb), sV)
                    cwr = sV.n

                    for ih in range(2):
                        for icl in range(4):
                            ic = ih * 4 + icl
                            for tc in range(2):
                                bank = banks[(2 * ic + tc) % 2]

                                @blk.tensor
                                def _(t, ih=ih, icl=icl, tc=tc, bank=bank,
                                      first=(e == 0 and ih == 0
                                             and icl == 0 and tc == 0)):
                                    t = TrackEng(t)
                                    t.wait_ge(dW.h, w1r[ih])
                                    if first:
                                        t.wait_ge(sV.h, xlat_ready)
                                    take(t, bk(bank))
                                    for lc in range(8):
                                        t.matmul(
                                            bank[:, :],
                                            w1b[ih][:, lc * 512 + icl * 128:
                                                    lc * 512 + icl * 128
                                                    + 128],
                                            xlat[:, lc * P + tc * 512:
                                                 lc * P + tc * 512 + 512],
                                            start=(lc == 0), stop=(lc == 7))
                                    sT.inc(t.last)
                                hmm_ = sT.n
                                if icl == 3 and tc == 1:
                                    put(("w1", ih), sT)

                                @blk.vector
                                def _(v, ic=ic, tc=tc, bank=bank, hmm_=hmm_,
                                      cwr=cwr,
                                      firsth=(ih == 0 and icl == 0
                                              and tc == 0)):
                                    v = TrackEng(v)
                                    v.wait_ge(sT.h, hmm_)
                                    if firsth:
                                        v.wait_ge(sV.h, cwr)
                                        take(v, ("he",))
                                    v.tensor_scalar(gsb[:, :], bank[:, :],
                                                    0.0, None, Op.max)
                                    v.tensor_tensor(
                                        usb[:, :], gsb[:, :],
                                        cwsb[:, tc * 512:(tc + 1) * 512],
                                        Op.mult)
                                    v.tensor_tensor(
                                        he[:, ic * 1024 + tc * 512:
                                           ic * 1024 + tc * 512 + 512],
                                        usb[:, :], gsb[:, :], Op.mult)
                                    sV.inc(v.last)
                                put(bk(bank), sV)
                    hdone = sV.n

                    w2r = [0, 0]
                    for lh in range(2):
                        @blk.gpsimd
                        def _(g, lh=lh, e=e):
                            g = TrackEng(g)
                            take(g, ("w2", lh))
                            dW.inc(load_w2(g, w2b[lh], e, lh), 16)
                        w2r[lh] = dW.n

                    for tc in range(2):
                        for lc in range(8):
                            lh, lcl = lc // 4, lc % 4
                            bank = banks[2 + (lc + tc) % 2]

                            @blk.tensor
                            def _(t, tc=tc, lc=lc, lh=lh, lcl=lcl, bank=bank,
                                  first=(tc == 0 and lc == 0)):
                                t = TrackEng(t)
                                t.wait_ge(dW.h, w2r[lh])
                                if first:
                                    t.wait_ge(sV.h, hdone)
                                take(t, bk(bank))
                                for ic in range(8):
                                    t.matmul(
                                        bank[:, :],
                                        w2b[lh][:, ic * 512 + lcl * 128:
                                                ic * 512 + lcl * 128 + 128],
                                        he[:, ic * 1024 + tc * 512:
                                           ic * 1024 + tc * 512 + 512],
                                        start=(ic == 0), stop=(ic == 7))
                                sT.inc(t.last)
                            ymm = sT.n
                            if tc == 1 and lc == 3:
                                put(("w2", 0), sT)
                            if tc == 1 and lc == 7:
                                put(("w2", 1), sT)
                                put(("he",), sT)

                            @blk.vector
                            def _(v, tc=tc, lc=lc, bank=bank, ymm=ymm):
                                v = TrackEng(v)
                                v.wait_ge(sT.h, ymm)
                                yap = ysb[:, lc * P + tc * 512:
                                          lc * P + tc * 512 + 512]
                                v.tensor_tensor(yap, yap, bank[:, :], Op.add)
                                sV.inc(v.last)
                            put(bk(bank), sV)
        y_done = sV.n

        with nc.Block() as blk:
            @blk.vector
            def _(v):
                v = TrackEng(v)
                for i in range(8):
                    v.tensor_copy(ylb[:, i * P:(i + 1) * P],
                                  ysb[:, i * P:(i + 1) * P])
                sV.inc(v.last)
        ylb_ready = sV.n
        stkX.close()
        stkE.close()

        stkS = ExitStack()
        with stkS:
            shfm = sbt(stkS, "shfm", [128, 32 * 512], BF16)
            dwn3 = [sbt(stkS, f"dwn{i}", [128, 16 * 256], BF16)
                    for i in range(3)]
            fc2b = sbt(stkS, "fc2b", [128, 8 * 256], BF16)
            otmh = sbt(stkS, "otmh", [128, 4 * D])

            if True:
                for half in range(2 if STAGE >= 5 else 0):
                    stkH = ExitStack()
                    lnh2 = sbt(stkH, f"lnh2_{half}", [128, NKC * 512], BF16)
                    upwb = [sbt(stkH, f"upw{half}_{i}", [128, NKC * 128],
                                BF16) for i in range(2)]
                    gsh = sbt(stkH, f"gsh{half}", [128, 512], BF16)
                    blk_cm = nc.Block()
                    blk = blk_cm.__enter__()
                    for tl in range(4):
                        tt = half * 4 + tl
                        xap = xres[:, tt * D:(tt + 1) * D]
                        ln_stats(blk, xap)

                        @blk.vector
                        def _(v, xap=xap):
                            v = TrackEng(v)
                            take(v, ("lntmp",))
                            v.tensor_scalar(lntmp[:, :], xap, mv2[:, 0:1],
                                            colA[:, :], Op.subtract, Op.mult)
                            sV.inc(v.last)
                        lnv = sV.n

                        def lnh_dst(b, tl=tl):
                            base = lnh2[:, :].rearrange("p (kc w) -> p kc w",
                                                        kc=16, w=512)
                            return base[:, b * 8:(b + 1) * 8,
                                        tl * 128:(tl + 1) * 128]
                        transpose_fm(blk, lntmp, lnh_dst,
                                     pe_waits=[(sV, lnv)])
                        put(("lntmp",), sT)
                    lnh2_ready = sV.n

                    for icg in range(32):
                        upw = upwb[icg % 2]

                        @blk.gpsimd
                        def _(g, icg=icg, upw=upw):
                            g = TrackEng(g)
                            take(g, ("upw", icg % 2))
                            dW.inc(load_pair(g, upw, upt, icg * 128,
                                             width=128), 16)
                        upr = dW.n

                        if True:
                            bank = banks[icg % 2]

                            @blk.tensor
                            def _(t, bank=bank, upr=upr, upw=upw,
                                  first=(icg == 0)):
                                t = TrackEng(t)
                                t.wait_ge(dW.h, upr)
                                if first:
                                    t.wait_ge(sV.h, lnh2_ready)
                                take(t, bk(bank))
                                for kc in range(16):
                                    t.matmul(
                                        bank[:, :],
                                        upw[:, kc * 128:(kc + 1) * 128],
                                        lnh2[:, kc * 512:(kc + 1) * 512],
                                        start=(kc == 0), stop=(kc == 15))
                                sT.inc(t.last)
                            umm = sT.n
                            put(("upw", icg % 2), sT)

                            @blk.vector
                            def _(v, icg=icg, bank=bank, umm=umm):
                                v = TrackEng(v)
                                v.wait_ge(sT.h, umm)
                                take(v, ("gsh",))
                                v.tensor_scalar(gsh[:, :], bank[:, :],
                                                0.0, None, Op.max)
                                v.tensor_tensor(
                                    shfm[:, icg * 512:(icg + 1) * 512],
                                    gsh[:, :], gsh[:, :], Op.mult)
                                sV.inc(v.last)
                            put(bk(bank), sV)
                            put(("gsh",), sV)
                    sh_ready = sV.n

                    blk_cm.__exit__(None, None, None)
                    stkH.close()
                    blk_cm = nc.Block()
                    blk = blk_cm.__enter__()

                    for oc in range(4):
                        for s2b in range(2):
                            col0 = oc * 512 + s2b * 256
                            step = oc * 2 + s2b
                            dhalf = []
                            for h2 in range(2):
                                bi = (half * 16 + step * 2 + h2) % 3
                                buf = dwn3[bi]

                                @blk.gpsimd
                                def _(g, col0=col0, h2=h2, bi=bi, buf=buf):
                                    g = TrackEng(g)
                                    take(g, ("dwn3", bi))
                                    ap = downt[h2 * 2048:(h2 + 1) * 2048,
                                               col0:col0 + 256].rearrange(
                                        "(ic p) o -> p ic o", p=128)
                                    dW.inc(g.dma_start(buf[:, :], ap), 16)
                                dhalf.append((bi, dW.n))

                            @blk.gpsimd
                            def _(g, col0=col0, fc2b=fc2b):
                                g = TrackEng(g)
                                take(g, ("fc2b",))
                                apf = fc2t[:, col0:col0 + 256].rearrange(
                                    "(lc p) o -> p lc o", p=128)
                                dW.inc(g.dma_start(fc2b[:, :], apf), 16)
                            fcr = dW.n

                            for tl in range(4):
                                tt = half * 4 + tl
                                bank = banks[2 + tl % 2]

                                @blk.tensor
                                def _(t, tl=tl, tt=tt, bank=bank,
                                      dhalf=tuple(dhalf), fcr=fcr, fc2b=fc2b,
                                      first=(oc == 0 and s2b == 0
                                             and tl == 0)):
                                    t = TrackEng(t)
                                    if first:
                                        t.wait_ge(sV.h, sh_ready)
                                        t.wait_ge(sV.h, ylb_ready)
                                    for h2 in range(2):
                                        bi, dwr = dhalf[h2]
                                        t.wait_ge(dW.h, dwr)
                                        if h2 == 0:
                                            take(t, bk(bank))
                                        buf = dwn3[bi]
                                        for i2 in range(16):
                                            ic = h2 * 16 + i2
                                            t.matmul(
                                                bank[:, 0:256],
                                                shfm[:, ic * 512 + tl * 128:
                                                     ic * 512 + tl * 128
                                                     + 128],
                                                buf[:, i2 * 256:
                                                    (i2 + 1) * 256],
                                                start=(ic == 0), stop=False)
                                    t.wait_ge(dW.h, fcr)
                                    for lc in range(8):
                                        t.matmul(
                                            bank[:, 0:256],
                                            ylb[:, lc * P + tt * 128:
                                                lc * P + tt * 128 + 128],
                                            fc2b[:, lc * 256:(lc + 1) * 256],
                                            start=False, stop=(lc == 7))
                                    sT.inc(t.last)
                                dmm = sT.n
                                if tl == 3:
                                    for _bi, _ in dhalf:
                                        put(("dwn3", _bi), sT)
                                    put(("fc2b",), sT)

                                @blk.vector
                                def _(v, tl=tl, oc=oc, s2b=s2b, bank=bank,
                                      dmm=dmm):
                                    v = TrackEng(v)
                                    v.wait_ge(sT.h, dmm)
                                    v.tensor_copy(
                                        otmh[:, tl * D + oc * 512
                                             + s2b * 256:
                                             tl * D + oc * 512
                                             + s2b * 256 + 256],
                                        bank[:, 0:256])
                                    sV.inc(v.last)
                                put(bk(bank), sV)
                    ffn_done = sV.n

                    for tl in range(4):
                        tt = half * 4 + tl
                        oap = otmh[:, tl * D:(tl + 1) * D]
                        ln_stats(blk, oap,
                                 [(sV, ffn_done)] if tl == 0 else [])

                        @blk.vector
                        def _(v, oap=oap, tt=tt):
                            v = TrackEng(v)
                            take(v, ("lntmp",))
                            v.tensor_scalar(lntmp[:, :], oap, mv2[:, 0:1],
                                            colA[:, :], Op.subtract, Op.mult)
                            v.tensor_tensor(xres[:, tt * D:(tt + 1) * D],
                                            xres[:, tt * D:(tt + 1) * D],
                                            lntmp[:, :], Op.add)
                            sV.inc(v.last)
                        put(("lntmp",), sV)
                        xdone = sV.n

                        @blk.sync
                        def _(s, tt=tt, xdone=xdone):
                            s = TrackEng(s)
                            s.wait_ge(sV.h, xdone)
                            dO.inc(s.dma_start(
                                out_hbm[tt * 128:(tt + 1) * 128, :],
                                xres[:, tt * D:(tt + 1) * D]), 16)

                    blk_cm.__exit__(None, None, None)

            if STAGE >= 5:
                with nc.Block() as blk:
                    @blk.sync
                    def _(s):
                        s = TrackEng(s)
                        s.wait_ge(dO.h, dO.n)

        stkY.close()
    return nc


_CACHE = {}


def kernel(**inputs):
    import ml_dtypes

    x = np.asarray(inputs["L0_attn_hidden_states"], dtype=np.float32)
    B = x.shape[0]
    assert x.shape == (B, S2, D) and B == 4

    f = np.ascontiguousarray
    wq = np.asarray(inputs["L0_attn_Wq"], np.float32)
    wk = np.asarray(inputs["L0_attn_Wk"], np.float32)
    wv = np.asarray(inputs["L0_attn_Wv"], np.float32)
    wo = np.asarray(inputs["L0_attn_Wo"], np.float32)
    bq = np.asarray(inputs["L0_attn_bq"], np.float32)
    bkb = np.asarray(inputs["L0_attn_bk"], np.float32)
    bv = np.asarray(inputs["L0_attn_bv"], np.float32)
    bo = np.asarray(inputs["L0_attn_bo"], np.float32)
    R = np.asarray(inputs["L0_attn_rand_matrix"], np.float32)
    gate_w = np.asarray(inputs["L0_ffn_gate_w"], np.float32)
    gate_b = np.asarray(inputs["L0_ffn_gate_bias"], np.float32)
    fc1 = np.asarray(inputs["L0_ffn_fc1_w"], np.float32)
    fc2 = np.asarray(inputs["L0_ffn_fc2_w"], np.float32)
    w1 = np.asarray(inputs["L0_ffn_w1"], np.float32)
    w2 = np.asarray(inputs["L0_ffn_w2"], np.float32)
    up = np.asarray(inputs["L0_ffn_shared_up_w"], np.float32)
    down = np.asarray(inputs["L0_ffn_shared_down_w"], np.float32)

    wqT = f(wq.T)
    shared = {
        "wkt": f(wk.T), "wvt": f(wv.T), "wot": f(wo.T),
        "rt": f(R.T),
        "gatet": f(gate_w.T), "fc1t": f(fc1.T), "fc2t": f(fc2.T),
        "w1t": f(w1.transpose(0, 2, 1)), "w2t": f(w2.transpose(0, 2, 1)),
        "upt": f(up.T), "downt": f(down.T),
        "bkf": f(bkb.reshape(HKV, 128).T),
        "bvb": f(np.broadcast_to(bv, (128, HKV * 128))),
        "bob": f(np.broadcast_to(bo, (128, D))),
        "gbb": f(np.broadcast_to(gate_b, (128, NE))),
    }
    bqf_full = f(bq.reshape(16, 128).T)

    ident = np.eye(128, dtype=ml_dtypes.bfloat16)
    sel = np.zeros((32, NE * 128), dtype=ml_dtypes.bfloat16)
    for e in range(NE):
        sel[e, e * 128:(e + 1) * 128] = 1
    ones1 = np.ones((128, 1), dtype=ml_dtypes.bfloat16)
    hd = 128
    t32 = np.arange(0, hd, 2).astype(np.float32) / np.float32(hd)
    inv_freq = (np.float32(1.0) / (np.float32(10000.0) ** t32)) \
        .astype(np.float32)
    pos = np.arange(S2, dtype=np.float32)
    freqs = (pos[:, None] * inv_freq[None, :]).astype(np.float32)
    emb = np.concatenate([freqs, freqs], axis=1)
    cos_t = np.cos(emb).astype(np.float32)
    sin_t = np.sin(emb).astype(np.float32)
    i_idx = np.arange(S2)
    tperm = 16 * (i_idx % 128) + (i_idx // 128)
    cosf = f(cos_t[tperm, :].T)
    sinf = f(sin_t[tperm, :].T)
    consts = {"ident": ident, "sel32": sel, "ones1": ones1,
              "cosf": cosf, "sinf": sinf}

    in_maps = []
    for core in range(8):
        b, hh = core // 2, core % 2
        m = dict(shared)
        m.update(consts)
        m["xfull"] = f(x[b])
        m["xr"] = f(x[b, hh * P:(hh + 1) * P, :])
        m["wqt"] = f(wqT[:, hh * 1024:(hh + 1) * 1024])
        m["bqf"] = f(bqf_full[:, hh * 8:(hh + 1) * 8])
        in_maps.append(m)

    if "nc" not in _CACHE:
        _CACHE["nc"] = build()
    nc = _CACHE["nc"]
    res = run_bass_kernel_spmd(nc, in_maps, core_ids=list(range(8)))
    out = np.empty((B, S2, D), dtype=np.float32)
    for core in range(8):
        b, hh = core // 2, core % 2
        out[b, hh * P:(hh + 1) * P, :] = res.results[core]["out"]
    return out
